# version 03
# baseline (speedup 1.0000x reference)
import numpy as np

import concourse.bass as bass
import concourse.mybir as mybir

F32 = mybir.dt.float32
BF16 = mybir.dt.bfloat16
I32 = mybir.dt.int32
U8 = mybir.dt.uint8
OP = mybir.AluOpType
AF = mybir.ActivationFunctionType

D = 2048
LSEQ = 4096
TOK = 1024
NDC = 16
E, I_FF = 16, 1536
NEG = -1.0e30
CLAMP = 3.1415925
TP_HI = float(np.float32(2 * np.pi))
TP_LO = float(np.float64(2 * np.pi) - np.float64(np.float32(2 * np.pi)))
INV2PI = float(np.float32(1.0 / (2 * np.pi)))
PIH = float(np.float32(np.pi / 2))
ENGS = ("sync", "vector", "scalar", "tensor", "gpsimd")
DEBUG = False
PHASES = 99
B_MS = 8
B_NIL = 99
SKIP_ROPE = False


class Ctr:
    def __init__(self):
        self.v = 0
    def inc(self, n=1):
        self.v += n
        return self.v


class KB:

    def __init__(self, nc):
        self.nc = nc
        self.q = {e: [] for e in ENGS}
        self.sem = {}
        self.ctr = {}
        self.dma_idx = {}
        self.es = None
        self.dbg = {}

    def on(self, eng, fn):
        self.q[eng].append(fn)

    def emit_into(self, block):
        for e in ENGS:
            fns = self.q[e]
            def runner(h, fns=fns):
                for f in fns:
                    f(h)
            getattr(block, e)(runner)

    def mksem(self, name):
        self.sem[name] = self.es.enter_context(self.nc.semaphore(name))
        self.ctr[name] = Ctr()

    def barrier(self):
        sem, c = self.sem["sb"], self.ctr["sb"]
        tgt = c.v + 5
        fences = self.queue_fence("sd") + self.queue_fence("sg") + self.queue_fence("sa")
        for e in ENGS:
            def f(h, tgt=tgt, sem=sem, fences=fences):
                for wn, wv in fences:
                    h.wait_ge(self.sem[wn], wv)
                h.nop().then_inc(sem, 1)
                h.wait_ge(sem, tgt)
            self.on(e, f)
        c.v = tgt

    NSLOT = 12

    def dma(self, eng, out, in_, waits=(), accum=None):
        base = {"sync": "sd", "gpsimd": "sg", "scalar": "sa"}[eng]
        idx = self.dma_idx[base] = self.dma_idx.get(base, -1) + 1
        sname = f"{base}{idx % self.NSLOT}"
        sem, c = self.sem[sname], self.ctr[sname]
        prev = c.v
        tgt = c.inc(16)
        def f(h, out=out, in_=in_, waits=tuple(waits), accum=accum, prev=prev):
            if prev > 0:
                h.wait_ge(sem, prev)
            for wn, wv in waits:
                h.wait_ge(self.sem[wn], wv)
            if accum is None:
                h.dma_start(out=out, in_=in_).then_inc(sem, 16)
            else:
                h.dma_start(out=out, in_=in_, accum_op=accum).then_inc(sem, 16)
        self.on(eng, f)
        return (sname, tgt)

    def queue_fence(self, base):
        return tuple((f"{base}{i}", self.ctr[f"{base}{i}"].v) for i in range(self.NSLOT)
                     if self.ctr[f"{base}{i}"].v > 0)

    def wait(self, eng, *wts):
        def f(h, wts=wts):
            for wn, wv in wts:
                if wv > 0:
                    h.wait_ge(self.sem[wn], wv)
        self.on(eng, f)

    def v_op(self, fn, inc=True):
        sem, c = self.sem["sv"], self.ctr["sv"]
        def f(h, fn=fn):
            fn(h)
            h.drain()
            h.nop().then_inc(sem, 1)
        self.on("vector", f)
        return ("sv", c.inc())

    def s_op(self, fn):
        sem, c = self.sem["ss"], self.ctr["ss"]
        def f(h, fn=fn):
            fn(h)
            h.drain()
            h.nop().then_inc(sem, 1)
        self.on("scalar", f)
        return ("ss", c.inc())

    def g_op(self, fn):
        sem, c = self.sem["sgp"], self.ctr["sgp"]
        def f(h, fn=fn):
            last = fn(h)
            if last is not None:
                last.then_inc(sem, 1)
            else:
                h.nop().then_inc(sem, 1)
        self.on("gpsimd", f)
        return ("sgp", c.inc())

    def t_op(self, fn):
        sem, c = self.sem["st"], self.ctr["st"]
        def f(h, fn=fn):
            last = fn(h)
            if last is not None:
                last.then_inc(sem, 1)
            else:
                h.nop().then_inc(sem, 1)
        self.on("tensor", f)
        return ("st", c.inc())

    def sb(self, name, shape, dt):
        return self.es.enter_context(self.nc.sbuf_tensor(name, list(shape), dt))

    def sb_on(self, stack, name, shape, dt):
        return stack.enter_context(self.nc.sbuf_tensor(name, list(shape), dt))


def mm3(t, psum_ap, w_hi, w_lo, x_hi, x_lo, start, stop, tp=None):
    kw = {} if tp is None else {"tile_position": tp}
    t.matmul(psum_ap, w_hi, x_hi, start=start, stop=False, **kw)
    t.matmul(psum_ap, w_lo, x_hi, start=False, stop=False, **kw)
    return t.matmul(psum_ap, w_hi, x_lo, start=False, stop=stop, **kw)


def declare_io(nc):
    k = KB(nc)
    d = {}
    def di(name, shape, dt):
        d[name] = nc.dram_tensor(name, list(shape), dt, kind="ExternalInput").ap()
    di("x_fm", (NDC, 128, LSEQ), F32)
    di("x_own", (NDC, 128, TOK), F32)
    di("qpos", (128, TOK), F32)
    di("mask_fine", (16, 128, 512), BF16)
    di("ident_bf", (128, 128), BF16)
    di("ident_f32", (128, 128), F32)
    di("ones_bf", (128, 128), BF16)
    di("ones_f32", (128, 128), F32)
    di("oh_rows", (16, E * 128), F32)
    di("inv_hi", (128, 1), F32)
    di("inv_lo", (128, 1), F32)
    di("wqT", (NDC, 128, 2048), F32)
    di("wkT", (NDC, 128, 512), F32)
    di("wvT", (NDC, 128, 512), F32)
    di("woT", (NDC, 128, 2048), F32)
    di("gwT", (NDC, 128, 16), F32)
    di("gup_b", (E, 16, 128, 3072), U8)
    di("gus", (E, 16, 3072), F32)
    di("dnp_b", (E, 12, 128, 2048), U8)
    di("dns", (E, 12, 2048), F32)
    di("sgup_b", (16, 128, 3072), U8)
    di("sgus", (16, 3072), F32)
    di("sdnp_b", (12, 128, 2048), U8)
    di("sdns", (12, 2048), F32)
    k.inp = d
    k.out = nc.dram_tensor("out", [8, 128, 2048], F32, kind="ExternalOutput").ap()
    k.wq_s = nc.dram_tensor("wq_s", [2, NDC, 128, 2048], BF16, kind="Internal").ap()
    k.wk_s = nc.dram_tensor("wk_s", [2, NDC, 128, 512], BF16, kind="Internal").ap()
    k.wv_s = nc.dram_tensor("wv_s", [2, NDC, 128, 512], BF16, kind="Internal").ap()
    k.wo_s = nc.dram_tensor("wo_s", [2, NDC, 128, 2048], BF16, kind="Internal").ap()
    k.k_s = nc.dram_tensor("k_s", [4, 2, 128, LSEQ], BF16, kind="Internal").ap()
    k.v_s = nc.dram_tensor("v_s", [4, 2, 32, 128, 128], BF16, kind="Internal").ap()
    k.acc_s = nc.dram_tensor("acc_s", [NDC, 128, TOK], F32, kind="Internal").ap()
    k.q_s = nc.dram_tensor("q_s", [2, 128, NDC * TOK], BF16, kind="Internal").ap()
    k.xres_s = nc.dram_tensor("xres_s", [NDC, 128, TOK], F32, kind="Internal").ap()
    if DEBUG:
        k.dbg["xln"] = nc.dram_tensor("dbg_xln", [NDC, 128, TOK], F32, kind="ExternalOutput").ap()
        k.dbg["att"] = nc.dram_tensor("dbg_att", [NDC, 128, TOK], F32, kind="ExternalOutput").ap()
    return k


def emit_prologue(k, es_rope, es_rtmp, es_wsp):
    k.ident_bf = k.sb("identb", (128, 128), BF16)
    k.ident_f = k.sb("identf", (128, 128), F32)
    k.ones_bf = k.sb("onesb", (128, 128), BF16)
    k.ones_f = k.sb("onesf", (128, 128), F32)
    k.inv_hi = k.sb("invhi", (128, 1), F32)
    k.inv_lo = k.sb("invlo", (128, 1), F32)
    k.gw_hi = k.sb("gwhi", (128, NDC * 16), BF16)
    k.gw_lo = k.sb("gwlo", (128, NDC * 16), BF16)
    k.cos_k = k.sb_on(es_rope, "cosk", (128, LSEQ), F32)
    k.sin_k = k.sb_on(es_rope, "sink", (128, LSEQ), F32)
    k.cos_q = k.sb_on(es_rope, "cosq", (128, TOK), F32)
    k.sin_q = k.sb_on(es_rope, "sinq", (128, TOK), F32)
    posq = k.sb_on(es_rtmp, "rt_posq", (128, TOK), F32)
    posk = k.sb_on(es_rtmp, "rt_posk", (128, LSEQ), F32)
    f_t = k.sb_on(es_rtmp, "rt_f", (128, LSEQ), F32)
    t1 = k.sb_on(es_rtmp, "rt_t1", (128, LSEQ), F32)
    ni = k.sb_on(es_rtmp, "rt_ni", (128, LSEQ), I32)
    nf = k.sb_on(es_rtmp, "rt_nf", (128, LSEQ), F32)

    for nm_, t_ in (("ident_bf", k.ident_bf), ("ident_f32", k.ident_f), ("ones_bf", k.ones_bf),
                    ("ones_f32", k.ones_f),
                    ("inv_hi", k.inv_hi), ("inv_lo", k.inv_lo), ("qpos", posq)):
        cw = k.dma("sync", t_[:], k.inp[nm_][:, :])
    cw2 = cw
    g2 = k.g_op(lambda g: g.iota(posk[:], pattern=[[1, LSEQ]], base=0,
                                 channel_multiplier=0, allow_small_or_imprecise_dtypes=True))

    def build_table(src_pos, n, cos_dst, sin_dst):
        wv0 = k.v_op(lambda v: (
            v.tensor_scalar(out=f_t[:, 0:n], in0=src_pos[:, 0:n], scalar1=k.inv_hi[:, 0:1],
                            scalar2=None, op0=OP.mult),
            v.drain(),
            v.tensor_scalar(out=t1[:, 0:n], in0=src_pos[:, 0:n], scalar1=k.inv_lo[:, 0:1],
                            scalar2=None, op0=OP.mult),
            v.drain(),
            v.tensor_tensor(out=f_t[:, 0:n], in0=f_t[:, 0:n], in1=t1[:, 0:n], op=OP.add)))
        for dst, bias in ((sin_dst, 0.0), (cos_dst, PIH)):
            def redv(v, n=n, bias=bias):
                v.tensor_scalar(out=t1[:, 0:n], in0=f_t[:, 0:n], scalar1=float(bias), scalar2=None,
                                op0=OP.add)
                v.drain()
                v.tensor_scalar(out=nf[:, 0:n], in0=t1[:, 0:n], scalar1=INV2PI, scalar2=0.5,
                                op0=OP.mult, op1=OP.add)
                v.drain()
                v.tensor_copy(out=ni[:, 0:n], in_=nf[:, 0:n])
                v.drain()
                v.tensor_copy(out=nf[:, 0:n], in_=ni[:, 0:n])
                v.drain()
                v.scalar_tensor_tensor(out=t1[:, 0:n], in0=nf[:, 0:n], scalar=-TP_HI, in1=t1[:, 0:n],
                                       op0=OP.mult, op1=OP.add)
                v.drain()
                v.scalar_tensor_tensor(out=t1[:, 0:n], in0=nf[:, 0:n], scalar=-TP_LO, in1=t1[:, 0:n],
                                       op0=OP.mult, op1=OP.add)
                v.drain()
                v.tensor_scalar(out=nf[:, 0:n], in0=t1[:, 0:n], scalar1=float(np.float32(np.pi)),
                                scalar2=None, op0=OP.is_gt)
                v.drain()
                v.scalar_tensor_tensor(out=t1[:, 0:n], in0=nf[:, 0:n], scalar=-TP_HI, in1=t1[:, 0:n],
                                       op0=OP.mult, op1=OP.add)
                v.drain()
                v.tensor_scalar(out=nf[:, 0:n], in0=t1[:, 0:n], scalar1=float(np.float32(-np.pi)),
                                scalar2=None, op0=OP.is_lt)
                v.drain()
                v.scalar_tensor_tensor(out=t1[:, 0:n], in0=nf[:, 0:n], scalar=TP_HI, in1=t1[:, 0:n],
                                       op0=OP.mult, op1=OP.add)
                v.drain()
                v.tensor_scalar(out=t1[:, 0:n], in0=t1[:, 0:n], scalar1=CLAMP, scalar2=-CLAMP,
                                op0=OP.min, op1=OP.max)
            wv2 = k.v_op(redv)
            k.wait("scalar", wv2)
            ws = k.s_op(lambda s, dst=dst, n=n: s.activation(out=dst[:, 0:n], in_=t1[:, 0:n], func=AF.Sin))
            k.wait("vector", ws)

    k.wait("vector", *(k.queue_fence("sd") + (g2,)))
    build_table(posq, TOK, k.cos_q, k.sin_q)
    build_table(posk, LSEQ, k.cos_k, k.sin_k)

    gwf = k.sb_on(es_wsp, "gwf", (128, NDC * 16), F32)
    wg = k.dma("sync", gwf[:], k.inp["gwT"].transpose([1, 0, 2]))
    k.wait("scalar", wg)
    s1 = k.s_op(lambda s: s.copy(out=k.gw_hi[:], in_=gwf[:]))
    k.wait("vector", s1)
    k.v_op(lambda v: v.tensor_tensor(out=k.gw_lo[:], in0=gwf[:], in1=k.gw_hi[:], op=OP.subtract))

    bufs = [k.sb_on(es_wsp, f"wspl{i}", (128, 2048), F32) for i in range(2)]
    hbufs = [k.sb_on(es_wsp, f"wsph{i}", (128, 2048), BF16) for i in range(2)]
    lbufs = [k.sb_on(es_wsp, f"wspl2{i}", (128, 2048), BF16) for i in range(2)]
    jobs = []
    for name, wd in (("wkT", 512), ("wvT", 512), ("wqT", 2048), ("woT", 2048)):
        for dc in range(NDC):
            jobs.append((name, wd, dc))
    scr = {"wqT": k.wq_s, "wkT": k.wk_s, "wvT": k.wv_s, "woT": k.wo_s}
    prev_store = [(), ()]
    for i, (name, wd, dc) in enumerate(jobs):
        b = i % 2
        wl = k.dma("sync", bufs[b][:, 0:wd], k.inp[name][dc, :, :], waits=prev_store[b])
        k.wait("scalar", wl, *prev_store[b])
        sh = k.s_op(lambda s, b=b, wd=wd: s.copy(out=hbufs[b][:, 0:wd], in_=bufs[b][:, 0:wd]))
        k.wait("vector", sh, *prev_store[b])
        lv = k.v_op(lambda v, b=b, wd=wd: v.tensor_tensor(out=lbufs[b][:, 0:wd], in0=bufs[b][:, 0:wd],
                                                          in1=hbufs[b][:, 0:wd], op=OP.subtract))
        s_hi = k.dma("gpsimd", scr[name][0, dc, :, :], hbufs[b][:, 0:wd], waits=(sh,))
        s_lo = k.dma("gpsimd", scr[name][1, dc, :, :], lbufs[b][:, 0:wd], waits=(lv,))
        prev_store[b] = (s_hi, s_lo)


def emit_a2(k):
    gdone = k.queue_fence("sg")
    xs = [k.sb(f"a2xs{i}", (128, 512), F32) for i in range(2)]
    xsh = [k.sb(f"a2xh{i}", (128, NDC * 512), BF16) for i in range(2)]
    xsl = [k.sb(f"a2xl{i}", (128, NDC * 512), BF16) for i in range(2)]
    wkh = k.sb("a2wkh", (128, NDC * 512), BF16)
    wkl = k.sb("a2wkl", (128, NDC * 512), BF16)
    wvh = k.sb("a2wvh", (128, NDC * 512), BF16)
    wvl = k.sb("a2wvl", (128, NDC * 512), BF16)
    kf = k.sb("a2kf", (128, 512), F32)
    ksw = k.sb("a2ksw", (128, 512), F32)
    kro = k.sb("a2kro", (128, 512), F32)
    ta = k.sb("a2ta", (128, 512), F32)
    tb = k.sb("a2tb", (128, 512), F32)
    khi = k.sb("a2khi", (128, 512), BF16)
    klo = k.sb("a2klo", (128, 512), BF16)
    vhi = k.sb("a2vhi", (128, 512), BF16)
    vlo = k.sb("a2vlo", (128, 512), BF16)
    vT = k.sb("a2vT", (128, 1024), BF16)
    lw = [k.dma("sync", wkh[:], k.wk_s[0].transpose([1, 0, 2]), waits=gdone),
          k.dma("sync", wkl[:], k.wk_s[1].transpose([1, 0, 2])),
          k.dma("sync", wvh[:], k.wv_s[0].transpose([1, 0, 2])),
          k.dma("sync", wvl[:], k.wv_s[1].transpose([1, 0, 2]))]
    bank_w = {i: () for i in range(4)}
    psb_w = ()
    tensor_last = [(), ()]
    pd1 = pd2 = pdv1 = pdv2 = None
    pttw = None
    prev_cs = [(), ()]
    prev_swp = ()
    for lt in range(8):
        b = lt % 2
        v1 = None
        for ch in range(NDC):
            xb = ch % 2
            sl = slice(ch * 512, (ch + 1) * 512)
            ldw = k.dma("sync", xs[xb][:], k.inp["x_fm"][ch, :, lt * 512:(lt + 1) * 512],
                        waits=prev_cs[xb] + (tensor_last[b] if ch == 0 else ()))
            k.wait("scalar", ldw, *(tensor_last[b] if ch == 0 else ()))
            s1 = k.s_op(lambda s, b=b, sl=sl, xb=xb: s.copy(out=xsh[b][:, sl], in_=xs[xb][:]))
            k.wait("vector", s1)
            v1 = k.v_op(lambda v, b=b, sl=sl, xb=xb: v.tensor_tensor(
                out=xsl[b][:, sl], in0=xs[xb][:], in1=xsh[b][:, sl], op=OP.subtract))
            prev_cs[xb] = (v1,)
        split_all = v1
        last_t = None
        for hp in range(4):
            PK = k.ps[2 * (hp % 2)]
            PV = k.ps[2 * (hp % 2) + 1]
            pk = bank_w[2 * (hp % 2)]
            pv_ = bank_w[2 * (hp % 2) + 1]
            def tk(t, b=b, pk=pk, PK=PK, hp=hp, split_all=split_all, lww=tuple(lw)):
                t.wait_ge(k.sem[split_all[0]], split_all[1])
                for wn, wv in lww:
                    t.wait_ge(k.sem[wn], wv)
                for wn, wv in pk:
                    t.wait_ge(k.sem[wn], wv)
                last = None
                for dc in range(NDC):
                    wsl = slice(dc * 512 + hp * 128, dc * 512 + (hp + 1) * 128)
                    xsl_ = slice(dc * 512, (dc + 1) * 512)
                    last = mm3(t, PK[:], wkh[:, wsl], wkl[:, wsl], xsh[b][:, xsl_], xsl[b][:, xsl_],
                               start=(dc == 0), stop=(dc == NDC - 1))
                return last
            tkw = k.t_op(tk)
            k.wait("vector", tkw, *prev_swp)
            ev = k.v_op(lambda v, PK=PK: v.tensor_copy(out=kf[:], in_=PK[:]))
            bank_w[2 * (hp % 2)] = (ev,)
            sw = []
            for bb in (0, 64):
                sw.append(k.dma("sync", ksw[bb:bb + 32, :], kf[bb + 32:bb + 64, :],
                                waits=(ev,) if bb == 0 else ()))
                sw.append(k.dma("sync", ksw[bb + 32:bb + 64, :], kf[bb:bb + 32, :]))
            prev_swp = tuple(sw)
            csl = slice(lt * 512, (lt + 1) * 512)
            k.wait("vector", *sw)
            if SKIP_ROPE:
                rw = k.v_op(lambda v: v.tensor_copy(out=kro[:], in_=kf[:]))
            else:
                rw = None
            def rope(v, csl=csl):
                for bb in (0, 64):
                    x1 = slice(bb, bb + 32)
                    x2 = slice(bb + 32, bb + 64)
                    v.tensor_tensor(out=ta[x1, :], in0=kf[x1, :], in1=k.cos_k[x1, csl], op=OP.mult)
                    v.tensor_tensor(out=tb[x1, :], in0=ksw[x1, :], in1=k.sin_k[x1, csl], op=OP.mult)
                    v.tensor_tensor(out=ta[x2, :], in0=ksw[x2, :], in1=k.sin_k[x2, csl], op=OP.mult)
                    v.tensor_tensor(out=tb[x2, :], in0=kf[x2, :], in1=k.cos_k[x2, csl], op=OP.mult)
                v.drain()
                v.tensor_tensor(out=kro[0:32, :], in0=ta[0:32, :], in1=tb[0:32, :], op=OP.subtract)
                v.tensor_tensor(out=kro[64:96, :], in0=ta[64:96, :], in1=tb[64:96, :], op=OP.subtract)
                v.drain()
                v.tensor_tensor(out=kro[32:64, :], in0=ta[32:64, :], in1=tb[32:64, :], op=OP.add)
                v.tensor_tensor(out=kro[96:128, :], in0=ta[96:128, :], in1=tb[96:128, :], op=OP.add)
            if rw is None:
                rw = k.v_op(rope)
            k.wait("scalar", rw, *( (pd1,) if pd1 else () ))
            s2 = k.s_op(lambda s: s.copy(out=khi[:], in_=kro[:]))
            k.wait("vector", s2, *( (pd2,) if pd2 else () ))
            v2 = k.v_op(lambda v: v.tensor_tensor(out=klo[:], in0=kro[:], in1=khi[:], op=OP.subtract))
            pd1 = k.dma("gpsimd", k.k_s[hp, 0, :, lt * 512:(lt + 1) * 512], khi[:], waits=(s2,))
            pd2 = k.dma("gpsimd", k.k_s[hp, 1, :, lt * 512:(lt + 1) * 512], klo[:], waits=(v2,))
            def tv(t, b=b, pv_=pv_, PV=PV, hp=hp):
                for wn, wv in pv_:
                    t.wait_ge(k.sem[wn], wv)
                last = None
                for dc in range(NDC):
                    wsl = slice(dc * 512 + hp * 128, dc * 512 + (hp + 1) * 128)
                    xsl_ = slice(dc * 512, (dc + 1) * 512)
                    last = mm3(t, PV[:], wvh[:, wsl], wvl[:, wsl], xsh[b][:, xsl_], xsl[b][:, xsl_],
                               start=(dc == 0), stop=(dc == NDC - 1))
                return last
            tvw = k.t_op(tv)
            k.wait("scalar", tvw, *( (pttw,) if pttw else () ))
            s3 = k.s_op(lambda s, PV=PV: s.copy(out=vhi[:], in_=PV[:]))
            k.wait("vector", s3, *( (pttw,) if pttw else () ))
            v3 = k.v_op(lambda v, PV=PV: v.tensor_tensor(out=vlo[:], in0=PV[:], in1=vhi[:], op=OP.subtract))
            bank_w[2 * (hp % 2) + 1] = (v3,)
            def tt(t, psb_w=psb_w, v3=v3):
                t.wait_ge(k.sem[v3[0]], v3[1])
                for wn, wv in psb_w:
                    t.wait_ge(k.sem[wn], wv)
                last = None
                for j in range(4):
                    t.transpose(k.psb[:, j * 128:(j + 1) * 128], vhi[:, j * 128:(j + 1) * 128], k.ident_bf[:])
                for j in range(4):
                    last = t.transpose(k.psb[:, 512 + j * 128: 512 + (j + 1) * 128],
                                       vlo[:, j * 128:(j + 1) * 128], k.ident_bf[:])
                return last
            pttw = ttw = k.t_op(tt)
            k.wait("vector", ttw, *( (pdv1, pdv2) if pdv2 else () ))
            v4 = k.v_op(lambda v: v.tensor_copy(out=vT[:], in_=k.psb[:]))
            psb_w = (v4,)
            pdv1 = k.dma("gpsimd", k.v_s[hp, 0, lt * 4:(lt + 1) * 4, :, :].transpose([1, 0, 2]),
                         vT[:, 0:512], waits=(v4,))
            pdv2 = k.dma("gpsimd", k.v_s[hp, 1, lt * 4:(lt + 1) * 4, :, :].transpose([1, 0, 2]),
                         vT[:, 512:1024], waits=(v4,))
            last_t = tvw
        tensor_last[b] = (last_t,)
    k.barrier()



def emit_a1(k):
    gdone = k.queue_fence("sg")
    qhi_t = k.sb("a1qhi", (128, TOK), BF16)
    qlo_t = k.sb("a1qlo", (128, TOK), BF16)
    xo = [k.sb(f"a1xo{i}", (128, TOK), F32) for i in range(2)]
    oh = k.sb("a1oh", (128, NDC * TOK), BF16)
    ol = k.sb("a1ol", (128, NDC * TOK), BF16)
    wqh = k.sb("a1wqh", (128, NDC * 128), BF16)
    wql = k.sb("a1wql", (128, NDC * 128), BF16)
    qf = k.sb("a1qf", (128, TOK), F32)
    qsw = k.sb("a1qsw", (128, TOK), F32)
    ta = k.sb("a1ta", (128, TOK), F32)
    tb = k.sb("a1tb", (128, TOK), F32)
    v1 = None
    prev_v = [(), ()]
    for ch in range(NDC):
        xb = ch % 2
        ld = k.dma("sync", xo[xb][:], k.inp["x_own"][ch, :, :], waits=prev_v[xb])
        sl = slice(ch * TOK, (ch + 1) * TOK)
        k.wait("scalar", ld)
        s1 = k.s_op(lambda s, sl=sl, xb=xb: s.copy(out=oh[:, sl], in_=xo[xb][:]))
        k.wait("vector", s1)
        v1 = k.v_op(lambda v, sl=sl, xb=xb: v.tensor_tensor(out=ol[:, sl], in0=xo[xb][:], in1=oh[:, sl],
                                                            op=OP.subtract))
        prev_v[xb] = (v1,)
    split_all = v1
    bank_w = {}
    wq_ld = ()
    ps2 = None
    pqd1 = pqd2 = None
    prev_swp = ()
    for oc in range(NDC):
        b = oc % 2
        lh = k.dma("sync", wqh[:, 0:NDC * 128],
                   k.wq_s[0, :, :, oc * 128:(oc + 1) * 128].transpose([1, 0, 2]),
                   waits=wq_ld + gdone)
        ll = k.dma("sync", wql[:, 0:NDC * 128],
                   k.wq_s[1, :, :, oc * 128:(oc + 1) * 128].transpose([1, 0, 2]),
                   waits=())
        evs = []
        for th in range(2):
            bk = 2 * b + th
            PQ = k.ps[bk]
            pk = bank_w.get(bk, ())
            def tq(t, th=th, pk=pk, lh=lh, ll=ll, split_all=split_all, PQ=PQ):
                t.wait_ge(k.sem[split_all[0]], split_all[1])
                t.wait_ge(k.sem[lh[0]], lh[1])
                t.wait_ge(k.sem[ll[0]], ll[1])
                for wn, wv in pk:
                    t.wait_ge(k.sem[wn], wv)
                last = None
                for dc in range(NDC):
                    wsl = slice(dc * 128, (dc + 1) * 128)
                    xsl_ = slice(dc * TOK + th * 512, dc * TOK + (th + 1) * 512)
                    last = mm3(t, PQ[:], wqh[:, wsl], wql[:, wsl], oh[:, xsl_], ol[:, xsl_],
                               start=(dc == 0), stop=(dc == NDC - 1))
                return last
            tw = k.t_op(tq)
            k.wait("vector", tw, *( (ps2,) if ps2 and th == 0 else () ))
            ev = k.v_op(lambda v, PQ=PQ, th=th: v.tensor_copy(out=qf[:, th * 512:(th + 1) * 512], in_=PQ[:]))
            evs.append(ev)
            bank_w[bk] = (ev,)
        wq_ld = (evs[1],)
        sw = []
        for bb in (0, 64):
            sw.append(k.dma("sync", qsw[bb:bb + 32, :], qf[bb + 32:bb + 64, :],
                            waits=((evs[1],) + prev_swp) if bb == 0 else ()))
            sw.append(k.dma("sync", qsw[bb + 32:bb + 64, :], qf[bb:bb + 32, :]))
        prev_swp = tuple(sw)
        k.wait("vector", *sw)
        def rope(v):
            for bb in (0, 64):
                x1 = slice(bb, bb + 32)
                x2 = slice(bb + 32, bb + 64)
                v.tensor_tensor(out=ta[x1, :], in0=qf[x1, :], in1=k.cos_q[x1, :], op=OP.mult)
                v.tensor_tensor(out=tb[x1, :], in0=qsw[x1, :], in1=k.sin_q[x1, :], op=OP.mult)
                v.tensor_tensor(out=ta[x2, :], in0=qsw[x2, :], in1=k.sin_q[x2, :], op=OP.mult)
                v.tensor_tensor(out=tb[x2, :], in0=qf[x2, :], in1=k.cos_q[x2, :], op=OP.mult)
            v.drain()
            v.tensor_tensor(out=qf[0:32, :], in0=ta[0:32, :], in1=tb[0:32, :], op=OP.subtract)
            v.tensor_tensor(out=qf[64:96, :], in0=ta[64:96, :], in1=tb[64:96, :], op=OP.subtract)
            v.drain()
            v.tensor_tensor(out=qf[32:64, :], in0=ta[32:64, :], in1=tb[32:64, :], op=OP.add)
            v.tensor_tensor(out=qf[96:128, :], in0=ta[96:128, :], in1=tb[96:128, :], op=OP.add)
        rw = k.v_op(rope)
        qsl = slice(oc * TOK, (oc + 1) * TOK)
        k.wait("scalar", rw, *( (pqd1,) if pqd1 else () ))
        ps2 = s2 = k.s_op(lambda s: s.copy(out=qhi_t[:], in_=qf[:]))
        k.wait("vector", s2, *( (pqd2,) if pqd2 else () ))
        v2 = k.v_op(lambda v: v.tensor_tensor(out=qlo_t[:], in0=qf[:], in1=qhi_t[:], op=OP.subtract))
        pqd1 = k.dma("gpsimd", k.q_s[0, :, qsl], qhi_t[:], waits=(s2,))
        pqd2 = k.dma("gpsimd", k.q_s[1, :, qsl], qlo_t[:], waits=(v2,))
    k.barrier()


def emit_b(k, es_att):
    gdone = k.queue_fence("sg")
    qh_ = [[[k.sb(f"bq{i}{h}{l}", (64, TOK), BF16) for l in range(2)] for h in range(2)]
           for i in range(2)]
    msk = k.sb("bmask", (128, 16 * 512), BF16)
    km = [[k.sb(f"bkm{i}{h}", (64, LSEQ), BF16) for h in range(2)] for i in range(2)]
    vm = [[k.sb(f"bvm{i}{h}", (128, 2048), BF16) for h in range(2)] for i in range(2)]
    p32 = k.sb("bp32", (128, 1024), F32)
    p_hi = k.sb("bphi", (128, 1024), BF16)
    p_lo = k.sb("bplo", (128, 1024), BF16)
    dsb = k.sb("bdsb", (33, 512), F32)
    rcp = k.sb("brcp", (33, 512), F32)
    rbc = k.sb("brbc", (128, 512), F32)
    atmp = k.sb("batmp", (128, 512), F32)
    mw = k.dma("sync", msk[:], k.inp["mask_fine"].transpose([1, 0, 2]))
    kload_gate = [(), ()]
    qgate = [(), ()]
    exp_done = {0: (), 1: ()}
    pv_gate = ()
    prev_vsplit = ()
    last_score = {0: None, 1: None}
    for m in range(B_MS):
        mb = m % 2
        pair, a = m // 2, m % 2
        lds = []
        for hl in range(2):
            lds.append(k.dma("sync", km[mb][hl][:], k.k_s[pair, hl, a * 64:(a + 1) * 64, :],
                             waits=((kload_gate[mb] + gdone) if hl == 0 else ())))
            lds.append(k.dma("gpsimd", vm[mb][hl][:],
                             k.v_s[pair, hl, :, :, a * 64:(a + 1) * 64].transpose([1, 0, 2]),
                             waits=((kload_gate[mb] + gdone) if (hl == 0) else ())))
        kld = tuple(lds)
        for P in (2 * m, 2 * m + 1):
            pb = P % 2
            qlds = []
            for head in range(2):
                for hl in range(2):
                    qlds.append(k.dma("sync", qh_[pb][head][hl][:],
                                      k.q_s[hl, head * 64:(head + 1) * 64, P * TOK:(P + 1) * TOK],
                                      waits=(qgate[pb] + (gdone if P < 2 else ()))
                                      if (head == 0 and hl == 0) else ()))
            qld = tuple(qlds)
            for T in range(2):
                nil = min(16 * T + 16, B_NIL)
                qcsl = slice(T * 512, (T + 1) * 512)
                pv_started = False
                for il in range(nil):
                    par = il % 2
                    SA = k.ps[2 * par]
                    SB = k.ps[2 * par + 1]
                    def tsc(t, par=par, il=il, SA=SA, SB=SB, mb=mb, pb=pb, qcsl=qcsl,
                            w_exp=exp_done[par], kld=kld, qld=qld):
                        for wn, wv in w_exp + kld + qld:
                            t.wait_ge(k.sem[wn], wv)
                        ksl = slice(il * 128, (il + 1) * 128)
                        kh, kl2 = km[mb][0], km[mb][1]
                        last = None
                        for head, ps_ in ((0, SA), (1, SB)):
                            qa_hi = qh_[pb][head][0]
                            qa_lo = qh_[pb][head][1]
                            t.matmul(ps_[:], kh[:, ksl], qa_hi[:, qcsl], start=True, stop=False)
                            t.matmul(ps_[:], kl2[:, ksl], qa_hi[:, qcsl], start=False, stop=False)
                            last = t.matmul(ps_[:], kh[:, ksl], qa_lo[:, qcsl], start=False, stop=True)
                        return last
                    tscw = k.t_op(tsc)
                    last_score[mb] = tscw
                    wv_pre = tscw
                    if il >= 16 * T:
                        sub = il - 16 * T
                        k.wait("vector", tscw, mw)
                        wv_pre = k.v_op(lambda v, SA=SA, SB=SB, sub=sub: (
                            v.tensor_tensor(out=SA[:], in0=SA[:], in1=msk[:, sub * 512:(sub + 1) * 512],
                                            op=OP.add),
                            v.drain(),
                            v.tensor_tensor(out=SB[:], in0=SB[:], in1=msk[:, sub * 512:(sub + 1) * 512],
                                            op=OP.add)))
                    k.wait("scalar", wv_pre, mw, *prev_vsplit)
                    se = k.s_op(lambda s, SA=SA, SB=SB: (
                        s.activation(out=p32[:, 0:512], in_=SA[:], func=AF.Exp, scale=0.125),
                        s.drain(),
                        s.activation(out=p32[:, 512:1024], in_=SB[:], func=AF.Exp, scale=0.125)))
                    exp_done[par] = (se,)
                    k.wait("scalar", se, *pv_gate)
                    sh = k.s_op(lambda s: s.copy(out=p_hi[:], in_=p32[:]))
                    k.wait("vector", sh, *pv_gate)
                    vsplit = k.v_op(lambda v: v.tensor_tensor(out=p_lo[:], in0=p32[:], in1=p_hi[:],
                                                              op=OP.subtract))
                    prev_vsplit = (vsplit,)
                    def tpv(t, il=il, mb=mb, vsplit=vsplit, pv_started=pv_started,
                            stop=(il == nil - 1)):
                        t.wait_ge(k.sem[vsplit[0]], vsplit[1])
                        vsl = slice(il * 64, (il + 1) * 64)
                        PV, DEN = k.ps[4], k.ps[5]
                        st = not pv_started
                        vh, vl = vm[mb][0], vm[mb][1]
                        for (vv, pp, fst, lst) in ((vh, p_hi, st, False), (vh, p_lo, False, False),
                                                   (vl, p_hi, False, stop)):
                            t.matmul(PV[0:64, :], vv[:, vsl], pp[:, 0:512], start=fst, stop=lst,
                                     tile_position=(0, 0))
                            t.matmul(PV[64:128, :], vv[:, vsl], pp[:, 512:1024], start=fst, stop=lst,
                                     tile_position=(0, 64), skip_group_check=True)
                        t.matmul(DEN[0:1, :], k.ones_bf[:, 0:1], p_hi[:, 0:512], start=st, stop=False,
                                 tile_position=(0, 0))
                        t.matmul(DEN[0:1, :], k.ones_bf[:, 0:1], p_lo[:, 0:512], start=False, stop=stop,
                                 tile_position=(0, 0))
                        t.matmul(DEN[32:33, :], k.ones_bf[:, 0:1], p_hi[:, 512:1024], start=st,
                                 stop=False, tile_position=(0, 32), skip_group_check=True)
                        last = t.matmul(DEN[32:33, :], k.ones_bf[:, 0:1], p_lo[:, 512:1024], start=False,
                                        stop=stop, tile_position=(0, 32), skip_group_check=True)
                        return last
                    tpvw = k.t_op(tpv)
                    pv_gate = (tpvw,)
                    pv_started = True
                k.wait("scalar", pv_gate[0])
                sden = k.s_op(lambda s: (s.copy(out=dsb[0:1, :], in_=k.ps[5][0:1, :]), s.drain(),
                                         s.copy(out=dsb[32:33, :], in_=k.ps[5][32:33, :])))
                k.wait("vector", sden)
                vr = k.v_op(lambda v: (v.reciprocal(out=rcp[0:1, :], in_=dsb[0:1, :]),
                                       v.drain(),
                                       v.reciprocal(out=rcp[32:33, :], in_=dsb[32:33, :])))
                def tbc(t, vr=vr):
                    t.wait_ge(k.sem[vr[0]], vr[1])
                    t.matmul(k.ps7[0:64, :], k.ones_f[0:1, 0:64], rcp[0:1, :], start=True, stop=True,
                             tile_position=(0, 0))
                    return t.matmul(k.ps7[64:128, :], k.ones_f[32:33, 0:64], rcp[32:33, :], start=True,
                                    stop=True, tile_position=(32, 64), skip_group_check=True)
                tb_ = k.t_op(tbc)
                k.wait("vector", tb_)
                vb = k.v_op(lambda v: v.tensor_copy(out=rbc[:], in_=k.ps7[:]))
                va = k.v_op(lambda v: v.tensor_tensor(out=atmp[:], in0=k.ps[4][:], in1=rbc[:], op=OP.mult))
                acsl = slice(P * TOK + T * 512, P * TOK + (T + 1) * 512)
                k.wait("scalar", va)
                sa_ = k.s_op(lambda s, acsl=acsl: s.copy(out=k.att_hi[:, acsl], in_=atmp[:]))
                k.wait("vector", sa_)
                vl_ = k.v_op(lambda v, acsl=acsl: v.tensor_tensor(out=k.att_lo[:, acsl], in0=atmp[:],
                                                                  in1=k.att_hi[:, acsl], op=OP.subtract))
                pv_gate = (va,)
            qgate[pb] = (last_score[mb],)
        kload_gate[mb] = (va,)
    k.barrier()


def emit_router(k, xln_hi, xln_lo, combT):
    lgf = k.sb("r_lgf", (16, 1024), F32)
    lgt = k.sb("r_lgt", (128, 128), F32)
    nm = k.sb("r_nm", (128, 8), F32)
    ex = k.sb("r_ex", (128, 16), F32)
    pp = k.sb("r_pp", (128, 16), F32)
    wk = k.sb("r_wk", (128, 16), F32)
    msk_ = k.sb("r_msk", (128, 16), F32)
    cmb = k.sb("r_cmb", (128, 128), F32)
    ve = None
    for th in range(2):
        def tl(t, th=th, ve=ve):
            if ve is not None:
                t.wait_ge(k.sem[ve[0]], ve[1])
            last = None
            for dc in range(NDC):
                gsl = slice(dc * 16, (dc + 1) * 16)
                xsl = slice(dc * TOK + th * 512, dc * TOK + (th + 1) * 512)
                last = mm3(t, k.ps[0][0:16, :], k.gw_hi[:, gsl], k.gw_lo[:, gsl],
                           xln_hi[:, xsl], xln_lo[:, xsl], start=(dc == 0), stop=(dc == NDC - 1))
            return last
        tw = k.t_op(tl)
        k.wait("vector", tw)
        ve = k.v_op(lambda v, th=th: v.tensor_copy(out=lgf[:, th * 512:(th + 1) * 512], in_=k.ps[0][0:16, :]))
    vt = None
    for tb in range(8):
        def tt(t, tb=tb, ve=ve, vt=vt):
            t.wait_ge(k.sem[ve[0]], ve[1])
            if vt is not None:
                t.wait_ge(k.sem[vt[0]], vt[1])
            return t.transpose(k.ps[1][:, 0:16], lgf[:, tb * 128:(tb + 1) * 128], k.ident_f[0:16, 0:16])
        tw = k.t_op(tt)
        k.wait("vector", tw)
        vt = k.v_op(lambda v, tb=tb: v.tensor_copy(out=lgt[:, tb * 16:(tb + 1) * 16], in_=k.ps[1][:, 0:16]))
    vcm = None
    for tb in range(8):
        tsl = slice(tb * 16, (tb + 1) * 16)
        v1 = k.v_op(lambda v, tsl=tsl, tb=tb: v.reduce_max(out=nm[:, tb:tb + 1], in_=lgt[:, tsl],
                                                           axis=mybir.AxisListType.X, negate=True))
        k.wait("scalar", v1)
        s1 = k.s_op(lambda s, tsl=tsl, tb=tb: s.activation(out=ex[:], in_=lgt[:, tsl], func=AF.Exp,
                                                           bias=nm[:, tb:tb + 1], scale=1.0))
        k.wait("vector", s1)
        def vtop(v):
            v.reduce_sum(out=wk[:, 0:1], in_=ex[:], axis=mybir.AxisListType.X)
            v.drain()
            v.reciprocal(out=wk[:, 1:2], in_=wk[:, 0:1])
            v.drain()
            v.tensor_scalar(out=pp[:], in0=ex[:], scalar1=wk[:, 1:2], scalar2=None, op0=OP.mult)
            v.drain()
            v.tensor_copy(out=wk[:], in_=pp[:])
            v.drain()
            for r in range(4):
                v.reduce_max(out=msk_[:, 0:1], in_=wk[:], axis=mybir.AxisListType.X)
                v.drain()
                if r < 3:
                    v.tensor_scalar(out=msk_[:], in0=wk[:], scalar1=msk_[:, 0:1], scalar2=None,
                                    op0=OP.is_ge)
                    v.drain()
                    v.scalar_tensor_tensor(out=wk[:], in0=msk_[:], scalar=NEG, in1=wk[:],
                                           op0=OP.mult, op1=OP.add)
                    v.drain()
            v.tensor_scalar(out=msk_[:], in0=pp[:], scalar1=msk_[:, 0:1], scalar2=None, op0=OP.is_ge)
            v.drain()
            v.tensor_tensor(out=pp[:], in0=pp[:], in1=msk_[:], op=OP.mult)
            v.drain()
            v.reduce_sum(out=wk[:, 2:3], in_=pp[:], axis=mybir.AxisListType.X)
            v.drain()
            v.reciprocal(out=wk[:, 3:4], in_=wk[:, 2:3])
            v.drain()
            v.tensor_scalar(out=pp[:], in0=pp[:], scalar1=wk[:, 3:4], scalar2=None, op0=OP.mult)
        v2 = k.v_op(vtop)
        vcm = k.v_op(lambda v, tb=tb: v.tensor_copy(out=cmb[:, tb * 16:(tb + 1) * 16], in_=pp[:]))
    vt2 = None
    for tb in range(8):
        def tt2(t, tb=tb, vcm=vcm, vt2=vt2):
            t.wait_ge(k.sem[vcm[0]], vcm[1])
            if vt2 is not None:
                t.wait_ge(k.sem[vt2[0]], vt2[1])
            return t.transpose(k.ps[1][0:16, 0:128], cmb[:, tb * 16:(tb + 1) * 16], k.ident_f[:])
        tw = k.t_op(tt2)
        k.wait("vector", tw)
        vt2 = k.v_op(lambda v, tb=tb: v.tensor_copy(out=combT[:, tb * 128:(tb + 1) * 128],
                                                    in_=k.ps[1][0:16, 0:128]))
    return vt2


def emit_moe(k, xln_hi, xln_lo, combT):
    bt = [k.sb(f"d_bt{i}", (128, 512), I32) for i in range(2)]
    srep = [k.sb(f"d_sr{i}", (128, 512), F32) for i in range(2)]
    b0 = k.sb("d_b0", (128, 512), I32)
    b1 = k.sb("d_b1", (128, 512), I32)
    b2 = k.sb("d_b2", (128, 512), I32)
    wf = k.sb("d_wf", (128, 512), F32)
    wwh = k.sb("d_wwh", (128, NDC * 512), BF16)
    wwl = k.sb("d_wwl", (128, NDC * 512), BF16)
    wdh = k.sb("d_wdh", (128, 12 * 512), BF16)
    wdl = k.sb("d_wdl", (128, 12 * 512), BF16)
    gbc = k.sb("d_gbc", (128, 1024), F32)
    sgb = k.sb("d_sgb", (128, 512), F32)
    act_hi = k.sb("d_ahi", (128, 12 * TOK), BF16)
    act_lo = k.sb("d_alo", (128, 12 * TOK), BF16)
    af = k.sb("d_af", (128, 512), F32)
    ob = k.sb("d_ob", (128, 512), F32)
    ohb = k.sb("d_ohb", (16, 128), F32)

    acc_dma = k.acc_init
    k.prev_gbc = None
    prev_ldb = [(), ()]
    deq_prev = [(), ()]
    ww_gate = ()
    wd_gate = ()
    last_acc = acc_dma

    def dequant_tile(bts, sreps, whi_ap, wlo_ap, lds):
        w = bts.shape[-1]
        def vchain(v, w=w):
            v.tensor_scalar(out=b0[0:64, 0:w], in0=bts[0:64, :], scalar1=15, scalar2=None,
                            op0=OP.bitwise_and)
            v.drain()
            v.tensor_scalar(out=b0[64:128, 0:w], in0=bts[64:128, :], scalar1=4, scalar2=None,
                            op0=OP.logical_shift_right)
            v.drain()
            v.tensor_scalar(out=b1[:, 0:w], in0=b0[:, 0:w], scalar1=7, scalar2=None, op0=OP.bitwise_and)
            v.drain()
            v.tensor_scalar(out=b2[:, 0:w], in0=b0[:, 0:w], scalar1=8, scalar2=28, op0=OP.bitwise_and,
                            op1=OP.logical_shift_left)
            v.drain()
            v.tensor_scalar(out=b0[:, 0:w], in0=b1[:, 0:w], scalar1=1.0, scalar2=None, op0=OP.min)
            v.drain()
            v.scalar_tensor_tensor(out=b0[:, 0:w], in0=b0[:, 0:w], scalar=251, in1=b1[:, 0:w],
                                   op0=OP.mult, op1=OP.add)
            v.drain()
            v.tensor_scalar(out=b1[:, 0:w], in0=b1[:, 0:w], scalar1=1, scalar2=None,
                            op0=OP.logical_shift_right)
            v.drain()
            v.tensor_scalar(out=b1[:, 0:w], in0=b1[:, 0:w], scalar1=1.0, scalar2=None, op0=OP.min)
            v.drain()
            v.tensor_tensor(out=b0[:, 0:w], in0=b0[:, 0:w], in1=b1[:, 0:w], op=OP.add)
            v.drain()
            v.tensor_scalar(out=b0[:, 0:w], in0=b0[:, 0:w], scalar1=22, scalar2=None,
                            op0=OP.logical_shift_left)
            v.drain()
            v.tensor_tensor(out=b0[:, 0:w], in0=b0[:, 0:w], in1=b2[:, 0:w], op=OP.bitwise_or)
            v.drain()
            v.tensor_tensor(out=wf[:, 0:w], in0=b0[:, 0:w].bitcast(F32), in1=sreps[:], op=OP.mult)
        k.wait("vector", *lds)
        vc = k.v_op(vchain)
        k.wait("scalar", vc)
        sh = k.s_op(lambda s, w=w: s.copy(out=whi_ap, in_=wf[:, 0:w]))
        k.wait("vector", sh)
        vl = k.v_op(lambda v, w=w: v.tensor_tensor(out=wlo_ap, in0=wf[:, 0:w], in1=whi_ap,
                                                   op=OP.subtract))
        return vl

    for e in range(E + 1):
        shared = (e == E)
        if shared:
            gup_ap = lambda dc, cs: k.inp["sgup_b"][dc, :, cs]
            gus_ap = lambda dc, cs: k.inp["sgus"][dc:dc + 1, cs]
            dnp_ap = lambda ic, cs: k.inp["sdnp_b"][ic, :, cs]
            dns_ap = lambda ic, cs: k.inp["sdns"][ic:ic + 1, cs]
        else:
            gup_ap = lambda dc, cs, e=e: k.inp["gup_b"][e, dc, :, cs]
            gus_ap = lambda dc, cs, e=e: k.inp["gus"][e, dc:dc + 1, cs]
            dnp_ap = lambda ic, cs, e=e: k.inp["dnp_b"][e, ic, :, cs]
            dns_ap = lambda ic, cs, e=e: k.inp["dns"][e, ic:ic + 1, cs]
        if not shared:
            ohl = k.dma("sync", ohb[:], k.inp["oh_rows"][:, e * 128:(e + 1) * 128],
                        waits=(k.prev_gbc,) if k.prev_gbc else ())
            vg = None
            for th in range(2):
                def tgb(t, th=th, vg=vg, e=e, ohl=ohl):
                    t.wait_ge(k.sem[ohl[0]], ohl[1])
                    if vg is not None:
                        t.wait_ge(k.sem[vg[0]], vg[1])
                    return t.matmul(k.ps7[:], ohb[:, :],
                                    combT[0:16, th * 512:(th + 1) * 512],
                                    start=True, stop=True, tile_position=(0, 0))
                w1 = k.t_op(tgb)
                k.wait("vector", w1)
                vg = k.v_op(lambda v, th=th: v.tensor_copy(out=gbc[:, th * 512:(th + 1) * 512],
                                                           in_=k.ps7[:]))
            gbc_ready = vg
            k.prev_gbc = w1
        else:
            gbc_ready = None
        for w in range(6):
            deq_done = []
            for half, base in ((0, 2 * w * 128), (1, (2 * w + 12) * 128)):
                for dc in range(NDC):
                    db = dc % 2
                    cs = slice(base, base + 256)
                    l1 = k.dma("gpsimd", bt[db][:, 0:256], gup_ap(dc, cs), waits=prev_ldb[db])
                    cs2 = slice(base + 0, base + 256)
                    l2 = k.dma("sync", srep[db][:, 0:256], gus_ap(dc, cs2).broadcast_to([128, 256]),
                               waits=prev_ldb[db])
                    wsl = slice(dc * 512 + half * 256, dc * 512 + half * 256 + 256)
                    vl = dequant_tile(bt[db][:, 0:256], srep[db][:, 0:256],
                                      wwh[:, wsl], wwl[:, wsl],
                                      ((l1, l2) + ww_gate) if (dc == 0 and half == 0) else (l1, l2))
                    prev_ldb[db] = (vl,)
                    deq_done.append(vl)
            deq_all = deq_done[-1]
            for ci in range(2):
                nc_gate = 2 * w + ci
                for th in range(2):
                    PG = k.ps[2 * ci]
                    PU = k.ps[2 * ci + 1]
                    def tgu(t, ci=ci, th=th, PG=PG, PU=PU, deq_all=deq_all,
                            gate=k.moe_bankw.get(2 * ci, ()), gate2=k.moe_bankw.get(2 * ci + 1, ())):
                        t.wait_ge(k.sem[deq_all[0]], deq_all[1])
                        for wn, wv in gate + gate2:
                            t.wait_ge(k.sem[wn], wv)
                        lastg = None
                        for dc in range(NDC):
                            wslg = slice(dc * 512 + ci * 128, dc * 512 + (ci + 1) * 128)
                            wslu = slice(dc * 512 + 256 + ci * 128, dc * 512 + 256 + (ci + 1) * 128)
                            xsl = slice(dc * TOK + th * 512, dc * TOK + (th + 1) * 512)
                            mm3(t, PG[:], wwh[:, wslg], wwl[:, wslg], xln_hi[:, xsl], xln_lo[:, xsl],
                                start=(dc == 0), stop=(dc == NDC - 1))
                            lastg = mm3(t, PU[:], wwh[:, wslu], wwl[:, wslu], xln_hi[:, xsl],
                                        xln_lo[:, xsl], start=(dc == 0), stop=(dc == NDC - 1))
                        return lastg
                    tw = k.t_op(tgu)
                    k.wait("scalar", tw)
                    s1 = k.s_op(lambda s, PG=PG: s.activation(out=sgb[:], in_=PG[:], func=AF.Silu))
                    asl = slice(nc_gate * TOK + th * 512, nc_gate * TOK + (th + 1) * 512)
                    k.wait("vector", s1, *( (gbc_ready,) if gbc_ready else () ))
                    if shared:
                        va = k.v_op(lambda v, PU=PU: v.tensor_tensor(out=af[:], in0=sgb[:], in1=PU[:],
                                                                     op=OP.mult))
                    else:
                        va = k.v_op(lambda v, PU=PU, th=th: (
                            v.tensor_tensor(out=af[:], in0=sgb[:], in1=PU[:], op=OP.mult),
                            v.drain(),
                            v.tensor_tensor(out=af[:], in0=af[:], in1=gbc[:, th * 512:(th + 1) * 512],
                                            op=OP.mult)))
                    k.wait("scalar", va)
                    sh = k.s_op(lambda s, asl=asl: s.copy(out=act_hi[:, asl], in_=af[:]))
                    k.wait("vector", sh)
                    va2 = k.v_op(lambda v, asl=asl: v.tensor_tensor(out=act_lo[:, asl], in0=af[:],
                                                                    in1=act_hi[:, asl], op=OP.subtract))
                    k.moe_bankw[2 * ci] = (va2,)
                    k.moe_bankw[2 * ci + 1] = (va2,)
            ww_gate = k.moe_bankw.get(2, ())
        act_ready = (k.moe_bankw[3] if k.moe_bankw.get(3) else k.moe_bankw[2])[0]
        for hw in range(4):
            deq_done = []
            for ic in range(12):
                db = ic % 2
                cs = slice(hw * 512, (hw + 1) * 512)
                l1 = k.dma("gpsimd", bt[db][:], dnp_ap(ic, cs), waits=prev_ldb[db])
                l2 = k.dma("sync", srep[db][:], dns_ap(ic, cs).broadcast_to([128, 512]),
                           waits=prev_ldb[db])
                wsl = slice(ic * 512, (ic + 1) * 512)
                vl = dequant_tile(bt[db], srep[db], wdh[:, wsl], wdl[:, wsl],
                                  ((l1, l2) + wd_gate) if ic == 0 else (l1, l2))
                prev_ldb[db] = (vl,)
                deq_done.append(vl)
            deq_all = deq_done[-1]
            for hcl in range(4):
                hc = hw * 4 + hcl
                for th in range(2):
                    PB = k.ps[(2 * hcl + th) % 4]
                    def tdn(t, hcl=hcl, th=th, PB=PB, deq_all=deq_all,
                            gate=k.moe_bankw.get((2 * hcl + th) % 4, ()), act_ready=act_ready):
                        t.wait_ge(k.sem[deq_all[0]], deq_all[1])
                        t.wait_ge(k.sem[act_ready[0]], act_ready[1])
                        for wn, wv in gate:
                            t.wait_ge(k.sem[wn], wv)
                        last = None
                        for ic in range(12):
                            wsl = slice(ic * 512 + hcl * 128, ic * 512 + (hcl + 1) * 128)
                            asl = slice(ic * TOK + th * 512, ic * TOK + (th + 1) * 512)
                            last = mm3(t, PB[:], wdh[:, wsl], wdl[:, wsl],
                                       act_hi[:, asl], act_lo[:, asl],
                                       start=(ic == 0), stop=(ic == 11))
                        return last
                    tw = k.t_op(tdn)
                    osl_dc = hc
                    k.wait("vector", tw, last_acc)
                    vo = k.v_op(lambda v, PB=PB: v.tensor_copy(out=ob[:], in_=PB[:]))
                    last_acc = k.dma("gpsimd", k.acc_s[hc, :, th * 512:(th + 1) * 512], ob[:],
                                     waits=(vo,), accum=OP.add)
                    k.moe_bankw[(2 * hcl + th) % 4] = (vo,)
            wd_gate = k.moe_bankw[(2 * 3 + 1) % 4]
    k.last_acc = last_acc
    k.barrier()


def emit_f(k, out32, ready):
    ob = k.sb("f_ob", (128, 2048), F32)
    pw = (ready,)
    obw = ()
    last = None
    for tb in range(8):
        for g4 in range(4):
            def tt(t, tb=tb, g4=g4, pw=pw):
                for wn, wv in pw:
                    t.wait_ge(k.sem[wn], wv)
                last_ = None
                for j in range(4):
                    dc = g4 * 4 + j
                    last_ = t.transpose(k.ps[4][:, j * 128:(j + 1) * 128],
                                        out32[:, dc * TOK + tb * 128: dc * TOK + (tb + 1) * 128],
                                        k.ident_f[:])
                return last_
            tw = k.t_op(tt)
            k.wait("vector", tw, *obw)
            vv = k.v_op(lambda v, g4=g4: v.tensor_copy(out=ob[:, g4 * 512:(g4 + 1) * 512], in_=k.ps[4][:]))
            pw = (vv,)
        last = k.dma("sync", k.out[tb, :, :], ob[:], waits=(pw[0],))
        obw = (last,)
    k.wait("sync", *k.queue_fence("sd"))
    k.wait("sync", *k.queue_fence("sg"))




def emit_stub_out(k):
    from contextlib import ExitStack
    es = ExitStack()
    ob = k.sb_on(es, "stub_ob", (128, 2048), F32)
    z = k.v_op(lambda v: v.memset(ob[:], 0.0))
    for tb in range(8):
        k.dma("sync", k.out[tb, :, :], ob[:], waits=(z,))
    k.wait("sync", *k.queue_fence("sd"))
    k.wait("sync", *k.queue_fence("sg"))
    es.close()

def build_nc(phases=99):
    from contextlib import ExitStack
    nc = bass.Bass("TRN2", target_bir_lowering=False)
    k = declare_io(nc)
    k.moe_bankw = {}
    with ExitStack() as es0:
        k.es = es0
        for s in ["sv", "ss", "st", "sgp", "sb"]:
            k.mksem(s)
        for base in ("sd", "sg", "sa"):
            for i in range(KB.NSLOT):
                k.mksem(f"{base}{i}")
        k.ps = [es0.enter_context(nc.psum_tensor(f"ps{i}", [128, 512], F32)) for i in range(6)]
        k.psb = es0.enter_context(nc.psum_tensor("psb", [128, 1024], BF16))
        k.ps7 = es0.enter_context(nc.psum_tensor("ps7", [128, 512], F32))
        with nc.Block() as block:
            es_rope = ExitStack()
            es_rtmp = ExitStack()
            es_wsp = ExitStack()
            emit_prologue(k, es_rope, es_rtmp, es_wsp)
            k.barrier()
            es_wsp.close()
            es_rtmp.close()
            if phases >= 2:
                es_a2 = ExitStack(); k.es = es_a2
                emit_a2(k)
                es_a2.close()
            if phases >= 3:
                es_a1 = ExitStack(); k.es = es_a1
                emit_a1(k)
                es_a1.close()
            es_rope.close()
            if phases < 4:
                emit_stub_out(k)
                k.emit_into(block)
                return nc
            es_att = ExitStack()
            k.att_hi = k.sb_on(es_att, "atth", (128, NDC * TOK), BF16)
            k.att_lo = k.sb_on(es_att, "attl", (128, NDC * TOK), BF16)
            es_b = ExitStack(); k.es = es_b
            emit_b(k, es_att)
            es_b.close()
            if phases < 5:
                es_att.close()
                emit_stub_out(k)
                k.emit_into(block)
                return nc
            es_c = ExitStack(); k.es = es_c
            emit_c(k)
            es_c.close()
            es_att.close()
            if phases < 6:
                emit_stub_out(k)
                k.emit_into(block)
                return nc
            es_mid = ExitStack()
            k.xln_hi = k.sb_on(es_mid, "xlnh", (128, NDC * TOK), BF16)
            k.xln_lo = k.sb_on(es_mid, "xlnl", (128, NDC * TOK), BF16)
            es_ln1 = ExitStack(); k.es = es_ln1
            ln1_done, acc_d = emit_ln_stream(k, k.xres_s, pfx="n1", out32=None, out_hi=k.xln_hi,
                                             out_lo=k.xln_lo, acc_hbm=k.acc_s,
                                             src_wait=k.c_spill)
            k.acc_init = acc_d
            k.barrier()
            es_ln1.close()
            if phases < 7:
                es_mid.close()
                emit_stub_out(k)
                k.emit_into(block)
                return nc
            es_d = ExitStack(); k.es = es_d
            combT = k.sb("combT", (16, 1024), F32)
            emit_router(k, k.xln_hi, k.xln_lo, combT)
            k.barrier()
            if phases >= 8:
                emit_moe(k, k.xln_hi, k.xln_lo, combT)
            else:
                k.last_acc = k.acc_init
            es_d.close()
            es_mid.close()
            if phases < 9:
                emit_stub_out(k)
                k.emit_into(block)
                return nc
            es_ln2 = ExitStack(); k.es = es_ln2
            k.out32 = k.sb("o32", (128, NDC * TOK), F32)
            es_ln2b = ExitStack(); k.es = es_ln2b
            ln2_done, _ = emit_ln_stream(k, k.acc_s, pfx="n2", out32=k.out32, src_wait=(k.last_acc,))
            k.barrier()
            es_ln2b.close()
            es_f = ExitStack(); k.es = es_f
            emit_f(k, k.out32, ln2_done)
            es_f.close()
            es_ln2.close()
            k.emit_into(block)
    return nc


def perm_ef(n):
    idx = []
    for c in range(0, n, 128):
        idx += list(range(c, c + 128, 2)) + list(range(c + 1, c + 128, 2))
    return np.array(idx)


PD = perm_ef(2048)
PI = perm_ef(1536)
INV_PD = np.argsort(PD)


def perm_rope(nheads):
    out = []
    for h in range(nheads):
        out += [h * 64 + j for j in range(0, 64, 2)]
        out += [h * 64 + j for j in range(1, 64, 2)]
    return np.array(out)


PQR = perm_rope(32)
PKR = perm_rope(8)


def stage_bytes(packed, kdim):
    sh = packed.shape
    n = sh[-1]
    w = sh[-2]
    b = np.ascontiguousarray(packed).view(np.uint8).reshape(sh[:-2] + (w, n, 4))
    b = np.moveaxis(b, -1, -2)
    b = b.reshape(sh[:-2] + (w * 4, n))
    kc = kdim // 128
    b = b.reshape(sh[:-2] + (kc, 64, n))
    b = np.concatenate([b, b], axis=-2)
    return np.ascontiguousarray(b)


def make_inputs(core, inp):
    b, off = core // 4, core % 4
    m = {}
    x = np.asarray(inp["L0_attn_x"])
    m["x_fm"] = np.ascontiguousarray(x[b].T[PD].reshape(NDC, 128, LSEQ))
    m["x_own"] = np.ascontiguousarray(x[b, off::4, :].T[PD].reshape(NDC, 128, TOK))
    m["qpos"] = np.tile(np.arange(off, LSEQ, 4, dtype=np.float32)[None, :], (128, 1))
    import ml_dtypes
    sub = np.arange(16)[:, None, None]
    p = np.arange(128)[None, :, None]
    i = np.arange(512)[None, None, :]
    mask = np.where(128 * sub + p <= off + 4 * i, 0.0, NEG).astype(ml_dtypes.bfloat16)
    m["mask_fine"] = mask
    m["ident_bf"] = np.eye(128, dtype=ml_dtypes.bfloat16)
    m["ident_f32"] = np.eye(128, dtype=np.float32)
    m["ones_bf"] = np.ones((128, 128), dtype=ml_dtypes.bfloat16)
    m["ones_f32"] = np.ones((128, 128), dtype=np.float32)
    oh = np.zeros((16, E * 128), dtype=np.float32)
    for e in range(E):
        oh[e, e * 128:(e + 1) * 128] = 1.0
    m["oh_rows"] = oh
    inv64 = 1.0 / np.power(10000.0, np.arange(0, 64, 2, dtype=np.float64) / 64.0)
    inv64 = np.tile(inv64, 4)
    ih = inv64.astype(np.float32)
    il = (inv64 - ih.astype(np.float64)).astype(np.float32)
    m["inv_hi"] = ih.reshape(128, 1)
    m["inv_lo"] = il.reshape(128, 1)
    wq = np.asarray(inp["L0_attn_Wq"])
    wk = np.asarray(inp["L0_attn_Wk"])
    wv = np.asarray(inp["L0_attn_Wv"])
    wo = np.asarray(inp["L0_attn_Wo"])
    m["wqT"] = np.ascontiguousarray(wq[np.ix_(PQR, PD)].T.reshape(NDC, 128, 2048))
    m["wkT"] = np.ascontiguousarray(wk[np.ix_(PKR, PD)].T.reshape(NDC, 128, 512))
    m["wvT"] = np.ascontiguousarray(wv[:, PD].T.reshape(NDC, 128, 512))
    m["woT"] = np.ascontiguousarray(wo[PD, :].T.reshape(NDC, 128, 2048))
    m["gwT"] = np.ascontiguousarray(np.asarray(inp["L0_ffn_gate_w"])[:, PD].T.reshape(NDC, 128, 16))
    cgu = np.concatenate([PI, 1536 + PI])
    gup = np.asarray(inp["L0_ffn_gate_up_packed"])[:, :, cgu]
    m["gup_b"] = stage_bytes(gup, 2048)
    m["gus"] = np.ascontiguousarray(np.asarray(inp["L0_ffn_gate_up_scales"])[:, :, cgu])
    dnp = np.asarray(inp["L0_ffn_down_packed"])[:, :, PD]
    m["dnp_b"] = stage_bytes(dnp, 1536)
    m["dns"] = np.ascontiguousarray(np.asarray(inp["L0_ffn_down_scales"])[:, :, PD])
    m["sgup_b"] = stage_bytes(np.asarray(inp["L0_ffn_shared_gate_up_packed"])[:, cgu], 2048)
    m["sgus"] = np.ascontiguousarray(np.asarray(inp["L0_ffn_shared_gate_up_scales"])[:, cgu])
    m["sdnp_b"] = stage_bytes(np.asarray(inp["L0_ffn_shared_down_packed"])[:, PD], 1536)
    m["sdns"] = np.ascontiguousarray(np.asarray(inp["L0_ffn_shared_down_scales"])[:, PD])
    return m


def kernel(**inputs):
    from concourse.bass_utils import run_bass_kernel_spmd
    in_maps = [make_inputs(c, inputs) for c in range(8)]
    nc = build_nc(PHASES)
    res = run_bass_kernel_spmd(nc, in_maps, core_ids=list(range(8)))
    out = np.zeros((2, LSEQ, 2048), dtype=np.float32)
    for c in range(8):
        b, off = c // 4, c % 4
        o = np.asarray(res.results[c]["out"]).reshape(TOK, 2048)
        out[b, off::4, :] = o[:, INV_PD]
    return out


def emit_c(k):
    gdone = k.queue_fence("sg")
    woh = k.sb("c_woh", (128, NDC * 128), BF16)
    wol = k.sb("c_wol", (128, NDC * 128), BF16)
    xot = [k.sb(f"c_xot{i}", (128, 512), F32) for i in range(2)]
    xrt = [k.sb(f"c_xrt{i}", (128, 512), F32) for i in range(2)]
    bank_w = {}
    w_ld = ()
    prev_xr = [(), ()]
    prev_xo = [(), ()]
    for oc in range(NDC):
        b = oc % 2
        lh = k.dma("sync", woh[:], k.wo_s[0, :, :, oc * 128:(oc + 1) * 128].transpose([1, 0, 2]),
                   waits=w_ld + (gdone if oc == 0 else ()))
        ll = k.dma("sync", wol[:], k.wo_s[1, :, :, oc * 128:(oc + 1) * 128].transpose([1, 0, 2]))
        evs = []
        for th in range(2):
            bk = 2 * b + th
            PQ = k.ps[2 + bk]
            pk = bank_w.get(bk, ())
            xb = (2 * oc + th) % 2
            lx = k.dma("sync", xot[xb][:], k.inp["x_own"][oc, :, th * 512:(th + 1) * 512],
                       waits=prev_xo[xb])
            def tq(t, th=th, pk=pk, lh=lh, ll=ll, PQ=PQ):
                t.wait_ge(k.sem[lh[0]], lh[1])
                t.wait_ge(k.sem[ll[0]], ll[1])
                for wn, wv in pk:
                    t.wait_ge(k.sem[wn], wv)
                last = None
                for dc in range(NDC):
                    wsl = slice(dc * 128, (dc + 1) * 128)
                    asl = slice(dc * TOK + th * 512, dc * TOK + (th + 1) * 512)
                    last = mm3(t, PQ[:], woh[:, wsl], wol[:, wsl], k.att_hi[:, asl], k.att_lo[:, asl],
                               start=(dc == 0), stop=(dc == NDC - 1))
                return last
            tw = k.t_op(tq)
            k.wait("vector", tw, lx, *prev_xr[xb])
            ev = k.v_op(lambda v, PQ=PQ, xb=xb: v.tensor_tensor(out=xrt[xb][:], in0=xot[xb][:],
                                                                in1=PQ[:], op=OP.add))
            dx = k.dma("gpsimd", k.xres_s[oc, :, th * 512:(th + 1) * 512], xrt[xb][:], waits=(ev,))
            prev_xr[xb] = (dx,)
            prev_xo[xb] = (ev,)
            evs.append(ev)
            bank_w[bk] = (ev,)
        w_ld = (evs[1],)
    k.c_spill = k.queue_fence("sg")
    k.barrier()


def emit_ln_stream(k, src_hbm, out32=None, out_hi=None, out_lo=None, acc_hbm=None, src_wait=(), pfx="ln"):
    st_ = [k.sb(f"{pfx}lns{i}", (128, 512), F32) for i in range(2)]
    tmph = k.sb(f"{pfx}_th", (128, 512), BF16)
    tmpl = k.sb(f"{pfx}_tl", (128, 512), BF16)
    sqf = k.sb(f"{pfx}_sq", (128, 512), F32)
    sqh = k.sb(f"{pfx}_sqh", (128, 512), BF16)
    sql = k.sb(f"{pfx}_sql", (128, 512), BF16)
    rows = k.sb(f"{pfx}_rows", (1, 7 * 1024), F32)
    bca = k.sb(f"{pfx}_bca", (128, 512), F32)
    o32t = [k.sb(f"{pfx}ln_o{i}", (128, 512), F32) for i in range(2)]

    def R(i, th=None):
        if th is None:
            return rows[0:1, i * 1024:(i + 1) * 1024]
        return rows[0:1, i * 1024 + th * 512: i * 1024 + (th + 1) * 512]

    prev_use = [(), ()]
    prev_tmm = None
    vev = None
    for th in range(2):
        for dc in range(NDC):
            xb = dc % 2
            ld = k.dma("sync", st_[xb][:], src_hbm[dc, :, th * 512:(th + 1) * 512],
                       waits=prev_use[xb] + (tuple(src_wait) if (th == 0 and dc == 0) else ()))
            wts = (prev_tmm,) if prev_tmm else ()
            k.wait("scalar", ld, *wts)
            s1 = k.s_op(lambda s, xb=xb: (s.copy(out=tmph[:], in_=st_[xb][:]), s.drain(),
                                          s.activation(out=sqf[:], in_=st_[xb][:], func=AF.Square)))
            k.wait("vector", s1, *wts)
            v1 = k.v_op(lambda v, xb=xb: v.tensor_tensor(out=tmpl[:], in0=st_[xb][:], in1=tmph[:],
                                                         op=OP.subtract))
            prev_use[xb] = (v1,)
            s2 = k.s_op(lambda s: s.copy(out=sqh[:], in_=sqf[:]))
            k.wait("vector", s2)
            v2 = k.v_op(lambda v: v.tensor_tensor(out=sql[:], in0=sqf[:], in1=sqh[:], op=OP.subtract))
            def tmm(t, dc=dc, v2=v2, vev=vev):
                t.wait_ge(k.sem[v2[0]], v2[1])
                if dc == 0 and vev is not None:
                    t.wait_ge(k.sem[vev[0]], vev[1])
                t.matmul(k.ps[0][0:1, :], k.ones_bf[:, 0:1], tmph[:], start=(dc == 0), stop=False,
                         tile_position=(0, 0))
                t.matmul(k.ps[0][0:1, :], k.ones_bf[:, 0:1], tmpl[:], start=False, stop=(dc == NDC - 1),
                         tile_position=(0, 0))
                t.matmul(k.ps[1][0:1, :], k.ones_bf[:, 0:1], sqh[:], start=(dc == 0), stop=False,
                         tile_position=(0, 0))
                return t.matmul(k.ps[1][0:1, :], k.ones_bf[:, 0:1], sql[:], start=False,
                                stop=(dc == NDC - 1), tile_position=(0, 0))
            prev_tmm = k.t_op(tmm)
        k.wait("vector", prev_tmm)
        vev = k.v_op(lambda v, th=th: (
            v.tensor_copy(out=R(0, th), in_=k.ps[0][0:1, :]),
            v.drain(),
            v.tensor_copy(out=R(1, th), in_=k.ps[1][0:1, :])))
    def vrow(v):
        v.tensor_scalar(out=R(2), in0=R(0), scalar1=1.0 / 2048.0, scalar2=None, op0=OP.mult)
        v.drain()
        v.tensor_scalar(out=R(3), in0=R(1), scalar1=1.0 / 2048.0, scalar2=None, op0=OP.mult)
        v.drain()
        v.tensor_tensor(out=R(4), in0=R(2), in1=R(2), op=OP.mult)
        v.drain()
        v.tensor_tensor(out=R(3), in0=R(3), in1=R(4), op=OP.subtract)
        v.drain()
        v.tensor_scalar(out=R(3), in0=R(3), scalar1=1e-5, scalar2=None, op0=OP.add)
    vr = k.v_op(vrow)
    k.wait("scalar", vr)
    sr = k.s_op(lambda s: s.activation(out=R(4), in_=R(3), func=AF.Sqrt))
    k.wait("vector", sr)
    vr2 = k.v_op(lambda v: (
        v.reciprocal(out=R(5), in_=R(4)),
        v.drain(),
        v.tensor_tensor(out=R(6), in0=R(2), in1=R(5), op=OP.mult),
        v.drain(),
        v.tensor_scalar(out=R(6), in0=R(6), scalar1=-1.0, scalar2=None, op0=OP.mult)))
    out_ready = vr2
    prev_o = [(), ()]
    acc_d = None
    for th in range(2):
        def tbc(t, th=th, vr2=vr2, prev=out_ready):
            t.wait_ge(k.sem[vr2[0]], vr2[1])
            t.wait_ge(k.sem[prev[0]], prev[1])
            t.matmul(k.ps[2][:], k.ones_f[0:1, 0:128], R(5, th), start=True, stop=True,
                     tile_position=(0, 0))
            return t.matmul(k.ps[3][:], k.ones_f[0:1, 0:128], R(6, th), start=True, stop=True,
                            tile_position=(0, 0))
        tb_ = k.t_op(tbc)
        k.wait("vector", tb_)
        vb = k.v_op(lambda v: v.tensor_copy(out=bca[:], in_=k.ps[2][:]))
        last = vb
        for dc in range(NDC):
            xb = dc % 2
            ld = k.dma("sync", st_[xb][:], src_hbm[dc, :, th * 512:(th + 1) * 512],
                       waits=prev_use[xb])
            k.wait("vector", ld, *prev_o[xb])
            va = k.v_op(lambda v, xb=xb: (
                v.tensor_tensor(out=o32t[xb][:], in0=st_[xb][:], in1=bca[:], op=OP.mult),
                v.drain(),
                v.tensor_tensor(out=o32t[xb][:], in0=o32t[xb][:], in1=k.ps[3][:], op=OP.add)))
            prev_use[xb] = (va,)
            last = va
            sl = slice(dc * TOK + th * 512, dc * TOK + (th + 1) * 512)
            if out32 is not None:
                last = k.v_op(lambda v, sl=sl, xb=xb: v.tensor_copy(out=out32[:, sl], in_=o32t[xb][:]))
            if acc_hbm is not None:
                acc_d = k.dma("gpsimd", acc_hbm[dc, :, th * 512:(th + 1) * 512], o32t[xb][:],
                              waits=(last,))
            if out_hi is not None:
                k.wait("scalar", va)
                sh = k.s_op(lambda s, sl=sl, xb=xb: s.copy(out=out_hi[:, sl], in_=o32t[xb][:]))
                k.wait("vector", sh)
                last = k.v_op(lambda v, sl=sl, xb=xb: v.tensor_tensor(out=out_lo[:, sl], in0=o32t[xb][:],
                                                                     in1=out_hi[:, sl], op=OP.subtract))
            prev_o[xb] = (last,) if acc_hbm is None else (last, acc_d)
        out_ready = last
    return out_ready, acc_d


# version 04
# speedup vs baseline: 1.3610x; 1.3610x over previous
import numpy as np

import concourse.bass as bass
import concourse.mybir as mybir

F32 = mybir.dt.float32
BF16 = mybir.dt.bfloat16
I32 = mybir.dt.int32
U8 = mybir.dt.uint8
OP = mybir.AluOpType
AF = mybir.ActivationFunctionType

D = 2048
LSEQ = 4096
TOK = 1024
NDC = 16
E, I_FF = 16, 1536
NEG = -1.0e30
CLAMP = 3.1415925
TP_HI = float(np.float32(2 * np.pi))
TP_LO = float(np.float64(2 * np.pi) - np.float64(np.float32(2 * np.pi)))
INV2PI = float(np.float32(1.0 / (2 * np.pi)))
PIH = float(np.float32(np.pi / 2))
ENGS = ("sync", "vector", "scalar", "tensor", "gpsimd")
DEBUG = False
PHASES = 99
B_MS = 8
B_NIL = 99
SKIP_ROPE = False


class Ctr:
    def __init__(self):
        self.v = 0
    def inc(self, n=1):
        self.v += n
        return self.v


class KB:

    def __init__(self, nc):
        self.nc = nc
        self.q = {e: [] for e in ENGS}
        self.sem = {}
        self.ctr = {}
        self.dma_idx = {}
        self.es = None
        self.dbg = {}

    def on(self, eng, fn):
        self.q[eng].append(fn)

    def emit_into(self, block):
        for e in ENGS:
            fns = self.q[e]
            def runner(h, fns=fns):
                for f in fns:
                    f(h)
            getattr(block, e)(runner)

    def mksem(self, name):
        self.sem[name] = self.es.enter_context(self.nc.semaphore(name))
        self.ctr[name] = Ctr()

    def barrier(self):
        sem, c = self.sem["sb"], self.ctr["sb"]
        tgt = c.v + 5
        fences = self.queue_fence("sd") + self.queue_fence("sg") + self.queue_fence("sa")
        for e in ENGS:
            def f(h, tgt=tgt, sem=sem, fences=fences):
                for wn, wv in fences:
                    h.wait_ge(self.sem[wn], wv)
                h.nop().then_inc(sem, 1)
                h.wait_ge(sem, tgt)
            self.on(e, f)
        c.v = tgt

    NSLOT = 12

    def dma(self, eng, out, in_, waits=(), accum=None):
        base = {"sync": "sd", "gpsimd": "sg", "scalar": "sa"}[eng]
        idx = self.dma_idx[base] = self.dma_idx.get(base, -1) + 1
        sname = f"{base}{idx % self.NSLOT}"
        sem, c = self.sem[sname], self.ctr[sname]
        prev = c.v
        tgt = c.inc(16)
        def f(h, out=out, in_=in_, waits=tuple(waits), accum=accum, prev=prev):
            if prev > 0:
                h.wait_ge(sem, prev)
            for wn, wv in waits:
                h.wait_ge(self.sem[wn], wv)
            if accum is None:
                h.dma_start(out=out, in_=in_).then_inc(sem, 16)
            else:
                h.dma_start(out=out, in_=in_, accum_op=accum).then_inc(sem, 16)
        self.on(eng, f)
        return (sname, tgt)

    def queue_fence(self, base):
        return tuple((f"{base}{i}", self.ctr[f"{base}{i}"].v) for i in range(self.NSLOT)
                     if self.ctr[f"{base}{i}"].v > 0)

    def wait(self, eng, *wts):
        def f(h, wts=wts):
            for wn, wv in wts:
                if wv > 0:
                    h.wait_ge(self.sem[wn], wv)
        self.on(eng, f)

    def v_op(self, fn, inc=True):
        sem, c = self.sem["sv"], self.ctr["sv"]
        def f(h, fn=fn):
            fn(h)
            h.drain()
            h.nop().then_inc(sem, 1)
        self.on("vector", f)
        return ("sv", c.inc())

    def s_op(self, fn):
        sem, c = self.sem["ss"], self.ctr["ss"]
        def f(h, fn=fn):
            fn(h)
            h.drain()
            h.nop().then_inc(sem, 1)
        self.on("scalar", f)
        return ("ss", c.inc())

    def g_op(self, fn):
        sem, c = self.sem["sgp"], self.ctr["sgp"]
        def f(h, fn=fn):
            last = fn(h)
            if last is not None:
                last.then_inc(sem, 1)
            else:
                h.nop().then_inc(sem, 1)
        self.on("gpsimd", f)
        return ("sgp", c.inc())

    def t_op(self, fn):
        sem, c = self.sem["st"], self.ctr["st"]
        def f(h, fn=fn):
            last = fn(h)
            if last is not None:
                last.then_inc(sem, 1)
            else:
                h.nop().then_inc(sem, 1)
        self.on("tensor", f)
        return ("st", c.inc())

    def sb(self, name, shape, dt):
        return self.es.enter_context(self.nc.sbuf_tensor(name, list(shape), dt))

    def sb_on(self, stack, name, shape, dt):
        return stack.enter_context(self.nc.sbuf_tensor(name, list(shape), dt))


def mm3(t, psum_ap, w_hi, w_lo, x_hi, x_lo, start, stop, tp=None):
    kw = {} if tp is None else {"tile_position": tp}
    t.matmul(psum_ap, w_hi, x_hi, start=start, stop=False, **kw)
    t.matmul(psum_ap, w_lo, x_hi, start=False, stop=False, **kw)
    return t.matmul(psum_ap, w_hi, x_lo, start=False, stop=stop, **kw)


def declare_io(nc):
    k = KB(nc)
    d = {}
    def di(name, shape, dt):
        d[name] = nc.dram_tensor(name, list(shape), dt, kind="ExternalInput").ap()
    di("x_fm", (NDC, 128, LSEQ), F32)
    di("x_own", (NDC, 128, TOK), F32)
    di("qpos", (128, TOK), F32)
    di("mask_fine", (16, 128, 512), BF16)
    di("ident_bf", (128, 128), BF16)
    di("ident_f32", (128, 128), F32)
    di("ones_bf", (128, 128), BF16)
    di("ones_f32", (128, 128), F32)
    di("oh_rows", (16, E * 128), F32)
    di("inv_hi", (128, 1), F32)
    di("inv_lo", (128, 1), F32)
    di("wqT", (NDC, 128, 2048), F32)
    di("wkT", (NDC, 128, 512), F32)
    di("wvT", (NDC, 128, 512), F32)
    di("woT", (NDC, 128, 2048), F32)
    di("gwT", (NDC, 128, 16), F32)
    di("gup_b", (E, 16, 128, 3072), U8)
    di("gus", (E, 16, 3072), F32)
    di("dnp_b", (E, 12, 128, 2048), U8)
    di("dns", (E, 12, 2048), F32)
    di("sgup_b", (16, 128, 3072), U8)
    di("sgus", (16, 3072), F32)
    di("sdnp_b", (12, 128, 2048), U8)
    di("sdns", (12, 2048), F32)
    k.inp = d
    k.out = nc.dram_tensor("out", [8, 128, 2048], F32, kind="ExternalOutput").ap()
    k.wq_s = nc.dram_tensor("wq_s", [2, NDC, 128, 2048], BF16, kind="Internal").ap()
    k.wk_s = nc.dram_tensor("wk_s", [2, NDC, 128, 512], BF16, kind="Internal").ap()
    k.wv_s = nc.dram_tensor("wv_s", [2, NDC, 128, 512], BF16, kind="Internal").ap()
    k.wo_s = nc.dram_tensor("wo_s", [2, NDC, 128, 2048], BF16, kind="Internal").ap()
    k.k_s = nc.dram_tensor("k_s", [4, 2, 128, LSEQ], BF16, kind="Internal").ap()
    k.v_s = nc.dram_tensor("v_s", [4, 2, 32, 128, 128], BF16, kind="Internal").ap()
    k.acc_s = nc.dram_tensor("acc_s", [NDC, 128, TOK], F32, kind="Internal").ap()
    k.q_s = nc.dram_tensor("q_s", [2, 128, NDC * TOK], BF16, kind="Internal").ap()
    k.xres_s = nc.dram_tensor("xres_s", [NDC, 128, TOK], F32, kind="Internal").ap()
    if DEBUG:
        k.dbg["xln"] = nc.dram_tensor("dbg_xln", [NDC, 128, TOK], F32, kind="ExternalOutput").ap()
        k.dbg["att"] = nc.dram_tensor("dbg_att", [NDC, 128, TOK], F32, kind="ExternalOutput").ap()
    return k


def emit_prologue(k, es_rope, es_rtmp, es_wsp):
    k.ident_bf = k.sb("identb", (128, 128), BF16)
    k.ident_f = k.sb("identf", (128, 128), F32)
    k.ones_bf = k.sb("onesb", (128, 128), BF16)
    k.ones_f = k.sb("onesf", (128, 128), F32)
    k.inv_hi = k.sb("invhi", (128, 1), F32)
    k.inv_lo = k.sb("invlo", (128, 1), F32)
    k.gw_hi = k.sb("gwhi", (128, NDC * 16), BF16)
    k.gw_lo = k.sb("gwlo", (128, NDC * 16), BF16)
    k.cos_k = k.sb_on(es_rope, "cosk", (128, LSEQ), F32)
    k.sin_k = k.sb_on(es_rope, "sink", (128, LSEQ), F32)
    k.cos_q = k.sb_on(es_rope, "cosq", (128, TOK), F32)
    k.sin_q = k.sb_on(es_rope, "sinq", (128, TOK), F32)
    posq = k.sb_on(es_rtmp, "rt_posq", (128, TOK), F32)
    posk = k.sb_on(es_rtmp, "rt_posk", (128, LSEQ), F32)
    f_t = k.sb_on(es_rtmp, "rt_f", (128, LSEQ), F32)
    t1 = k.sb_on(es_rtmp, "rt_t1", (128, LSEQ), F32)
    ni = k.sb_on(es_rtmp, "rt_ni", (128, LSEQ), I32)
    nf = k.sb_on(es_rtmp, "rt_nf", (128, LSEQ), F32)

    for nm_, t_ in (("ident_bf", k.ident_bf), ("ident_f32", k.ident_f), ("ones_bf", k.ones_bf),
                    ("ones_f32", k.ones_f),
                    ("inv_hi", k.inv_hi), ("inv_lo", k.inv_lo), ("qpos", posq)):
        cw = k.dma("sync", t_[:], k.inp[nm_][:, :])
    cw2 = cw
    g2 = k.g_op(lambda g: g.iota(posk[:], pattern=[[1, LSEQ]], base=0,
                                 channel_multiplier=0, allow_small_or_imprecise_dtypes=True))

    def build_table(src_pos, n, cos_dst, sin_dst):
        wv0 = k.v_op(lambda v: (
            v.tensor_scalar(out=f_t[:, 0:n], in0=src_pos[:, 0:n], scalar1=k.inv_hi[:, 0:1],
                            scalar2=None, op0=OP.mult),
            v.drain(),
            v.tensor_scalar(out=t1[:, 0:n], in0=src_pos[:, 0:n], scalar1=k.inv_lo[:, 0:1],
                            scalar2=None, op0=OP.mult),
            v.drain(),
            v.tensor_tensor(out=f_t[:, 0:n], in0=f_t[:, 0:n], in1=t1[:, 0:n], op=OP.add)))
        for dst, bias in ((sin_dst, 0.0), (cos_dst, PIH)):
            def redv(v, n=n, bias=bias):
                v.tensor_scalar(out=t1[:, 0:n], in0=f_t[:, 0:n], scalar1=float(bias), scalar2=None,
                                op0=OP.add)
                v.drain()
                v.tensor_scalar(out=nf[:, 0:n], in0=t1[:, 0:n], scalar1=INV2PI, scalar2=0.5,
                                op0=OP.mult, op1=OP.add)
                v.drain()
                v.tensor_copy(out=ni[:, 0:n], in_=nf[:, 0:n])
                v.drain()
                v.tensor_copy(out=nf[:, 0:n], in_=ni[:, 0:n])
                v.drain()
                v.scalar_tensor_tensor(out=t1[:, 0:n], in0=nf[:, 0:n], scalar=-TP_HI, in1=t1[:, 0:n],
                                       op0=OP.mult, op1=OP.add)
                v.drain()
                v.scalar_tensor_tensor(out=t1[:, 0:n], in0=nf[:, 0:n], scalar=-TP_LO, in1=t1[:, 0:n],
                                       op0=OP.mult, op1=OP.add)
                v.drain()
                v.tensor_scalar(out=nf[:, 0:n], in0=t1[:, 0:n], scalar1=float(np.float32(np.pi)),
                                scalar2=None, op0=OP.is_gt)
                v.drain()
                v.scalar_tensor_tensor(out=t1[:, 0:n], in0=nf[:, 0:n], scalar=-TP_HI, in1=t1[:, 0:n],
                                       op0=OP.mult, op1=OP.add)
                v.drain()
                v.tensor_scalar(out=nf[:, 0:n], in0=t1[:, 0:n], scalar1=float(np.float32(-np.pi)),
                                scalar2=None, op0=OP.is_lt)
                v.drain()
                v.scalar_tensor_tensor(out=t1[:, 0:n], in0=nf[:, 0:n], scalar=TP_HI, in1=t1[:, 0:n],
                                       op0=OP.mult, op1=OP.add)
                v.drain()
                v.tensor_scalar(out=t1[:, 0:n], in0=t1[:, 0:n], scalar1=CLAMP, scalar2=-CLAMP,
                                op0=OP.min, op1=OP.max)
            wv2 = k.v_op(redv)
            k.wait("scalar", wv2)
            ws = k.s_op(lambda s, dst=dst, n=n: s.activation(out=dst[:, 0:n], in_=t1[:, 0:n], func=AF.Sin))
            k.wait("vector", ws)

    k.wait("vector", *(k.queue_fence("sd") + (g2,)))
    build_table(posq, TOK, k.cos_q, k.sin_q)
    build_table(posk, LSEQ, k.cos_k, k.sin_k)

    gwf = k.sb_on(es_wsp, "gwf", (128, NDC * 16), F32)
    wg = k.dma("sync", gwf[:], k.inp["gwT"].transpose([1, 0, 2]))
    k.wait("scalar", wg)
    s1 = k.s_op(lambda s: s.copy(out=k.gw_hi[:], in_=gwf[:]))
    k.wait("vector", s1)
    k.v_op(lambda v: v.tensor_tensor(out=k.gw_lo[:], in0=gwf[:], in1=k.gw_hi[:], op=OP.subtract))

    bufs = [k.sb_on(es_wsp, f"wspl{i}", (128, 2048), F32) for i in range(2)]
    hbufs = [k.sb_on(es_wsp, f"wsph{i}", (128, 2048), BF16) for i in range(2)]
    lbufs = [k.sb_on(es_wsp, f"wspl2{i}", (128, 2048), BF16) for i in range(2)]
    jobs = []
    for name, wd in (("wkT", 512), ("wvT", 512), ("wqT", 2048), ("woT", 2048)):
        for dc in range(NDC):
            jobs.append((name, wd, dc))
    scr = {"wqT": k.wq_s, "wkT": k.wk_s, "wvT": k.wv_s, "woT": k.wo_s}
    prev_store = [(), ()]
    for i, (name, wd, dc) in enumerate(jobs):
        b = i % 2
        wl = k.dma("sync", bufs[b][:, 0:wd], k.inp[name][dc, :, :], waits=prev_store[b])
        k.wait("scalar", wl, *prev_store[b])
        sh = k.s_op(lambda s, b=b, wd=wd: s.copy(out=hbufs[b][:, 0:wd], in_=bufs[b][:, 0:wd]))
        k.wait("vector", sh, *prev_store[b])
        lv = k.v_op(lambda v, b=b, wd=wd: v.tensor_tensor(out=lbufs[b][:, 0:wd], in0=bufs[b][:, 0:wd],
                                                          in1=hbufs[b][:, 0:wd], op=OP.subtract))
        s_hi = k.dma("gpsimd", scr[name][0, dc, :, :], hbufs[b][:, 0:wd], waits=(sh,))
        s_lo = k.dma("gpsimd", scr[name][1, dc, :, :], lbufs[b][:, 0:wd], waits=(lv,))
        prev_store[b] = (s_hi, s_lo)


def emit_a2(k):
    gdone = k.queue_fence("sg")
    xs = [k.sb(f"a2xs{i}", (128, 512), F32) for i in range(2)]
    xsh = [k.sb(f"a2xh{i}", (128, NDC * 512), BF16) for i in range(2)]
    xsl = [k.sb(f"a2xl{i}", (128, NDC * 512), BF16) for i in range(2)]
    wkh = k.sb("a2wkh", (128, NDC * 512), BF16)
    wkl = k.sb("a2wkl", (128, NDC * 512), BF16)
    wvh = k.sb("a2wvh", (128, NDC * 512), BF16)
    wvl = k.sb("a2wvl", (128, NDC * 512), BF16)
    kf = k.sb("a2kf", (128, 512), F32)
    ksw = k.sb("a2ksw", (128, 512), F32)
    kro = k.sb("a2kro", (128, 512), F32)
    ta = k.sb("a2ta", (128, 512), F32)
    tb = k.sb("a2tb", (128, 512), F32)
    khi = k.sb("a2khi", (128, 512), BF16)
    klo = k.sb("a2klo", (128, 512), BF16)
    vhi = k.sb("a2vhi", (128, 512), BF16)
    vlo = k.sb("a2vlo", (128, 512), BF16)
    vT = k.sb("a2vT", (128, 1024), BF16)
    lw = [k.dma("sync", wkh[:], k.wk_s[0].transpose([1, 0, 2]), waits=gdone),
          k.dma("sync", wkl[:], k.wk_s[1].transpose([1, 0, 2])),
          k.dma("sync", wvh[:], k.wv_s[0].transpose([1, 0, 2])),
          k.dma("sync", wvl[:], k.wv_s[1].transpose([1, 0, 2]))]
    bank_w = {i: () for i in range(4)}
    psb_w = ()
    tensor_last = [(), ()]
    pd1 = pd2 = pdv1 = pdv2 = None
    pttw = None
    prev_cs = [(), ()]
    prev_swp = ()
    for lt in range(8):
        b = lt % 2
        v1 = None
        for ch in range(NDC):
            xb = ch % 2
            sl = slice(ch * 512, (ch + 1) * 512)
            ldw = k.dma("sync", xs[xb][:], k.inp["x_fm"][ch, :, lt * 512:(lt + 1) * 512],
                        waits=prev_cs[xb] + (tensor_last[b] if ch == 0 else ()))
            k.wait("scalar", ldw, *(tensor_last[b] if ch == 0 else ()))
            s1 = k.s_op(lambda s, b=b, sl=sl, xb=xb: s.copy(out=xsh[b][:, sl], in_=xs[xb][:]))
            k.wait("vector", s1)
            v1 = k.v_op(lambda v, b=b, sl=sl, xb=xb: v.tensor_tensor(
                out=xsl[b][:, sl], in0=xs[xb][:], in1=xsh[b][:, sl], op=OP.subtract))
            prev_cs[xb] = (v1,)
        split_all = v1
        last_t = None
        for hp in range(4):
            PK = k.ps[2 * (hp % 2)]
            PV = k.ps[2 * (hp % 2) + 1]
            pk = bank_w[2 * (hp % 2)]
            pv_ = bank_w[2 * (hp % 2) + 1]
            def tk(t, b=b, pk=pk, PK=PK, hp=hp, split_all=split_all, lww=tuple(lw)):
                t.wait_ge(k.sem[split_all[0]], split_all[1])
                for wn, wv in lww:
                    t.wait_ge(k.sem[wn], wv)
                for wn, wv in pk:
                    t.wait_ge(k.sem[wn], wv)
                last = None
                for dc in range(NDC):
                    wsl = slice(dc * 512 + hp * 128, dc * 512 + (hp + 1) * 128)
                    xsl_ = slice(dc * 512, (dc + 1) * 512)
                    last = mm3(t, PK[:], wkh[:, wsl], wkl[:, wsl], xsh[b][:, xsl_], xsl[b][:, xsl_],
                               start=(dc == 0), stop=(dc == NDC - 1))
                return last
            tkw = k.t_op(tk)
            k.wait("vector", tkw, *prev_swp)
            ev = k.v_op(lambda v, PK=PK: v.tensor_copy(out=kf[:], in_=PK[:]))
            bank_w[2 * (hp % 2)] = (ev,)
            sw = []
            for bb in (0, 64):
                sw.append(k.dma("sync", ksw[bb:bb + 32, :], kf[bb + 32:bb + 64, :],
                                waits=(ev,) if bb == 0 else ()))
                sw.append(k.dma("sync", ksw[bb + 32:bb + 64, :], kf[bb:bb + 32, :]))
            prev_swp = tuple(sw)
            csl = slice(lt * 512, (lt + 1) * 512)
            k.wait("vector", *sw)
            if SKIP_ROPE:
                rw = k.v_op(lambda v: v.tensor_copy(out=kro[:], in_=kf[:]))
            else:
                rw = None
            def rope(v, csl=csl):
                for bb in (0, 64):
                    x1 = slice(bb, bb + 32)
                    x2 = slice(bb + 32, bb + 64)
                    v.tensor_tensor(out=ta[x1, :], in0=kf[x1, :], in1=k.cos_k[x1, csl], op=OP.mult)
                    v.tensor_tensor(out=tb[x1, :], in0=ksw[x1, :], in1=k.sin_k[x1, csl], op=OP.mult)
                    v.tensor_tensor(out=ta[x2, :], in0=ksw[x2, :], in1=k.sin_k[x2, csl], op=OP.mult)
                    v.tensor_tensor(out=tb[x2, :], in0=kf[x2, :], in1=k.cos_k[x2, csl], op=OP.mult)
                v.drain()
                v.tensor_tensor(out=kro[0:32, :], in0=ta[0:32, :], in1=tb[0:32, :], op=OP.subtract)
                v.tensor_tensor(out=kro[64:96, :], in0=ta[64:96, :], in1=tb[64:96, :], op=OP.subtract)
                v.drain()
                v.tensor_tensor(out=kro[32:64, :], in0=ta[32:64, :], in1=tb[32:64, :], op=OP.add)
                v.tensor_tensor(out=kro[96:128, :], in0=ta[96:128, :], in1=tb[96:128, :], op=OP.add)
            if rw is None:
                rw = k.v_op(rope)
            k.wait("scalar", rw, *( (pd1,) if pd1 else () ))
            s2 = k.s_op(lambda s: s.copy(out=khi[:], in_=kro[:]))
            k.wait("vector", s2, *( (pd2,) if pd2 else () ))
            v2 = k.v_op(lambda v: v.tensor_tensor(out=klo[:], in0=kro[:], in1=khi[:], op=OP.subtract))
            pd1 = k.dma("gpsimd", k.k_s[hp, 0, :, lt * 512:(lt + 1) * 512], khi[:], waits=(s2,))
            pd2 = k.dma("gpsimd", k.k_s[hp, 1, :, lt * 512:(lt + 1) * 512], klo[:], waits=(v2,))
            def tv(t, b=b, pv_=pv_, PV=PV, hp=hp):
                for wn, wv in pv_:
                    t.wait_ge(k.sem[wn], wv)
                last = None
                for dc in range(NDC):
                    wsl = slice(dc * 512 + hp * 128, dc * 512 + (hp + 1) * 128)
                    xsl_ = slice(dc * 512, (dc + 1) * 512)
                    last = mm3(t, PV[:], wvh[:, wsl], wvl[:, wsl], xsh[b][:, xsl_], xsl[b][:, xsl_],
                               start=(dc == 0), stop=(dc == NDC - 1))
                return last
            tvw = k.t_op(tv)
            k.wait("scalar", tvw, *( (pttw,) if pttw else () ))
            s3 = k.s_op(lambda s, PV=PV: s.copy(out=vhi[:], in_=PV[:]))
            k.wait("vector", s3, *( (pttw,) if pttw else () ))
            v3 = k.v_op(lambda v, PV=PV: v.tensor_tensor(out=vlo[:], in0=PV[:], in1=vhi[:], op=OP.subtract))
            bank_w[2 * (hp % 2) + 1] = (v3,)
            def tt(t, psb_w=psb_w, v3=v3):
                t.wait_ge(k.sem[v3[0]], v3[1])
                for wn, wv in psb_w:
                    t.wait_ge(k.sem[wn], wv)
                last = None
                for j in range(4):
                    t.transpose(k.psb[:, j * 128:(j + 1) * 128], vhi[:, j * 128:(j + 1) * 128], k.ident_bf[:])
                for j in range(4):
                    last = t.transpose(k.psb[:, 512 + j * 128: 512 + (j + 1) * 128],
                                       vlo[:, j * 128:(j + 1) * 128], k.ident_bf[:])
                return last
            pttw = ttw = k.t_op(tt)
            k.wait("vector", ttw, *( (pdv1, pdv2) if pdv2 else () ))
            v4 = k.v_op(lambda v: v.tensor_copy(out=vT[:], in_=k.psb[:]))
            psb_w = (v4,)
            pdv1 = k.dma("gpsimd", k.v_s[hp, 0, lt * 4:(lt + 1) * 4, :, :].transpose([1, 0, 2]),
                         vT[:, 0:512], waits=(v4,))
            pdv2 = k.dma("gpsimd", k.v_s[hp, 1, lt * 4:(lt + 1) * 4, :, :].transpose([1, 0, 2]),
                         vT[:, 512:1024], waits=(v4,))
            last_t = tvw
        tensor_last[b] = (last_t,)
    k.barrier()



def emit_a1(k):
    gdone = k.queue_fence("sg")
    qhi_t = k.sb("a1qhi", (128, TOK), BF16)
    qlo_t = k.sb("a1qlo", (128, TOK), BF16)
    xo = [k.sb(f"a1xo{i}", (128, TOK), F32) for i in range(2)]
    oh = k.sb("a1oh", (128, NDC * TOK), BF16)
    ol = k.sb("a1ol", (128, NDC * TOK), BF16)
    wqh = k.sb("a1wqh", (128, NDC * 128), BF16)
    wql = k.sb("a1wql", (128, NDC * 128), BF16)
    qf = k.sb("a1qf", (128, TOK), F32)
    qsw = k.sb("a1qsw", (128, TOK), F32)
    ta = k.sb("a1ta", (128, TOK), F32)
    tb = k.sb("a1tb", (128, TOK), F32)
    v1 = None
    prev_v = [(), ()]
    for ch in range(NDC):
        xb = ch % 2
        ld = k.dma("sync", xo[xb][:], k.inp["x_own"][ch, :, :], waits=prev_v[xb])
        sl = slice(ch * TOK, (ch + 1) * TOK)
        k.wait("scalar", ld)
        s1 = k.s_op(lambda s, sl=sl, xb=xb: s.copy(out=oh[:, sl], in_=xo[xb][:]))
        k.wait("vector", s1)
        v1 = k.v_op(lambda v, sl=sl, xb=xb: v.tensor_tensor(out=ol[:, sl], in0=xo[xb][:], in1=oh[:, sl],
                                                            op=OP.subtract))
        prev_v[xb] = (v1,)
    split_all = v1
    bank_w = {}
    wq_ld = ()
    ps2 = None
    pqd1 = pqd2 = None
    prev_swp = ()
    for oc in range(NDC):
        b = oc % 2
        lh = k.dma("sync", wqh[:, 0:NDC * 128],
                   k.wq_s[0, :, :, oc * 128:(oc + 1) * 128].transpose([1, 0, 2]),
                   waits=wq_ld + gdone)
        ll = k.dma("sync", wql[:, 0:NDC * 128],
                   k.wq_s[1, :, :, oc * 128:(oc + 1) * 128].transpose([1, 0, 2]),
                   waits=())
        evs = []
        for th in range(2):
            bk = 2 * b + th
            PQ = k.ps[bk]
            pk = bank_w.get(bk, ())
            def tq(t, th=th, pk=pk, lh=lh, ll=ll, split_all=split_all, PQ=PQ):
                t.wait_ge(k.sem[split_all[0]], split_all[1])
                t.wait_ge(k.sem[lh[0]], lh[1])
                t.wait_ge(k.sem[ll[0]], ll[1])
                for wn, wv in pk:
                    t.wait_ge(k.sem[wn], wv)
                last = None
                for dc in range(NDC):
                    wsl = slice(dc * 128, (dc + 1) * 128)
                    xsl_ = slice(dc * TOK + th * 512, dc * TOK + (th + 1) * 512)
                    last = mm3(t, PQ[:], wqh[:, wsl], wql[:, wsl], oh[:, xsl_], ol[:, xsl_],
                               start=(dc == 0), stop=(dc == NDC - 1))
                return last
            tw = k.t_op(tq)
            k.wait("vector", tw, *( (ps2,) if ps2 and th == 0 else () ))
            ev = k.v_op(lambda v, PQ=PQ, th=th: v.tensor_copy(out=qf[:, th * 512:(th + 1) * 512], in_=PQ[:]))
            evs.append(ev)
            bank_w[bk] = (ev,)
        wq_ld = (evs[1],)
        sw = []
        for bb in (0, 64):
            sw.append(k.dma("sync", qsw[bb:bb + 32, :], qf[bb + 32:bb + 64, :],
                            waits=((evs[1],) + prev_swp) if bb == 0 else ()))
            sw.append(k.dma("sync", qsw[bb + 32:bb + 64, :], qf[bb:bb + 32, :]))
        prev_swp = tuple(sw)
        k.wait("vector", *sw)
        def rope(v):
            for bb in (0, 64):
                x1 = slice(bb, bb + 32)
                x2 = slice(bb + 32, bb + 64)
                v.tensor_tensor(out=ta[x1, :], in0=qf[x1, :], in1=k.cos_q[x1, :], op=OP.mult)
                v.tensor_tensor(out=tb[x1, :], in0=qsw[x1, :], in1=k.sin_q[x1, :], op=OP.mult)
                v.tensor_tensor(out=ta[x2, :], in0=qsw[x2, :], in1=k.sin_q[x2, :], op=OP.mult)
                v.tensor_tensor(out=tb[x2, :], in0=qf[x2, :], in1=k.cos_q[x2, :], op=OP.mult)
            v.drain()
            v.tensor_tensor(out=qf[0:32, :], in0=ta[0:32, :], in1=tb[0:32, :], op=OP.subtract)
            v.tensor_tensor(out=qf[64:96, :], in0=ta[64:96, :], in1=tb[64:96, :], op=OP.subtract)
            v.drain()
            v.tensor_tensor(out=qf[32:64, :], in0=ta[32:64, :], in1=tb[32:64, :], op=OP.add)
            v.tensor_tensor(out=qf[96:128, :], in0=ta[96:128, :], in1=tb[96:128, :], op=OP.add)
        rw = k.v_op(rope)
        qsl = slice(oc * TOK, (oc + 1) * TOK)
        k.wait("scalar", rw, *( (pqd1,) if pqd1 else () ))
        ps2 = s2 = k.s_op(lambda s: s.copy(out=qhi_t[:], in_=qf[:]))
        k.wait("vector", s2, *( (pqd2,) if pqd2 else () ))
        v2 = k.v_op(lambda v: v.tensor_tensor(out=qlo_t[:], in0=qf[:], in1=qhi_t[:], op=OP.subtract))
        pqd1 = k.dma("gpsimd", k.q_s[0, :, qsl], qhi_t[:], waits=(s2,))
        pqd2 = k.dma("gpsimd", k.q_s[1, :, qsl], qlo_t[:], waits=(v2,))
    k.barrier()


def emit_b(k, es_att):
    gdone = k.queue_fence("sg")
    qh_ = [[[k.sb(f"bq{i}{h}{l}", (64, TOK), BF16) for l in range(2)] for h in range(2)]
           for i in range(2)]
    msk = k.sb("bmask", (128, 16 * 512), BF16)
    km = [[k.sb(f"bkm{i}{h}", (64, LSEQ), BF16) for h in range(2)] for i in range(2)]
    vm = [[k.sb(f"bvm{i}{h}", (128, 2048), BF16) for h in range(2)] for i in range(2)]
    p32 = k.sb("bp32", (128, 1024), F32)
    p_hi = k.sb("bphi", (128, 1024), BF16)
    p_lo = k.sb("bplo", (128, 1024), BF16)
    dsb = k.sb("bdsb", (33, 512), F32)
    rcp = k.sb("brcp", (33, 512), F32)
    rbc = k.sb("brbc", (128, 512), F32)
    atmp = k.sb("batmp", (128, 512), F32)
    mw = k.dma("sync", msk[:], k.inp["mask_fine"].transpose([1, 0, 2]))
    kload_gate = [(), ()]
    qgate = [(), ()]
    exp_done = {0: (), 1: ()}
    pv_gate = ()
    prev_vsplit = ()
    last_score = {0: None, 1: None}
    for m in range(B_MS):
        mb = m % 2
        pair, a = m // 2, m % 2
        lds = []
        for hl in range(2):
            lds.append(k.dma("sync", km[mb][hl][:], k.k_s[pair, hl, a * 64:(a + 1) * 64, :],
                             waits=((kload_gate[mb] + gdone) if hl == 0 else ())))
            lds.append(k.dma("gpsimd", vm[mb][hl][:],
                             k.v_s[pair, hl, :, :, a * 64:(a + 1) * 64].transpose([1, 0, 2]),
                             waits=((kload_gate[mb] + gdone) if (hl == 0) else ())))
        kld = tuple(lds)
        for P in (2 * m, 2 * m + 1):
            pb = P % 2
            qlds = []
            for head in range(2):
                for hl in range(2):
                    qlds.append(k.dma("sync", qh_[pb][head][hl][:],
                                      k.q_s[hl, head * 64:(head + 1) * 64, P * TOK:(P + 1) * TOK],
                                      waits=(qgate[pb] + (gdone if P < 2 else ()))
                                      if (head == 0 and hl == 0) else ()))
            qld = tuple(qlds)
            for T in range(2):
                nil = min(16 * T + 16, B_NIL)
                qcsl = slice(T * 512, (T + 1) * 512)
                pv_started = False
                for il in range(nil):
                    par = il % 2
                    SA = k.ps[2 * par]
                    SB = k.ps[2 * par + 1]
                    def tsc(t, par=par, il=il, SA=SA, SB=SB, mb=mb, pb=pb, qcsl=qcsl,
                            w_exp=exp_done[par], kld=kld, qld=qld):
                        for wn, wv in w_exp + kld + qld:
                            t.wait_ge(k.sem[wn], wv)
                        ksl = slice(il * 128, (il + 1) * 128)
                        kh, kl2 = km[mb][0], km[mb][1]
                        last = None
                        for head, ps_ in ((0, SA), (1, SB)):
                            qa_hi = qh_[pb][head][0]
                            qa_lo = qh_[pb][head][1]
                            t.matmul(ps_[:], kh[:, ksl], qa_hi[:, qcsl], start=True, stop=False)
                            t.matmul(ps_[:], kl2[:, ksl], qa_hi[:, qcsl], start=False, stop=False)
                            last = t.matmul(ps_[:], kh[:, ksl], qa_lo[:, qcsl], start=False, stop=True)
                        return last
                    tscw = k.t_op(tsc)
                    last_score[mb] = tscw
                    wv_pre = tscw
                    if il >= 16 * T:
                        sub = il - 16 * T
                        k.wait("vector", tscw, mw)
                        wv_pre = k.v_op(lambda v, SA=SA, SB=SB, sub=sub: (
                            v.tensor_tensor(out=SA[:], in0=SA[:], in1=msk[:, sub * 512:(sub + 1) * 512],
                                            op=OP.add),
                            v.drain(),
                            v.tensor_tensor(out=SB[:], in0=SB[:], in1=msk[:, sub * 512:(sub + 1) * 512],
                                            op=OP.add)))
                    k.wait("scalar", wv_pre, mw, *prev_vsplit)
                    se = k.s_op(lambda s, SA=SA, SB=SB: (
                        s.activation(out=p32[:, 0:512], in_=SA[:], func=AF.Exp, scale=0.125),
                        s.drain(),
                        s.activation(out=p32[:, 512:1024], in_=SB[:], func=AF.Exp, scale=0.125)))
                    exp_done[par] = (se,)
                    k.wait("scalar", se, *pv_gate)
                    sh = k.s_op(lambda s: s.copy(out=p_hi[:], in_=p32[:]))
                    k.wait("vector", sh, *pv_gate)
                    vsplit = k.v_op(lambda v: v.tensor_tensor(out=p_lo[:], in0=p32[:], in1=p_hi[:],
                                                              op=OP.subtract))
                    prev_vsplit = (vsplit,)
                    def tpv(t, il=il, mb=mb, vsplit=vsplit, pv_started=pv_started,
                            stop=(il == nil - 1)):
                        t.wait_ge(k.sem[vsplit[0]], vsplit[1])
                        vsl = slice(il * 64, (il + 1) * 64)
                        PV, DEN = k.ps[4], k.ps[5]
                        st = not pv_started
                        vh, vl = vm[mb][0], vm[mb][1]
                        for (vv, pp, fst, lst) in ((vh, p_hi, st, False), (vh, p_lo, False, False),
                                                   (vl, p_hi, False, stop)):
                            t.matmul(PV[0:64, :], vv[:, vsl], pp[:, 0:512], start=fst, stop=lst,
                                     tile_position=(0, 0))
                            t.matmul(PV[64:128, :], vv[:, vsl], pp[:, 512:1024], start=fst, stop=lst,
                                     tile_position=(0, 64), skip_group_check=True)
                        t.matmul(DEN[0:1, :], k.ones_bf[:, 0:1], p_hi[:, 0:512], start=st, stop=False,
                                 tile_position=(0, 0))
                        t.matmul(DEN[0:1, :], k.ones_bf[:, 0:1], p_lo[:, 0:512], start=False, stop=stop,
                                 tile_position=(0, 0))
                        t.matmul(DEN[32:33, :], k.ones_bf[:, 0:1], p_hi[:, 512:1024], start=st,
                                 stop=False, tile_position=(0, 32), skip_group_check=True)
                        last = t.matmul(DEN[32:33, :], k.ones_bf[:, 0:1], p_lo[:, 512:1024], start=False,
                                        stop=stop, tile_position=(0, 32), skip_group_check=True)
                        return last
                    tpvw = k.t_op(tpv)
                    pv_gate = (tpvw,)
                    pv_started = True
                k.wait("scalar", pv_gate[0])
                sden = k.s_op(lambda s: (s.copy(out=dsb[0:1, :], in_=k.ps[5][0:1, :]), s.drain(),
                                         s.copy(out=dsb[32:33, :], in_=k.ps[5][32:33, :])))
                k.wait("vector", sden)
                vr = k.v_op(lambda v: (v.reciprocal(out=rcp[0:1, :], in_=dsb[0:1, :]),
                                       v.drain(),
                                       v.reciprocal(out=rcp[32:33, :], in_=dsb[32:33, :])))
                def tbc(t, vr=vr):
                    t.wait_ge(k.sem[vr[0]], vr[1])
                    t.matmul(k.ps7[0:64, :], k.ones_f[0:1, 0:64], rcp[0:1, :], start=True, stop=True,
                             tile_position=(0, 0))
                    return t.matmul(k.ps7[64:128, :], k.ones_f[32:33, 0:64], rcp[32:33, :], start=True,
                                    stop=True, tile_position=(32, 64), skip_group_check=True)
                tb_ = k.t_op(tbc)
                k.wait("vector", tb_)
                vb = k.v_op(lambda v: v.tensor_copy(out=rbc[:], in_=k.ps7[:]))
                va = k.v_op(lambda v: v.tensor_tensor(out=atmp[:], in0=k.ps[4][:], in1=rbc[:], op=OP.mult))
                acsl = slice(P * TOK + T * 512, P * TOK + (T + 1) * 512)
                k.wait("scalar", va)
                sa_ = k.s_op(lambda s, acsl=acsl: s.copy(out=k.att_hi[:, acsl], in_=atmp[:]))
                k.wait("vector", sa_)
                vl_ = k.v_op(lambda v, acsl=acsl: v.tensor_tensor(out=k.att_lo[:, acsl], in0=atmp[:],
                                                                  in1=k.att_hi[:, acsl], op=OP.subtract))
                pv_gate = (va,)
            qgate[pb] = (last_score[mb],)
        kload_gate[mb] = (va,)
    k.barrier()


def emit_router(k, xln_hi, xln_lo, combT):
    lgf = k.sb("r_lgf", (16, 1024), F32)
    lgt = k.sb("r_lgt", (128, 128), F32)
    nm = k.sb("r_nm", (128, 8), F32)
    ex = k.sb("r_ex", (128, 16), F32)
    pp = k.sb("r_pp", (128, 16), F32)
    wk = k.sb("r_wk", (128, 16), F32)
    msk_ = k.sb("r_msk", (128, 16), F32)
    cmb = k.sb("r_cmb", (128, 128), F32)
    ve = None
    for th in range(2):
        def tl(t, th=th, ve=ve):
            if ve is not None:
                t.wait_ge(k.sem[ve[0]], ve[1])
            last = None
            for dc in range(NDC):
                gsl = slice(dc * 16, (dc + 1) * 16)
                xsl = slice(dc * TOK + th * 512, dc * TOK + (th + 1) * 512)
                last = mm3(t, k.ps[0][0:16, :], k.gw_hi[:, gsl], k.gw_lo[:, gsl],
                           xln_hi[:, xsl], xln_lo[:, xsl], start=(dc == 0), stop=(dc == NDC - 1))
            return last
        tw = k.t_op(tl)
        k.wait("vector", tw)
        ve = k.v_op(lambda v, th=th: v.tensor_copy(out=lgf[:, th * 512:(th + 1) * 512], in_=k.ps[0][0:16, :]))
    vt = None
    for tb in range(8):
        def tt(t, tb=tb, ve=ve, vt=vt):
            t.wait_ge(k.sem[ve[0]], ve[1])
            if vt is not None:
                t.wait_ge(k.sem[vt[0]], vt[1])
            return t.transpose(k.ps[1][:, 0:16], lgf[:, tb * 128:(tb + 1) * 128], k.ident_f[0:16, 0:16])
        tw = k.t_op(tt)
        k.wait("vector", tw)
        vt = k.v_op(lambda v, tb=tb: v.tensor_copy(out=lgt[:, tb * 16:(tb + 1) * 16], in_=k.ps[1][:, 0:16]))
    vcm = None
    for tb in range(8):
        tsl = slice(tb * 16, (tb + 1) * 16)
        v1 = k.v_op(lambda v, tsl=tsl, tb=tb: v.reduce_max(out=nm[:, tb:tb + 1], in_=lgt[:, tsl],
                                                           axis=mybir.AxisListType.X, negate=True))
        k.wait("scalar", v1)
        s1 = k.s_op(lambda s, tsl=tsl, tb=tb: s.activation(out=ex[:], in_=lgt[:, tsl], func=AF.Exp,
                                                           bias=nm[:, tb:tb + 1], scale=1.0))
        k.wait("vector", s1)
        def vtop(v):
            v.reduce_sum(out=wk[:, 0:1], in_=ex[:], axis=mybir.AxisListType.X)
            v.drain()
            v.reciprocal(out=wk[:, 1:2], in_=wk[:, 0:1])
            v.drain()
            v.tensor_scalar(out=pp[:], in0=ex[:], scalar1=wk[:, 1:2], scalar2=None, op0=OP.mult)
            v.drain()
            v.tensor_copy(out=wk[:], in_=pp[:])
            v.drain()
            for r in range(4):
                v.reduce_max(out=msk_[:, 0:1], in_=wk[:], axis=mybir.AxisListType.X)
                v.drain()
                if r < 3:
                    v.tensor_scalar(out=msk_[:], in0=wk[:], scalar1=msk_[:, 0:1], scalar2=None,
                                    op0=OP.is_ge)
                    v.drain()
                    v.scalar_tensor_tensor(out=wk[:], in0=msk_[:], scalar=NEG, in1=wk[:],
                                           op0=OP.mult, op1=OP.add)
                    v.drain()
            v.tensor_scalar(out=msk_[:], in0=pp[:], scalar1=msk_[:, 0:1], scalar2=None, op0=OP.is_ge)
            v.drain()
            v.tensor_tensor(out=pp[:], in0=pp[:], in1=msk_[:], op=OP.mult)
            v.drain()
            v.reduce_sum(out=wk[:, 2:3], in_=pp[:], axis=mybir.AxisListType.X)
            v.drain()
            v.reciprocal(out=wk[:, 3:4], in_=wk[:, 2:3])
            v.drain()
            v.tensor_scalar(out=pp[:], in0=pp[:], scalar1=wk[:, 3:4], scalar2=None, op0=OP.mult)
        v2 = k.v_op(vtop)
        vcm = k.v_op(lambda v, tb=tb: v.tensor_copy(out=cmb[:, tb * 16:(tb + 1) * 16], in_=pp[:]))
    vt2 = None
    for tb in range(8):
        def tt2(t, tb=tb, vcm=vcm, vt2=vt2):
            t.wait_ge(k.sem[vcm[0]], vcm[1])
            if vt2 is not None:
                t.wait_ge(k.sem[vt2[0]], vt2[1])
            return t.transpose(k.ps[1][0:16, 0:128], cmb[:, tb * 16:(tb + 1) * 16], k.ident_f[:])
        tw = k.t_op(tt2)
        k.wait("vector", tw)
        vt2 = k.v_op(lambda v, tb=tb: v.tensor_copy(out=combT[:, tb * 128:(tb + 1) * 128],
                                                    in_=k.ps[1][0:16, 0:128]))
    return vt2


def emit_moe(k, xln_hi, xln_lo, combT):
    bt = [k.sb(f"d_bt{i}", (128, 1024), I32) for i in range(2)]
    srep = [k.sb(f"d_sr{i}", (128, 1024), F32) for i in range(2)]
    b0 = k.sb("d_b0", (128, 1024), I32)
    b1 = k.sb("d_b1", (128, 1024), I32)
    b2 = k.sb("d_b2", (128, 1024), I32)
    wf0 = k.sb("d_wf", (128, 1024), F32)
    wf = [wf0, wf0]
    wwh = [k.sb(f"d_wwh{i}", (128, NDC * 256), BF16) for i in range(2)]
    wwl = [k.sb(f"d_wwl{i}", (128, NDC * 256), BF16) for i in range(2)]
    wdh = wwh
    wdl = wwl
    gbc = k.sb("d_gbc", (128, 1024), F32)
    sgb = k.sb("d_sgb", (128, 512), F32)
    act_hi = k.sb("d_ahi", (128, 12 * TOK), BF16)
    act_lo = k.sb("d_alo", (128, 12 * TOK), BF16)
    af = k.sb("d_af", (128, 512), F32)
    ob = k.sb("d_ob", (128, 512), F32)
    ohb = k.sb("d_ohb", (16, 128), F32)

    state = {"last_acc": k.acc_init, "prev_gbc": None, "act_ready": None}
    prev_ldb = [(), ()]
    wbuf_gate = [(), ()]

    def _chain(v, bts, sreps, n, wfp):
        v.tensor_scalar(out=b0[0:64, 0:n], in0=bts[0:64, 0:n], scalar1=15, scalar2=None,
                        op0=OP.bitwise_and)
        v.drain()
        v.tensor_scalar(out=b0[64:128, 0:n], in0=bts[64:128, 0:n], scalar1=4, scalar2=None,
                        op0=OP.logical_shift_right)
        v.drain()
        v.tensor_scalar(out=b1[:, 0:n], in0=b0[:, 0:n], scalar1=7, scalar2=None, op0=OP.bitwise_and)
        v.drain()
        v.tensor_scalar(out=b2[:, 0:n], in0=b0[:, 0:n], scalar1=8, scalar2=28, op0=OP.bitwise_and,
                        op1=OP.logical_shift_left)
        v.drain()
        v.tensor_scalar(out=b0[:, 0:n], in0=b1[:, 0:n], scalar1=1.0, scalar2=None, op0=OP.min)
        v.drain()
        v.scalar_tensor_tensor(out=b0[:, 0:n], in0=b0[:, 0:n], scalar=251, in1=b1[:, 0:n],
                               op0=OP.mult, op1=OP.add)
        v.drain()
        v.tensor_scalar(out=b1[:, 0:n], in0=b1[:, 0:n], scalar1=1, scalar2=None,
                        op0=OP.logical_shift_right)
        v.drain()
        v.tensor_scalar(out=b1[:, 0:n], in0=b1[:, 0:n], scalar1=1.0, scalar2=None, op0=OP.min)
        v.drain()
        v.tensor_tensor(out=b0[:, 0:n], in0=b0[:, 0:n], in1=b1[:, 0:n], op=OP.add)
        v.drain()
        v.tensor_scalar(out=b0[:, 0:n], in0=b0[:, 0:n], scalar1=22, scalar2=None,
                        op0=OP.logical_shift_left)
        v.drain()
        v.tensor_tensor(out=b0[:, 0:n], in0=b0[:, 0:n], in1=b2[:, 0:n], op=OP.bitwise_or)
        v.drain()
        v.tensor_tensor(out=wfp[:, 0:n], in0=b0[:, 0:n].bitcast(F32), in1=sreps[:, 0:n], op=OP.mult)

    def emit_deq(H):
        kind, w, h, par = H["kind"], H["w"], H["h"], H["par"]
        gate = wbuf_gate[par]
        vl = None
        nrows = NDC if kind == "gu" else 12
        for g in range(0, nrows, 4):
            db = (g // 4) % 2
            lds = []
            for j in range(4):
                r = g + j
                if kind == "gu":
                    nc_gate, nc_up = 2 * w + h, 2 * w + 12 + h
                    lds.append(k.dma("gpsimd", bt[db][:, j * 256:j * 256 + 128],
                                     H["bap"](r, slice(nc_gate * 128, (nc_gate + 1) * 128)),
                                     waits=prev_ldb[db] if j == 0 else ()))
                    lds.append(k.dma("gpsimd", bt[db][:, j * 256 + 128:(j + 1) * 256],
                                     H["bap"](r, slice(nc_up * 128, (nc_up + 1) * 128))))
                    lds.append(k.dma("sync", srep[db][:, j * 256:j * 256 + 128],
                                     H["sap"](r, slice(nc_gate * 128, (nc_gate + 1) * 128))
                                     .broadcast_to([128, 128]),
                                     waits=prev_ldb[db] if j == 0 else ()))
                    lds.append(k.dma("sync", srep[db][:, j * 256 + 128:(j + 1) * 256],
                                     H["sap"](r, slice(nc_up * 128, (nc_up + 1) * 128))
                                     .broadcast_to([128, 128])))
                else:
                    cs = slice(w * 512 + h * 256, w * 512 + (h + 1) * 256)
                    lds.append(k.dma("gpsimd", bt[db][:, j * 256:(j + 1) * 256], H["bap"](r, cs),
                                     waits=prev_ldb[db] if j == 0 else ()))
                    lds.append(k.dma("sync", srep[db][:, j * 256:(j + 1) * 256],
                                     H["sap"](r, cs).broadcast_to([128, 256]),
                                     waits=prev_ldb[db] if j == 0 else ()))
            k.wait("vector", *lds, *( gate if g == 0 else () ))
            vc = k.v_op(lambda v, db=db, par=par: _chain(v, bt[db], srep[db], 1024, wf[par]))
            for j in range(4):
                r = g + j
                jsl = slice(j * 256, (j + 1) * 256)
                if kind == "gu":
                    dh = wwh[par][:, r * 256:(r + 1) * 256]
                    dl = wwl[par][:, r * 256:(r + 1) * 256]
                else:
                    dh = wdh[par][:, r * 256:(r + 1) * 256]
                    dl = wdl[par][:, r * 256:(r + 1) * 256]
                k.wait("scalar", vc, *( gate if (g == 0 and j == 0) else () ))
                sh = k.s_op(lambda s, d=dh, jsl=jsl, par=par: s.copy(out=d, in_=wf[par][:, jsl]))
                k.wait("vector", sh)
                vl = k.v_op(lambda v, d=dl, dh2=dh, jsl=jsl, par=par: v.tensor_tensor(
                    out=d, in0=wf[par][:, jsl], in1=dh2, op=OP.subtract))
            prev_ldb[db] = (vl,)
        return vl

    def emit_mm(H, deq_done):
        kind, w, h, par, bp = H["kind"], H["w"], H["h"], H["par"], H["bp"]
        outs = []
        if kind == "gu":
            for th in (H["th"],):
                PG = k.ps[2 * bp]
                PU = k.ps[2 * bp + 1]
                def tgu(t, th=th, par=par, PG=PG, PU=PU, deq_done=deq_done,
                        gate=k.moe_bankw.get(2 * bp, ()) + k.moe_bankw.get(2 * bp + 1, ())):
                    t.wait_ge(k.sem[deq_done[0]], deq_done[1])
                    for wn, wv in gate:
                        t.wait_ge(k.sem[wn], wv)
                    lastg = None
                    for dc in range(NDC):
                        wslg = slice(dc * 256, dc * 256 + 128)
                        wslu = slice(dc * 256 + 128, (dc + 1) * 256)
                        xsl = slice(dc * TOK + th * 512, dc * TOK + (th + 1) * 512)
                        mm3(t, PG[:], wwh[par][:, wslg], wwl[par][:, wslg],
                            xln_hi[:, xsl], xln_lo[:, xsl], start=(dc == 0), stop=(dc == NDC - 1))
                        lastg = mm3(t, PU[:], wwh[par][:, wslu], wwl[par][:, wslu],
                                    xln_hi[:, xsl], xln_lo[:, xsl], start=(dc == 0),
                                    stop=(dc == NDC - 1))
                    return lastg
                tw = k.t_op(tgu)
                outs.append((th, PG, PU, tw))
                k.moe_bankw[2 * bp] = (tw,)
                k.moe_bankw[2 * bp + 1] = (tw,)
        else:
            hcl = H["hcl"]
            for th in range(2):
                bidx = 2 * bp + th
                PB = k.ps[bidx]
                def tdn(t, h=h, hcl=hcl, th=th, par=par, PB=PB, deq_done=deq_done,
                        gate=k.moe_bankw.get(bidx, ()), act_ready=state["act_ready"]):
                    t.wait_ge(k.sem[deq_done[0]], deq_done[1])
                    t.wait_ge(k.sem[act_ready[0]], act_ready[1])
                    for wn, wv in gate:
                        t.wait_ge(k.sem[wn], wv)
                    last = None
                    for ic in range(12):
                        wsl = slice(ic * 256 + hcl * 128, ic * 256 + (hcl + 1) * 128)
                        asl = slice(ic * TOK + th * 512, ic * TOK + (th + 1) * 512)
                        last = mm3(t, PB[:], wdh[par][:, wsl], wdl[par][:, wsl],
                                   act_hi[:, asl], act_lo[:, asl],
                                   start=(ic == 0), stop=(ic == 11))
                    return last
                tw = k.t_op(tdn)
                outs.append((hcl, th, PB, tw))
                k.moe_bankw[2 * bp + th] = (tw,)
        return outs

    def emit_evac(H, outs):
        kind, w, h, par = H["kind"], H["w"], H["h"], H["par"]
        shared = H["shared"]
        last = None
        if kind == "gu":
            nc_gate = 2 * w + h
            for (th, PG, PU, tw) in outs:
                k.wait("scalar", tw)
                s1 = k.s_op(lambda s, PG=PG: s.activation(out=sgb[:], in_=PG[:], func=AF.Silu))
                asl = slice(nc_gate * TOK + th * 512, nc_gate * TOK + (th + 1) * 512)
                k.wait("vector", s1, *( (H["gbc_ready"],) if H["gbc_ready"] else () ))
                if shared:
                    va = k.v_op(lambda v, PU=PU: v.tensor_tensor(out=af[:], in0=sgb[:], in1=PU[:],
                                                                 op=OP.mult))
                else:
                    va = k.v_op(lambda v, PU=PU, th=th: (
                        v.tensor_tensor(out=af[:], in0=sgb[:], in1=PU[:], op=OP.mult),
                        v.drain(),
                        v.tensor_tensor(out=af[:], in0=af[:], in1=gbc[:, th * 512:(th + 1) * 512],
                                        op=OP.mult)))
                k.wait("scalar", va)
                sh = k.s_op(lambda s, asl=asl: s.copy(out=act_hi[:, asl], in_=af[:]))
                k.wait("vector", sh)
                last = k.v_op(lambda v, asl=asl: v.tensor_tensor(
                    out=act_lo[:, asl], in0=af[:], in1=act_hi[:, asl], op=OP.subtract))
            state["act_ready"] = last
            k.moe_bankw[2 * H["bp"]] = (last,)
            k.moe_bankw[2 * H["bp"] + 1] = (last,)
        else:
            for (hcl, th, PB, tw) in outs:
                hc = w * 4 + h * 2 + hcl
                k.wait("vector", tw, state["last_acc"])
                vo = k.v_op(lambda v, PB=PB: v.tensor_copy(out=ob[:], in_=PB[:]))
                state["last_acc"] = k.dma("gpsimd", k.acc_s[hc, :, th * 512:(th + 1) * 512], ob[:],
                                          waits=(vo,), accum=OP.add)
                last = vo
                k.moe_bankw[2 * H["bp"] + th] = (last,)
        wbuf_gate[par] = (last,)
        return last

    halves = []
    for e in range(E + 1):
        shared = (e == E)
        if shared:
            bap_g = lambda dc, cs: k.inp["sgup_b"][dc, :, cs]
            sap_g = lambda dc, cs: k.inp["sgus"][dc:dc + 1, cs]
            bap_d = lambda ic, cs: k.inp["sdnp_b"][ic, :, cs]
            sap_d = lambda ic, cs: k.inp["sdns"][ic:ic + 1, cs]
        else:
            bap_g = lambda dc, cs, e=e: k.inp["gup_b"][e, dc, :, cs]
            sap_g = lambda dc, cs, e=e: k.inp["gus"][e, dc:dc + 1, cs]
            bap_d = lambda ic, cs, e=e: k.inp["dnp_b"][e, ic, :, cs]
            sap_d = lambda ic, cs, e=e: k.inp["dns"][e, ic:ic + 1, cs]
        for w in range(6):
            for h in range(2):
                for th in range(2):
                    halves.append({"kind": "gu", "e": e, "w": w, "h": h, "th": th,
                                   "shared": shared, "bap": bap_g, "sap": sap_g,
                                   "deq": (th == 0), "hcl": None})
        for hw in range(4):
            for h in range(2):
                for hcl in range(2):
                    halves.append({"kind": "dn", "e": e, "w": hw, "h": h, "th": None,
                                   "shared": shared, "bap": bap_d, "sap": sap_d,
                                   "deq": (hcl == 0), "hcl": hcl})
    np_ = 0
    for i, H in enumerate(halves):
        if H["deq"]:
            np_ += 1
        H["par"] = np_ % 2
        H["bp"] = i % 2

    def emit_gbc(e):
        ohl = k.dma("sync", ohb[:], k.inp["oh_rows"][:, e * 128:(e + 1) * 128],
                    waits=(state["prev_gbc"],) if state["prev_gbc"] else ())
        vg = None
        w1 = None
        for th in range(2):
            def tgb(t, th=th, vg=vg, ohl=ohl):
                t.wait_ge(k.sem[ohl[0]], ohl[1])
                if vg is not None:
                    t.wait_ge(k.sem[vg[0]], vg[1])
                return t.matmul(k.ps7[:], ohb[:, :], combT[0:16, th * 512:(th + 1) * 512],
                                start=True, stop=True, tile_position=(0, 0))
            w1 = k.t_op(tgb)
            k.wait("vector", w1)
            vg = k.v_op(lambda v, th=th: v.tensor_copy(out=gbc[:, th * 512:(th + 1) * 512],
                                                       in_=k.ps7[:]))
        state["prev_gbc"] = w1
        return vg

    pend = None
    cur_e = -1
    gbc_ready = None
    for H in halves:
        if H["e"] != cur_e:
            if pend is not None:
                emit_evac(*pend)
                pend = None
            cur_e = H["e"]
            gbc_ready = None if H["shared"] else emit_gbc(cur_e)
        H["gbc_ready"] = gbc_ready
        if H["deq"]:
            dq = emit_deq(H)
            last_dq = dq
        else:
            dq = last_dq
        if pend is not None:
            emit_evac(*pend)
        outs = emit_mm(H, dq)
        pend = (H, outs)
    emit_evac(*pend)
    k.last_acc = state["last_acc"]
    k.barrier()


def emit_f(k, out32, ready):
    ob = k.sb("f_ob", (128, 2048), F32)
    pw = (ready,)
    obw = ()
    last = None
    for tb in range(8):
        for g4 in range(4):
            def tt(t, tb=tb, g4=g4, pw=pw):
                for wn, wv in pw:
                    t.wait_ge(k.sem[wn], wv)
                last_ = None
                for j in range(4):
                    dc = g4 * 4 + j
                    last_ = t.transpose(k.ps[4][:, j * 128:(j + 1) * 128],
                                        out32[:, dc * TOK + tb * 128: dc * TOK + (tb + 1) * 128],
                                        k.ident_f[:])
                return last_
            tw = k.t_op(tt)
            k.wait("vector", tw, *obw)
            vv = k.v_op(lambda v, g4=g4: v.tensor_copy(out=ob[:, g4 * 512:(g4 + 1) * 512], in_=k.ps[4][:]))
            pw = (vv,)
        last = k.dma("sync", k.out[tb, :, :], ob[:], waits=(pw[0],))
        obw = (last,)
    k.wait("sync", *k.queue_fence("sd"))
    k.wait("sync", *k.queue_fence("sg"))


def emit_stub_out(k):
    from contextlib import ExitStack
    es = ExitStack()
    ob = k.sb_on(es, "stub_ob", (128, 2048), F32)
    z = k.v_op(lambda v: v.memset(ob[:], 0.0))
    for tb in range(8):
        k.dma("sync", k.out[tb, :, :], ob[:], waits=(z,))
    k.wait("sync", *k.queue_fence("sd"))
    k.wait("sync", *k.queue_fence("sg"))
    es.close()

def build_nc(phases=99):
    from contextlib import ExitStack
    nc = bass.Bass("TRN2", target_bir_lowering=False)
    k = declare_io(nc)
    k.moe_bankw = {}
    with ExitStack() as es0:
        k.es = es0
        for s in ["sv", "ss", "st", "sgp", "sb"]:
            k.mksem(s)
        for base in ("sd", "sg", "sa"):
            for i in range(KB.NSLOT):
                k.mksem(f"{base}{i}")
        k.ps = [es0.enter_context(nc.psum_tensor(f"ps{i}", [128, 512], F32)) for i in range(6)]
        k.psb = es0.enter_context(nc.psum_tensor("psb", [128, 1024], BF16))
        k.ps7 = es0.enter_context(nc.psum_tensor("ps7", [128, 512], F32))
        with nc.Block() as block:
            es_rope = ExitStack()
            es_rtmp = ExitStack()
            es_wsp = ExitStack()
            emit_prologue(k, es_rope, es_rtmp, es_wsp)
            k.barrier()
            es_wsp.close()
            es_rtmp.close()
            if phases >= 2:
                es_a2 = ExitStack(); k.es = es_a2
                emit_a2(k)
                es_a2.close()
            if phases >= 3:
                es_a1 = ExitStack(); k.es = es_a1
                emit_a1(k)
                es_a1.close()
            es_rope.close()
            if phases < 4:
                emit_stub_out(k)
                k.emit_into(block)
                return nc
            es_att = ExitStack()
            k.att_hi = k.sb_on(es_att, "atth", (128, NDC * TOK), BF16)
            k.att_lo = k.sb_on(es_att, "attl", (128, NDC * TOK), BF16)
            es_b = ExitStack(); k.es = es_b
            emit_b(k, es_att)
            es_b.close()
            if phases < 5:
                es_att.close()
                emit_stub_out(k)
                k.emit_into(block)
                return nc
            es_c = ExitStack(); k.es = es_c
            emit_c(k)
            es_c.close()
            es_att.close()
            if phases < 6:
                emit_stub_out(k)
                k.emit_into(block)
                return nc
            es_mid = ExitStack()
            k.xln_hi = k.sb_on(es_mid, "xlnh", (128, NDC * TOK), BF16)
            k.xln_lo = k.sb_on(es_mid, "xlnl", (128, NDC * TOK), BF16)
            es_ln1 = ExitStack(); k.es = es_ln1
            ln1_done, acc_d = emit_ln_stream(k, k.xres_s, pfx="n1", out32=None, out_hi=k.xln_hi,
                                             out_lo=k.xln_lo, acc_hbm=k.acc_s,
                                             src_wait=k.c_spill)
            k.acc_init = acc_d
            k.barrier()
            es_ln1.close()
            if phases < 7:
                es_mid.close()
                emit_stub_out(k)
                k.emit_into(block)
                return nc
            es_d = ExitStack(); k.es = es_d
            combT = k.sb("combT", (16, 1024), F32)
            emit_router(k, k.xln_hi, k.xln_lo, combT)
            k.barrier()
            if phases >= 8:
                emit_moe(k, k.xln_hi, k.xln_lo, combT)
            else:
                k.last_acc = k.acc_init
            es_d.close()
            es_mid.close()
            if phases < 9:
                emit_stub_out(k)
                k.emit_into(block)
                return nc
            es_ln2 = ExitStack(); k.es = es_ln2
            k.out32 = k.sb("o32", (128, NDC * TOK), F32)
            es_ln2b = ExitStack(); k.es = es_ln2b
            ln2_done, _ = emit_ln_stream(k, k.acc_s, pfx="n2", out32=k.out32, src_wait=(k.last_acc,))
            k.barrier()
            es_ln2b.close()
            es_f = ExitStack(); k.es = es_f
            emit_f(k, k.out32, ln2_done)
            es_f.close()
            es_ln2.close()
            k.emit_into(block)
    return nc


def perm_ef(n):
    idx = []
    for c in range(0, n, 128):
        idx += list(range(c, c + 128, 2)) + list(range(c + 1, c + 128, 2))
    return np.array(idx)


PD = perm_ef(2048)
PI = perm_ef(1536)
INV_PD = np.argsort(PD)


def perm_rope(nheads):
    out = []
    for h in range(nheads):
        out += [h * 64 + j for j in range(0, 64, 2)]
        out += [h * 64 + j for j in range(1, 64, 2)]
    return np.array(out)


PQR = perm_rope(32)
PKR = perm_rope(8)


def stage_bytes(packed, kdim):
    sh = packed.shape
    n = sh[-1]
    w = sh[-2]
    b = np.ascontiguousarray(packed).view(np.uint8).reshape(sh[:-2] + (w, n, 4))
    b = np.moveaxis(b, -1, -2)
    b = b.reshape(sh[:-2] + (w * 4, n))
    kc = kdim // 128
    b = b.reshape(sh[:-2] + (kc, 64, n))
    b = np.concatenate([b, b], axis=-2)
    return np.ascontiguousarray(b)


def make_inputs(core, inp):
    b, off = core // 4, core % 4
    m = {}
    x = np.asarray(inp["L0_attn_x"])
    m["x_fm"] = np.ascontiguousarray(x[b].T[PD].reshape(NDC, 128, LSEQ))
    m["x_own"] = np.ascontiguousarray(x[b, off::4, :].T[PD].reshape(NDC, 128, TOK))
    m["qpos"] = np.tile(np.arange(off, LSEQ, 4, dtype=np.float32)[None, :], (128, 1))
    import ml_dtypes
    sub = np.arange(16)[:, None, None]
    p = np.arange(128)[None, :, None]
    i = np.arange(512)[None, None, :]
    mask = np.where(128 * sub + p <= off + 4 * i, 0.0, NEG).astype(ml_dtypes.bfloat16)
    m["mask_fine"] = mask
    m["ident_bf"] = np.eye(128, dtype=ml_dtypes.bfloat16)
    m["ident_f32"] = np.eye(128, dtype=np.float32)
    m["ones_bf"] = np.ones((128, 128), dtype=ml_dtypes.bfloat16)
    m["ones_f32"] = np.ones((128, 128), dtype=np.float32)
    oh = np.zeros((16, E * 128), dtype=np.float32)
    for e in range(E):
        oh[e, e * 128:(e + 1) * 128] = 1.0
    m["oh_rows"] = oh
    inv64 = 1.0 / np.power(10000.0, np.arange(0, 64, 2, dtype=np.float64) / 64.0)
    inv64 = np.tile(inv64, 4)
    ih = inv64.astype(np.float32)
    il = (inv64 - ih.astype(np.float64)).astype(np.float32)
    m["inv_hi"] = ih.reshape(128, 1)
    m["inv_lo"] = il.reshape(128, 1)
    wq = np.asarray(inp["L0_attn_Wq"])
    wk = np.asarray(inp["L0_attn_Wk"])
    wv = np.asarray(inp["L0_attn_Wv"])
    wo = np.asarray(inp["L0_attn_Wo"])
    m["wqT"] = np.ascontiguousarray(wq[np.ix_(PQR, PD)].T.reshape(NDC, 128, 2048))
    m["wkT"] = np.ascontiguousarray(wk[np.ix_(PKR, PD)].T.reshape(NDC, 128, 512))
    m["wvT"] = np.ascontiguousarray(wv[:, PD].T.reshape(NDC, 128, 512))
    m["woT"] = np.ascontiguousarray(wo[PD, :].T.reshape(NDC, 128, 2048))
    m["gwT"] = np.ascontiguousarray(np.asarray(inp["L0_ffn_gate_w"])[:, PD].T.reshape(NDC, 128, 16))
    cgu = np.concatenate([PI, 1536 + PI])
    gup = np.asarray(inp["L0_ffn_gate_up_packed"])[:, :, cgu]
    m["gup_b"] = stage_bytes(gup, 2048)
    m["gus"] = np.ascontiguousarray(np.asarray(inp["L0_ffn_gate_up_scales"])[:, :, cgu])
    dnp = np.asarray(inp["L0_ffn_down_packed"])[:, :, PD]
    m["dnp_b"] = stage_bytes(dnp, 1536)
    m["dns"] = np.ascontiguousarray(np.asarray(inp["L0_ffn_down_scales"])[:, :, PD])
    m["sgup_b"] = stage_bytes(np.asarray(inp["L0_ffn_shared_gate_up_packed"])[:, cgu], 2048)
    m["sgus"] = np.ascontiguousarray(np.asarray(inp["L0_ffn_shared_gate_up_scales"])[:, cgu])
    m["sdnp_b"] = stage_bytes(np.asarray(inp["L0_ffn_shared_down_packed"])[:, PD], 1536)
    m["sdns"] = np.ascontiguousarray(np.asarray(inp["L0_ffn_shared_down_scales"])[:, PD])
    return m


def kernel(**inputs):
    from concourse.bass_utils import run_bass_kernel_spmd
    in_maps = [make_inputs(c, inputs) for c in range(8)]
    nc = build_nc(PHASES)
    res = run_bass_kernel_spmd(nc, in_maps, core_ids=list(range(8)))
    out = np.zeros((2, LSEQ, 2048), dtype=np.float32)
    for c in range(8):
        b, off = c // 4, c % 4
        o = np.asarray(res.results[c]["out"]).reshape(TOK, 2048)
        out[b, off::4, :] = o[:, INV_PD]
    return out


def emit_c(k):
    gdone = k.queue_fence("sg")
    woh = k.sb("c_woh", (128, NDC * 128), BF16)
    wol = k.sb("c_wol", (128, NDC * 128), BF16)
    xot = [k.sb(f"c_xot{i}", (128, 512), F32) for i in range(2)]
    xrt = [k.sb(f"c_xrt{i}", (128, 512), F32) for i in range(2)]
    bank_w = {}
    w_ld = ()
    prev_xr = [(), ()]
    prev_xo = [(), ()]
    for oc in range(NDC):
        b = oc % 2
        lh = k.dma("sync", woh[:], k.wo_s[0, :, :, oc * 128:(oc + 1) * 128].transpose([1, 0, 2]),
                   waits=w_ld + (gdone if oc == 0 else ()))
        ll = k.dma("sync", wol[:], k.wo_s[1, :, :, oc * 128:(oc + 1) * 128].transpose([1, 0, 2]))
        evs = []
        for th in range(2):
            bk = 2 * b + th
            PQ = k.ps[2 + bk]
            pk = bank_w.get(bk, ())
            xb = (2 * oc + th) % 2
            lx = k.dma("sync", xot[xb][:], k.inp["x_own"][oc, :, th * 512:(th + 1) * 512],
                       waits=prev_xo[xb])
            def tq(t, th=th, pk=pk, lh=lh, ll=ll, PQ=PQ):
                t.wait_ge(k.sem[lh[0]], lh[1])
                t.wait_ge(k.sem[ll[0]], ll[1])
                for wn, wv in pk:
                    t.wait_ge(k.sem[wn], wv)
                last = None
                for dc in range(NDC):
                    wsl = slice(dc * 128, (dc + 1) * 128)
                    asl = slice(dc * TOK + th * 512, dc * TOK + (th + 1) * 512)
                    last = mm3(t, PQ[:], woh[:, wsl], wol[:, wsl], k.att_hi[:, asl], k.att_lo[:, asl],
                               start=(dc == 0), stop=(dc == NDC - 1))
                return last
            tw = k.t_op(tq)
            k.wait("vector", tw, lx, *prev_xr[xb])
            ev = k.v_op(lambda v, PQ=PQ, xb=xb: v.tensor_tensor(out=xrt[xb][:], in0=xot[xb][:],
                                                                in1=PQ[:], op=OP.add))
            dx = k.dma("gpsimd", k.xres_s[oc, :, th * 512:(th + 1) * 512], xrt[xb][:], waits=(ev,))
            prev_xr[xb] = (dx,)
            prev_xo[xb] = (ev,)
            evs.append(ev)
            bank_w[bk] = (ev,)
        w_ld = (evs[1],)
    k.c_spill = k.queue_fence("sg")
    k.barrier()


def emit_ln_stream(k, src_hbm, out32=None, out_hi=None, out_lo=None, acc_hbm=None, src_wait=(), pfx="ln"):
    st_ = [k.sb(f"{pfx}lns{i}", (128, 512), F32) for i in range(2)]
    tmph = k.sb(f"{pfx}_th", (128, 512), BF16)
    tmpl = k.sb(f"{pfx}_tl", (128, 512), BF16)
    sqf = k.sb(f"{pfx}_sq", (128, 512), F32)
    sqh = k.sb(f"{pfx}_sqh", (128, 512), BF16)
    sql = k.sb(f"{pfx}_sql", (128, 512), BF16)
    rows = k.sb(f"{pfx}_rows", (1, 7 * 1024), F32)
    bca = k.sb(f"{pfx}_bca", (128, 512), F32)
    o32t = [k.sb(f"{pfx}ln_o{i}", (128, 512), F32) for i in range(2)]

    def R(i, th=None):
        if th is None:
            return rows[0:1, i * 1024:(i + 1) * 1024]
        return rows[0:1, i * 1024 + th * 512: i * 1024 + (th + 1) * 512]

    prev_use = [(), ()]
    prev_tmm = None
    vev = None
    for th in range(2):
        for dc in range(NDC):
            xb = dc % 2
            ld = k.dma("sync", st_[xb][:], src_hbm[dc, :, th * 512:(th + 1) * 512],
                       waits=prev_use[xb] + (tuple(src_wait) if (th == 0 and dc == 0) else ()))
            wts = (prev_tmm,) if prev_tmm else ()
            k.wait("scalar", ld, *wts)
            s1 = k.s_op(lambda s, xb=xb: (s.copy(out=tmph[:], in_=st_[xb][:]), s.drain(),
                                          s.activation(out=sqf[:], in_=st_[xb][:], func=AF.Square)))
            k.wait("vector", s1, *wts)
            v1 = k.v_op(lambda v, xb=xb: v.tensor_tensor(out=tmpl[:], in0=st_[xb][:], in1=tmph[:],
                                                         op=OP.subtract))
            prev_use[xb] = (v1,)
            s2 = k.s_op(lambda s: s.copy(out=sqh[:], in_=sqf[:]))
            k.wait("vector", s2)
            v2 = k.v_op(lambda v: v.tensor_tensor(out=sql[:], in0=sqf[:], in1=sqh[:], op=OP.subtract))
            def tmm(t, dc=dc, v2=v2, vev=vev):
                t.wait_ge(k.sem[v2[0]], v2[1])
                if dc == 0 and vev is not None:
                    t.wait_ge(k.sem[vev[0]], vev[1])
                t.matmul(k.ps[0][0:1, :], k.ones_bf[:, 0:1], tmph[:], start=(dc == 0), stop=False,
                         tile_position=(0, 0))
                t.matmul(k.ps[0][0:1, :], k.ones_bf[:, 0:1], tmpl[:], start=False, stop=(dc == NDC - 1),
                         tile_position=(0, 0))
                t.matmul(k.ps[1][0:1, :], k.ones_bf[:, 0:1], sqh[:], start=(dc == 0), stop=False,
                         tile_position=(0, 0))
                return t.matmul(k.ps[1][0:1, :], k.ones_bf[:, 0:1], sql[:], start=False,
                                stop=(dc == NDC - 1), tile_position=(0, 0))
            prev_tmm = k.t_op(tmm)
        k.wait("vector", prev_tmm)
        vev = k.v_op(lambda v, th=th: (
            v.tensor_copy(out=R(0, th), in_=k.ps[0][0:1, :]),
            v.drain(),
            v.tensor_copy(out=R(1, th), in_=k.ps[1][0:1, :])))
    def vrow(v):
        v.tensor_scalar(out=R(2), in0=R(0), scalar1=1.0 / 2048.0, scalar2=None, op0=OP.mult)
        v.drain()
        v.tensor_scalar(out=R(3), in0=R(1), scalar1=1.0 / 2048.0, scalar2=None, op0=OP.mult)
        v.drain()
        v.tensor_tensor(out=R(4), in0=R(2), in1=R(2), op=OP.mult)
        v.drain()
        v.tensor_tensor(out=R(3), in0=R(3), in1=R(4), op=OP.subtract)
        v.drain()
        v.tensor_scalar(out=R(3), in0=R(3), scalar1=1e-5, scalar2=None, op0=OP.add)
    vr = k.v_op(vrow)
    k.wait("scalar", vr)
    sr = k.s_op(lambda s: s.activation(out=R(4), in_=R(3), func=AF.Sqrt))
    k.wait("vector", sr)
    vr2 = k.v_op(lambda v: (
        v.reciprocal(out=R(5), in_=R(4)),
        v.drain(),
        v.tensor_tensor(out=R(6), in0=R(2), in1=R(5), op=OP.mult),
        v.drain(),
        v.tensor_scalar(out=R(6), in0=R(6), scalar1=-1.0, scalar2=None, op0=OP.mult)))
    out_ready = vr2
    prev_o = [(), ()]
    acc_d = None
    for th in range(2):
        def tbc(t, th=th, vr2=vr2, prev=out_ready):
            t.wait_ge(k.sem[vr2[0]], vr2[1])
            t.wait_ge(k.sem[prev[0]], prev[1])
            t.matmul(k.ps[2][:], k.ones_f[0:1, 0:128], R(5, th), start=True, stop=True,
                     tile_position=(0, 0))
            return t.matmul(k.ps[3][:], k.ones_f[0:1, 0:128], R(6, th), start=True, stop=True,
                            tile_position=(0, 0))
        tb_ = k.t_op(tbc)
        k.wait("vector", tb_)
        vb = k.v_op(lambda v: v.tensor_copy(out=bca[:], in_=k.ps[2][:]))
        last = vb
        for dc in range(NDC):
            xb = dc % 2
            ld = k.dma("sync", st_[xb][:], src_hbm[dc, :, th * 512:(th + 1) * 512],
                       waits=prev_use[xb])
            k.wait("vector", ld, *prev_o[xb])
            va = k.v_op(lambda v, xb=xb: (
                v.tensor_tensor(out=o32t[xb][:], in0=st_[xb][:], in1=bca[:], op=OP.mult),
                v.drain(),
                v.tensor_tensor(out=o32t[xb][:], in0=o32t[xb][:], in1=k.ps[3][:], op=OP.add)))
            prev_use[xb] = (va,)
            last = va
            sl = slice(dc * TOK + th * 512, dc * TOK + (th + 1) * 512)
            if out32 is not None:
                last = k.v_op(lambda v, sl=sl, xb=xb: v.tensor_copy(out=out32[:, sl], in_=o32t[xb][:]))
            if acc_hbm is not None:
                acc_d = k.dma("gpsimd", acc_hbm[dc, :, th * 512:(th + 1) * 512], o32t[xb][:],
                              waits=(last,))
            if out_hi is not None:
                k.wait("scalar", va)
                sh = k.s_op(lambda s, sl=sl, xb=xb: s.copy(out=out_hi[:, sl], in_=o32t[xb][:]))
                k.wait("vector", sh)
                last = k.v_op(lambda v, sl=sl, xb=xb: v.tensor_tensor(out=out_lo[:, sl], in0=o32t[xb][:],
                                                                     in1=out_hi[:, sl], op=OP.subtract))
            prev_o[xb] = (last,) if acc_hbm is None else (last, acc_d)
        out_ready = last
    return out_ready, acc_d


# version 05
# speedup vs baseline: 1.5190x; 1.1161x over previous
import numpy as np

import concourse.bass as bass
import concourse.mybir as mybir

F32 = mybir.dt.float32
BF16 = mybir.dt.bfloat16
I32 = mybir.dt.int32
U8 = mybir.dt.uint8
U16 = mybir.dt.uint16
OP = mybir.AluOpType
AF = mybir.ActivationFunctionType

D = 2048
LSEQ = 4096
TOK = 1024
NDC = 16
E, I_FF = 16, 1536
NEG = -1.0e30
CLAMP = 3.1415925
TP_HI = float(np.float32(2 * np.pi))
TP_LO = float(np.float64(2 * np.pi) - np.float64(np.float32(2 * np.pi)))
INV2PI = float(np.float32(1.0 / (2 * np.pi)))
PIH = float(np.float32(np.pi / 2))
ENGS = ("sync", "vector", "scalar", "tensor", "gpsimd")
DEBUG = False
PHASES = 99
B_MS = 8
B_NIL = 99
SKIP_ROPE = False


class Ctr:
    def __init__(self):
        self.v = 0
    def inc(self, n=1):
        self.v += n
        return self.v


class KB:

    def __init__(self, nc):
        self.nc = nc
        self.q = {e: [] for e in ENGS}
        self.sem = {}
        self.ctr = {}
        self.dma_idx = {}
        self.es = None
        self.dbg = {}

    def on(self, eng, fn):
        self.q[eng].append(fn)

    def emit_into(self, block):
        for e in ENGS:
            fns = self.q[e]
            def runner(h, fns=fns):
                for f in fns:
                    f(h)
            getattr(block, e)(runner)

    def mksem(self, name):
        self.sem[name] = self.es.enter_context(self.nc.semaphore(name))
        self.ctr[name] = Ctr()

    def barrier(self):
        sem, c = self.sem["sb"], self.ctr["sb"]
        tgt = c.v + 5
        fences = self.queue_fence("sd") + self.queue_fence("sg") + self.queue_fence("sa")
        for e in ENGS:
            def f(h, tgt=tgt, sem=sem, fences=fences):
                for wn, wv in fences:
                    h.wait_ge(self.sem[wn], wv)
                h.nop().then_inc(sem, 1)
                h.wait_ge(sem, tgt)
            self.on(e, f)
        c.v = tgt

    NSLOT = 12

    def dma(self, eng, out, in_, waits=(), accum=None):
        base = {"sync": "sd", "gpsimd": "sg", "scalar": "sa"}[eng]
        idx = self.dma_idx[base] = self.dma_idx.get(base, -1) + 1
        sname = f"{base}{idx % self.NSLOT}"
        sem, c = self.sem[sname], self.ctr[sname]
        prev = c.v
        tgt = c.inc(16)
        def f(h, out=out, in_=in_, waits=tuple(waits), accum=accum, prev=prev):
            if prev > 0:
                h.wait_ge(sem, prev)
            for wn, wv in waits:
                h.wait_ge(self.sem[wn], wv)
            if accum is None:
                h.dma_start(out=out, in_=in_).then_inc(sem, 16)
            else:
                h.dma_start(out=out, in_=in_, accum_op=accum).then_inc(sem, 16)
        self.on(eng, f)
        return (sname, tgt)

    def queue_fence(self, base):
        return tuple((f"{base}{i}", self.ctr[f"{base}{i}"].v) for i in range(self.NSLOT)
                     if self.ctr[f"{base}{i}"].v > 0)

    def wait(self, eng, *wts):
        def f(h, wts=wts):
            for wn, wv in wts:
                if wv > 0:
                    h.wait_ge(self.sem[wn], wv)
        self.on(eng, f)

    def v_op(self, fn, inc=True):
        sem, c = self.sem["sv"], self.ctr["sv"]
        def f(h, fn=fn):
            fn(h)
            h.drain()
            h.nop().then_inc(sem, 1)
        self.on("vector", f)
        return ("sv", c.inc())

    def s_op(self, fn):
        sem, c = self.sem["ss"], self.ctr["ss"]
        def f(h, fn=fn):
            fn(h)
            h.drain()
            h.nop().then_inc(sem, 1)
        self.on("scalar", f)
        return ("ss", c.inc())

    def g_op(self, fn):
        sem, c = self.sem["sgp"], self.ctr["sgp"]
        def f(h, fn=fn):
            last = fn(h)
            if last is not None:
                last.then_inc(sem, 1)
            else:
                h.nop().then_inc(sem, 1)
        self.on("gpsimd", f)
        return ("sgp", c.inc())

    def t_op(self, fn):
        sem, c = self.sem["st"], self.ctr["st"]
        def f(h, fn=fn):
            last = fn(h)
            if last is not None:
                last.then_inc(sem, 1)
            else:
                h.nop().then_inc(sem, 1)
        self.on("tensor", f)
        return ("st", c.inc())

    def sb(self, name, shape, dt):
        return self.es.enter_context(self.nc.sbuf_tensor(name, list(shape), dt))

    def sb_on(self, stack, name, shape, dt):
        return stack.enter_context(self.nc.sbuf_tensor(name, list(shape), dt))


def mm3(t, psum_ap, w_hi, w_lo, x_hi, x_lo, start, stop, tp=None):
    kw = {} if tp is None else {"tile_position": tp}
    t.matmul(psum_ap, w_hi, x_hi, start=start, stop=False, **kw)
    t.matmul(psum_ap, w_lo, x_hi, start=False, stop=False, **kw)
    return t.matmul(psum_ap, w_hi, x_lo, start=False, stop=stop, **kw)


def declare_io(nc):
    k = KB(nc)
    d = {}
    def di(name, shape, dt):
        d[name] = nc.dram_tensor(name, list(shape), dt, kind="ExternalInput").ap()
    di("x_fm", (NDC, 128, LSEQ), F32)
    di("x_own", (NDC, 128, TOK), F32)
    di("qpos", (128, TOK), F32)
    di("mask_fine", (16, 128, 512), BF16)
    di("ident_bf", (128, 128), BF16)
    di("ident_f32", (128, 128), F32)
    di("ones_bf", (128, 128), BF16)
    di("ones_f32", (128, 128), F32)
    di("oh_rows", (16, E * 128), F32)
    di("inv_hi", (128, 1), F32)
    di("inv_lo", (128, 1), F32)
    di("wqT", (NDC, 128, 2048), F32)
    di("wkT", (NDC, 128, 512), F32)
    di("wvT", (NDC, 128, 512), F32)
    di("woT", (NDC, 128, 2048), F32)
    di("gwT", (NDC, 128, 16), F32)
    di("gup_b", (E, 16, 128, 3072), U8)
    di("gus", (E, 16, 8, 3072), F32)
    di("dnp_b", (E, 12, 128, 2048), U8)
    di("dns", (E, 12, 8, 2048), F32)
    di("sgup_b", (16, 128, 3072), U8)
    di("sgus", (16, 8, 3072), F32)
    di("sdnp_b", (12, 128, 2048), U8)
    di("sdns", (12, 8, 2048), F32)
    k.inp = d
    k.out = nc.dram_tensor("out", [8, 128, 2048], F32, kind="ExternalOutput").ap()
    k.wq_s = nc.dram_tensor("wq_s", [2, NDC, 128, 2048], BF16, kind="Internal").ap()
    k.wk_s = nc.dram_tensor("wk_s", [2, NDC, 128, 512], BF16, kind="Internal").ap()
    k.wv_s = nc.dram_tensor("wv_s", [2, NDC, 128, 512], BF16, kind="Internal").ap()
    k.wo_s = nc.dram_tensor("wo_s", [2, NDC, 128, 2048], BF16, kind="Internal").ap()
    k.k_s = nc.dram_tensor("k_s", [4, 2, 128, LSEQ], BF16, kind="Internal").ap()
    k.v_s = nc.dram_tensor("v_s", [4, 2, 32, 128, 128], BF16, kind="Internal").ap()
    k.acc_s = nc.dram_tensor("acc_s", [NDC, 128, TOK], F32, kind="Internal").ap()
    k.q_s = nc.dram_tensor("q_s", [2, 128, NDC * TOK], BF16, kind="Internal").ap()
    k.xres_s = nc.dram_tensor("xres_s", [NDC, 128, TOK], F32, kind="Internal").ap()
    if DEBUG:
        k.dbg["xln"] = nc.dram_tensor("dbg_xln", [NDC, 128, TOK], F32, kind="ExternalOutput").ap()
        k.dbg["att"] = nc.dram_tensor("dbg_att", [NDC, 128, TOK], F32, kind="ExternalOutput").ap()
    return k


def emit_prologue(k, es_rope, es_rtmp, es_wsp):
    k.ident_bf = k.sb("identb", (128, 128), BF16)
    k.ident_f = k.sb("identf", (128, 128), F32)
    k.ones_bf = k.sb("onesb", (128, 128), BF16)
    k.ones_f = k.sb("onesf", (128, 128), F32)
    k.inv_hi = k.sb("invhi", (128, 1), F32)
    k.inv_lo = k.sb("invlo", (128, 1), F32)
    k.gw_hi = k.sb("gwhi", (128, NDC * 16), BF16)
    k.gw_lo = k.sb("gwlo", (128, NDC * 16), BF16)
    k.cos_k = k.sb_on(es_rope, "cosk", (128, LSEQ), F32)
    k.sin_k = k.sb_on(es_rope, "sink", (128, LSEQ), F32)
    k.cos_q = k.sb_on(es_rope, "cosq", (128, TOK), F32)
    k.sin_q = k.sb_on(es_rope, "sinq", (128, TOK), F32)
    posq = k.sb_on(es_rtmp, "rt_posq", (128, TOK), F32)
    posk = k.sb_on(es_rtmp, "rt_posk", (128, LSEQ), F32)
    f_t = k.sb_on(es_rtmp, "rt_f", (128, LSEQ), F32)
    t1 = k.sb_on(es_rtmp, "rt_t1", (128, LSEQ), F32)
    ni = k.sb_on(es_rtmp, "rt_ni", (128, LSEQ), I32)
    nf = k.sb_on(es_rtmp, "rt_nf", (128, LSEQ), F32)

    for nm_, t_ in (("ident_bf", k.ident_bf), ("ident_f32", k.ident_f), ("ones_bf", k.ones_bf),
                    ("ones_f32", k.ones_f),
                    ("inv_hi", k.inv_hi), ("inv_lo", k.inv_lo), ("qpos", posq)):
        cw = k.dma("sync", t_[:], k.inp[nm_][:, :])
    cw2 = cw
    g2 = k.g_op(lambda g: g.iota(posk[:], pattern=[[1, LSEQ]], base=0,
                                 channel_multiplier=0, allow_small_or_imprecise_dtypes=True))

    def build_table(src_pos, n, cos_dst, sin_dst):
        wv0 = k.v_op(lambda v: (
            v.tensor_scalar(out=f_t[:, 0:n], in0=src_pos[:, 0:n], scalar1=k.inv_hi[:, 0:1],
                            scalar2=None, op0=OP.mult),
            v.drain(),
            v.tensor_scalar(out=t1[:, 0:n], in0=src_pos[:, 0:n], scalar1=k.inv_lo[:, 0:1],
                            scalar2=None, op0=OP.mult),
            v.drain(),
            v.tensor_tensor(out=f_t[:, 0:n], in0=f_t[:, 0:n], in1=t1[:, 0:n], op=OP.add)))
        for dst, bias in ((sin_dst, 0.0), (cos_dst, PIH)):
            def redv(v, n=n, bias=bias):
                v.tensor_scalar(out=t1[:, 0:n], in0=f_t[:, 0:n], scalar1=float(bias), scalar2=None,
                                op0=OP.add)
                v.drain()
                v.tensor_scalar(out=nf[:, 0:n], in0=t1[:, 0:n], scalar1=INV2PI, scalar2=0.5,
                                op0=OP.mult, op1=OP.add)
                v.drain()
                v.tensor_copy(out=ni[:, 0:n], in_=nf[:, 0:n])
                v.drain()
                v.tensor_copy(out=nf[:, 0:n], in_=ni[:, 0:n])
                v.drain()
                v.scalar_tensor_tensor(out=t1[:, 0:n], in0=nf[:, 0:n], scalar=-TP_HI, in1=t1[:, 0:n],
                                       op0=OP.mult, op1=OP.add)
                v.drain()
                v.scalar_tensor_tensor(out=t1[:, 0:n], in0=nf[:, 0:n], scalar=-TP_LO, in1=t1[:, 0:n],
                                       op0=OP.mult, op1=OP.add)
                v.drain()
                v.tensor_scalar(out=nf[:, 0:n], in0=t1[:, 0:n], scalar1=float(np.float32(np.pi)),
                                scalar2=None, op0=OP.is_gt)
                v.drain()
                v.scalar_tensor_tensor(out=t1[:, 0:n], in0=nf[:, 0:n], scalar=-TP_HI, in1=t1[:, 0:n],
                                       op0=OP.mult, op1=OP.add)
                v.drain()
                v.tensor_scalar(out=nf[:, 0:n], in0=t1[:, 0:n], scalar1=float(np.float32(-np.pi)),
                                scalar2=None, op0=OP.is_lt)
                v.drain()
                v.scalar_tensor_tensor(out=t1[:, 0:n], in0=nf[:, 0:n], scalar=TP_HI, in1=t1[:, 0:n],
                                       op0=OP.mult, op1=OP.add)
                v.drain()
                v.tensor_scalar(out=t1[:, 0:n], in0=t1[:, 0:n], scalar1=CLAMP, scalar2=-CLAMP,
                                op0=OP.min, op1=OP.max)
            wv2 = k.v_op(redv)
            k.wait("scalar", wv2)
            ws = k.s_op(lambda s, dst=dst, n=n: s.activation(out=dst[:, 0:n], in_=t1[:, 0:n], func=AF.Sin))
            k.wait("vector", ws)

    k.wait("vector", *(k.queue_fence("sd") + (g2,)))
    build_table(posq, TOK, k.cos_q, k.sin_q)
    build_table(posk, LSEQ, k.cos_k, k.sin_k)

    gwf = k.sb_on(es_wsp, "gwf", (128, NDC * 16), F32)
    wg = k.dma("sync", gwf[:], k.inp["gwT"].transpose([1, 0, 2]))
    k.wait("scalar", wg)
    s1 = k.s_op(lambda s: s.copy(out=k.gw_hi[:], in_=gwf[:]))
    k.wait("vector", s1)
    k.v_op(lambda v: v.tensor_tensor(out=k.gw_lo[:], in0=gwf[:], in1=k.gw_hi[:], op=OP.subtract))

    bufs = [k.sb_on(es_wsp, f"wspl{i}", (128, 2048), F32) for i in range(2)]
    hbufs = [k.sb_on(es_wsp, f"wsph{i}", (128, 2048), BF16) for i in range(2)]
    lbufs = [k.sb_on(es_wsp, f"wspl2{i}", (128, 2048), BF16) for i in range(2)]
    jobs = []
    for name, wd in (("wkT", 512), ("wvT", 512), ("wqT", 2048), ("woT", 2048)):
        for dc in range(NDC):
            jobs.append((name, wd, dc))
    scr = {"wqT": k.wq_s, "wkT": k.wk_s, "wvT": k.wv_s, "woT": k.wo_s}
    prev_store = [(), ()]
    for i, (name, wd, dc) in enumerate(jobs):
        b = i % 2
        wl = k.dma("sync", bufs[b][:, 0:wd], k.inp[name][dc, :, :], waits=prev_store[b])
        k.wait("scalar", wl, *prev_store[b])
        sh = k.s_op(lambda s, b=b, wd=wd: s.copy(out=hbufs[b][:, 0:wd], in_=bufs[b][:, 0:wd]))
        k.wait("vector", sh, *prev_store[b])
        lv = k.v_op(lambda v, b=b, wd=wd: v.tensor_tensor(out=lbufs[b][:, 0:wd], in0=bufs[b][:, 0:wd],
                                                          in1=hbufs[b][:, 0:wd], op=OP.subtract))
        s_hi = k.dma("gpsimd", scr[name][0, dc, :, :], hbufs[b][:, 0:wd], waits=(sh,))
        s_lo = k.dma("gpsimd", scr[name][1, dc, :, :], lbufs[b][:, 0:wd], waits=(lv,))
        prev_store[b] = (s_hi, s_lo)


def emit_a2(k):
    gdone = k.queue_fence("sg")
    xs = [k.sb(f"a2xs{i}", (128, 512), F32) for i in range(2)]
    xsh = [k.sb(f"a2xh{i}", (128, NDC * 512), BF16) for i in range(2)]
    xsl = [k.sb(f"a2xl{i}", (128, NDC * 512), BF16) for i in range(2)]
    wkh = k.sb("a2wkh", (128, NDC * 512), BF16)
    wkl = k.sb("a2wkl", (128, NDC * 512), BF16)
    wvh = k.sb("a2wvh", (128, NDC * 512), BF16)
    wvl = k.sb("a2wvl", (128, NDC * 512), BF16)
    kf = k.sb("a2kf", (128, 512), F32)
    ksw = k.sb("a2ksw", (128, 512), F32)
    kro = k.sb("a2kro", (128, 512), F32)
    ta = k.sb("a2ta", (128, 512), F32)
    tb = k.sb("a2tb", (128, 512), F32)
    khi = k.sb("a2khi", (128, 512), BF16)
    klo = k.sb("a2klo", (128, 512), BF16)
    vhi = k.sb("a2vhi", (128, 512), BF16)
    vlo = k.sb("a2vlo", (128, 512), BF16)
    vT = k.sb("a2vT", (128, 1024), BF16)
    lw = [k.dma("sync", wkh[:], k.wk_s[0].transpose([1, 0, 2]), waits=gdone),
          k.dma("sync", wkl[:], k.wk_s[1].transpose([1, 0, 2])),
          k.dma("sync", wvh[:], k.wv_s[0].transpose([1, 0, 2])),
          k.dma("sync", wvl[:], k.wv_s[1].transpose([1, 0, 2]))]
    bank_w = {i: () for i in range(4)}
    psb_w = ()
    tensor_last = [(), ()]
    pd1 = pd2 = pdv1 = pdv2 = None
    pttw = None
    prev_cs = [(), ()]
    prev_swp = ()
    for lt in range(8):
        b = lt % 2
        v1 = None
        for ch in range(NDC):
            xb = ch % 2
            sl = slice(ch * 512, (ch + 1) * 512)
            ldw = k.dma("sync", xs[xb][:], k.inp["x_fm"][ch, :, lt * 512:(lt + 1) * 512],
                        waits=prev_cs[xb] + (tensor_last[b] if ch == 0 else ()))
            k.wait("scalar", ldw, *(tensor_last[b] if ch == 0 else ()))
            s1 = k.s_op(lambda s, b=b, sl=sl, xb=xb: s.copy(out=xsh[b][:, sl], in_=xs[xb][:]))
            k.wait("vector", s1)
            v1 = k.v_op(lambda v, b=b, sl=sl, xb=xb: v.tensor_tensor(
                out=xsl[b][:, sl], in0=xs[xb][:], in1=xsh[b][:, sl], op=OP.subtract))
            prev_cs[xb] = (v1,)
        split_all = v1
        last_t = None
        for hp in range(4):
            PK = k.ps[2 * (hp % 2)]
            PV = k.ps[2 * (hp % 2) + 1]
            pk = bank_w[2 * (hp % 2)]
            pv_ = bank_w[2 * (hp % 2) + 1]
            def tk(t, b=b, pk=pk, PK=PK, hp=hp, split_all=split_all, lww=tuple(lw)):
                t.wait_ge(k.sem[split_all[0]], split_all[1])
                for wn, wv in lww:
                    t.wait_ge(k.sem[wn], wv)
                for wn, wv in pk:
                    t.wait_ge(k.sem[wn], wv)
                last = None
                for dc in range(NDC):
                    wsl = slice(dc * 512 + hp * 128, dc * 512 + (hp + 1) * 128)
                    xsl_ = slice(dc * 512, (dc + 1) * 512)
                    last = mm3(t, PK[:], wkh[:, wsl], wkl[:, wsl], xsh[b][:, xsl_], xsl[b][:, xsl_],
                               start=(dc == 0), stop=(dc == NDC - 1))
                return last
            tkw = k.t_op(tk)
            k.wait("vector", tkw, *prev_swp)
            ev = k.v_op(lambda v, PK=PK: v.tensor_copy(out=kf[:], in_=PK[:]))
            bank_w[2 * (hp % 2)] = (ev,)
            sw = []
            for bb in (0, 64):
                sw.append(k.dma("sync", ksw[bb:bb + 32, :], kf[bb + 32:bb + 64, :],
                                waits=(ev,) if bb == 0 else ()))
                sw.append(k.dma("sync", ksw[bb + 32:bb + 64, :], kf[bb:bb + 32, :]))
            prev_swp = tuple(sw)
            csl = slice(lt * 512, (lt + 1) * 512)
            k.wait("vector", *sw)
            if SKIP_ROPE:
                rw = k.v_op(lambda v: v.tensor_copy(out=kro[:], in_=kf[:]))
            else:
                rw = None
            def rope(v, csl=csl):
                for bb in (0, 64):
                    x1 = slice(bb, bb + 32)
                    x2 = slice(bb + 32, bb + 64)
                    v.tensor_tensor(out=ta[x1, :], in0=kf[x1, :], in1=k.cos_k[x1, csl], op=OP.mult)
                    v.tensor_tensor(out=tb[x1, :], in0=ksw[x1, :], in1=k.sin_k[x1, csl], op=OP.mult)
                    v.tensor_tensor(out=ta[x2, :], in0=ksw[x2, :], in1=k.sin_k[x2, csl], op=OP.mult)
                    v.tensor_tensor(out=tb[x2, :], in0=kf[x2, :], in1=k.cos_k[x2, csl], op=OP.mult)
                v.drain()
                v.tensor_tensor(out=kro[0:32, :], in0=ta[0:32, :], in1=tb[0:32, :], op=OP.subtract)
                v.tensor_tensor(out=kro[64:96, :], in0=ta[64:96, :], in1=tb[64:96, :], op=OP.subtract)
                v.drain()
                v.tensor_tensor(out=kro[32:64, :], in0=ta[32:64, :], in1=tb[32:64, :], op=OP.add)
                v.tensor_tensor(out=kro[96:128, :], in0=ta[96:128, :], in1=tb[96:128, :], op=OP.add)
            if rw is None:
                rw = k.v_op(rope)
            k.wait("scalar", rw, *( (pd1,) if pd1 else () ))
            s2 = k.s_op(lambda s: s.copy(out=khi[:], in_=kro[:]))
            k.wait("vector", s2, *( (pd2,) if pd2 else () ))
            v2 = k.v_op(lambda v: v.tensor_tensor(out=klo[:], in0=kro[:], in1=khi[:], op=OP.subtract))
            pd1 = k.dma("gpsimd", k.k_s[hp, 0, :, lt * 512:(lt + 1) * 512], khi[:], waits=(s2,))
            pd2 = k.dma("gpsimd", k.k_s[hp, 1, :, lt * 512:(lt + 1) * 512], klo[:], waits=(v2,))
            def tv(t, b=b, pv_=pv_, PV=PV, hp=hp):
                for wn, wv in pv_:
                    t.wait_ge(k.sem[wn], wv)
                last = None
                for dc in range(NDC):
                    wsl = slice(dc * 512 + hp * 128, dc * 512 + (hp + 1) * 128)
                    xsl_ = slice(dc * 512, (dc + 1) * 512)
                    last = mm3(t, PV[:], wvh[:, wsl], wvl[:, wsl], xsh[b][:, xsl_], xsl[b][:, xsl_],
                               start=(dc == 0), stop=(dc == NDC - 1))
                return last
            tvw = k.t_op(tv)
            k.wait("scalar", tvw, *( (pttw,) if pttw else () ))
            s3 = k.s_op(lambda s, PV=PV: s.copy(out=vhi[:], in_=PV[:]))
            k.wait("vector", s3, *( (pttw,) if pttw else () ))
            v3 = k.v_op(lambda v, PV=PV: v.tensor_tensor(out=vlo[:], in0=PV[:], in1=vhi[:], op=OP.subtract))
            bank_w[2 * (hp % 2) + 1] = (v3,)
            def tt(t, psb_w=psb_w, v3=v3):
                t.wait_ge(k.sem[v3[0]], v3[1])
                for wn, wv in psb_w:
                    t.wait_ge(k.sem[wn], wv)
                last = None
                for j in range(4):
                    t.transpose(k.psb[:, j * 128:(j + 1) * 128], vhi[:, j * 128:(j + 1) * 128], k.ident_bf[:])
                for j in range(4):
                    last = t.transpose(k.psb[:, 512 + j * 128: 512 + (j + 1) * 128],
                                       vlo[:, j * 128:(j + 1) * 128], k.ident_bf[:])
                return last
            pttw = ttw = k.t_op(tt)
            k.wait("vector", ttw, *( (pdv1, pdv2) if pdv2 else () ))
            v4 = k.v_op(lambda v: v.tensor_copy(out=vT[:], in_=k.psb[:]))
            psb_w = (v4,)
            pdv1 = k.dma("gpsimd", k.v_s[hp, 0, lt * 4:(lt + 1) * 4, :, :].transpose([1, 0, 2]),
                         vT[:, 0:512], waits=(v4,))
            pdv2 = k.dma("gpsimd", k.v_s[hp, 1, lt * 4:(lt + 1) * 4, :, :].transpose([1, 0, 2]),
                         vT[:, 512:1024], waits=(v4,))
            last_t = tvw
        tensor_last[b] = (last_t,)
    k.barrier()



def emit_a1(k):
    gdone = k.queue_fence("sg")
    qhi_t = k.sb("a1qhi", (128, TOK), BF16)
    qlo_t = k.sb("a1qlo", (128, TOK), BF16)
    xo = [k.sb(f"a1xo{i}", (128, TOK), F32) for i in range(2)]
    oh = k.sb("a1oh", (128, NDC * TOK), BF16)
    ol = k.sb("a1ol", (128, NDC * TOK), BF16)
    wqh = k.sb("a1wqh", (128, NDC * 128), BF16)
    wql = k.sb("a1wql", (128, NDC * 128), BF16)
    qf = k.sb("a1qf", (128, TOK), F32)
    qsw = k.sb("a1qsw", (128, TOK), F32)
    ta = k.sb("a1ta", (128, TOK), F32)
    tb = k.sb("a1tb", (128, TOK), F32)
    v1 = None
    prev_v = [(), ()]
    for ch in range(NDC):
        xb = ch % 2
        ld = k.dma("sync", xo[xb][:], k.inp["x_own"][ch, :, :], waits=prev_v[xb])
        sl = slice(ch * TOK, (ch + 1) * TOK)
        k.wait("scalar", ld)
        s1 = k.s_op(lambda s, sl=sl, xb=xb: s.copy(out=oh[:, sl], in_=xo[xb][:]))
        k.wait("vector", s1)
        v1 = k.v_op(lambda v, sl=sl, xb=xb: v.tensor_tensor(out=ol[:, sl], in0=xo[xb][:], in1=oh[:, sl],
                                                            op=OP.subtract))
        prev_v[xb] = (v1,)
    split_all = v1
    bank_w = {}
    wq_ld = ()
    ps2 = None
    pqd1 = pqd2 = None
    prev_swp = ()
    for oc in range(NDC):
        b = oc % 2
        lh = k.dma("sync", wqh[:, 0:NDC * 128],
                   k.wq_s[0, :, :, oc * 128:(oc + 1) * 128].transpose([1, 0, 2]),
                   waits=wq_ld + gdone)
        ll = k.dma("sync", wql[:, 0:NDC * 128],
                   k.wq_s[1, :, :, oc * 128:(oc + 1) * 128].transpose([1, 0, 2]),
                   waits=())
        evs = []
        for th in range(2):
            bk = 2 * b + th
            PQ = k.ps[bk]
            pk = bank_w.get(bk, ())
            def tq(t, th=th, pk=pk, lh=lh, ll=ll, split_all=split_all, PQ=PQ):
                t.wait_ge(k.sem[split_all[0]], split_all[1])
                t.wait_ge(k.sem[lh[0]], lh[1])
                t.wait_ge(k.sem[ll[0]], ll[1])
                for wn, wv in pk:
                    t.wait_ge(k.sem[wn], wv)
                last = None
                for dc in range(NDC):
                    wsl = slice(dc * 128, (dc + 1) * 128)
                    xsl_ = slice(dc * TOK + th * 512, dc * TOK + (th + 1) * 512)
                    last = mm3(t, PQ[:], wqh[:, wsl], wql[:, wsl], oh[:, xsl_], ol[:, xsl_],
                               start=(dc == 0), stop=(dc == NDC - 1))
                return last
            tw = k.t_op(tq)
            k.wait("vector", tw, *( (ps2,) if ps2 and th == 0 else () ))
            ev = k.v_op(lambda v, PQ=PQ, th=th: v.tensor_copy(out=qf[:, th * 512:(th + 1) * 512], in_=PQ[:]))
            evs.append(ev)
            bank_w[bk] = (ev,)
        wq_ld = (evs[1],)
        sw = []
        for bb in (0, 64):
            sw.append(k.dma("sync", qsw[bb:bb + 32, :], qf[bb + 32:bb + 64, :],
                            waits=((evs[1],) + prev_swp) if bb == 0 else ()))
            sw.append(k.dma("sync", qsw[bb + 32:bb + 64, :], qf[bb:bb + 32, :]))
        prev_swp = tuple(sw)
        k.wait("vector", *sw)
        def rope(v):
            for bb in (0, 64):
                x1 = slice(bb, bb + 32)
                x2 = slice(bb + 32, bb + 64)
                v.tensor_tensor(out=ta[x1, :], in0=qf[x1, :], in1=k.cos_q[x1, :], op=OP.mult)
                v.tensor_tensor(out=tb[x1, :], in0=qsw[x1, :], in1=k.sin_q[x1, :], op=OP.mult)
                v.tensor_tensor(out=ta[x2, :], in0=qsw[x2, :], in1=k.sin_q[x2, :], op=OP.mult)
                v.tensor_tensor(out=tb[x2, :], in0=qf[x2, :], in1=k.cos_q[x2, :], op=OP.mult)
            v.drain()
            v.tensor_tensor(out=qf[0:32, :], in0=ta[0:32, :], in1=tb[0:32, :], op=OP.subtract)
            v.tensor_tensor(out=qf[64:96, :], in0=ta[64:96, :], in1=tb[64:96, :], op=OP.subtract)
            v.drain()
            v.tensor_tensor(out=qf[32:64, :], in0=ta[32:64, :], in1=tb[32:64, :], op=OP.add)
            v.tensor_tensor(out=qf[96:128, :], in0=ta[96:128, :], in1=tb[96:128, :], op=OP.add)
        rw = k.v_op(rope)
        qsl = slice(oc * TOK, (oc + 1) * TOK)
        k.wait("scalar", rw, *( (pqd1,) if pqd1 else () ))
        ps2 = s2 = k.s_op(lambda s: s.copy(out=qhi_t[:], in_=qf[:]))
        k.wait("vector", s2, *( (pqd2,) if pqd2 else () ))
        v2 = k.v_op(lambda v: v.tensor_tensor(out=qlo_t[:], in0=qf[:], in1=qhi_t[:], op=OP.subtract))
        pqd1 = k.dma("gpsimd", k.q_s[0, :, qsl], qhi_t[:], waits=(s2,))
        pqd2 = k.dma("gpsimd", k.q_s[1, :, qsl], qlo_t[:], waits=(v2,))
    k.barrier()


def emit_b(k, es_att):
    gdone = k.queue_fence("sg")
    qh_ = [[[k.sb(f"bq{i}{h}{l}", (64, TOK), BF16) for l in range(2)] for h in range(2)]
           for i in range(2)]
    msk = k.sb("bmask", (128, 16 * 512), BF16)
    km = [[k.sb(f"bkm{i}{h}", (64, LSEQ), BF16) for h in range(2)] for i in range(2)]
    vm = [[k.sb(f"bvm{i}{h}", (128, 2048), BF16) for h in range(2)] for i in range(2)]
    p32 = k.sb("bp32", (128, 1024), F32)
    p_hi = k.sb("bphi", (128, 1024), BF16)
    p_lo = k.sb("bplo", (128, 1024), BF16)
    dsb = k.sb("bdsb", (33, 512), F32)
    rcp = k.sb("brcp", (33, 512), F32)
    rbc = k.sb("brbc", (128, 512), F32)
    atmp = k.sb("batmp", (128, 512), F32)
    mw = k.dma("sync", msk[:], k.inp["mask_fine"].transpose([1, 0, 2]))
    kload_gate = [(), ()]
    qgate = [(), ()]
    exp_done = {0: (), 1: ()}
    pv_gate = ()
    prev_vsplit = ()
    last_score = {0: None, 1: None}
    for m in range(B_MS):
        mb = m % 2
        pair, a = m // 2, m % 2
        lds = []
        for hl in range(2):
            lds.append(k.dma("sync", km[mb][hl][:], k.k_s[pair, hl, a * 64:(a + 1) * 64, :],
                             waits=((kload_gate[mb] + gdone) if hl == 0 else ())))
            lds.append(k.dma("gpsimd", vm[mb][hl][:],
                             k.v_s[pair, hl, :, :, a * 64:(a + 1) * 64].transpose([1, 0, 2]),
                             waits=((kload_gate[mb] + gdone) if (hl == 0) else ())))
        kld = tuple(lds)
        for P in (2 * m, 2 * m + 1):
            pb = P % 2
            qlds = []
            for head in range(2):
                for hl in range(2):
                    qlds.append(k.dma("sync", qh_[pb][head][hl][:],
                                      k.q_s[hl, head * 64:(head + 1) * 64, P * TOK:(P + 1) * TOK],
                                      waits=(qgate[pb] + (gdone if P < 2 else ()))
                                      if (head == 0 and hl == 0) else ()))
            qld = tuple(qlds)
            for T in range(2):
                nil = min(16 * T + 16, B_NIL)
                qcsl = slice(T * 512, (T + 1) * 512)
                pv_started = False
                for il in range(nil):
                    par = il % 2
                    SA = k.ps[2 * par]
                    SB = k.ps[2 * par + 1]
                    def tsc(t, par=par, il=il, SA=SA, SB=SB, mb=mb, pb=pb, qcsl=qcsl,
                            w_exp=exp_done[par], kld=kld, qld=qld):
                        for wn, wv in w_exp + kld + qld:
                            t.wait_ge(k.sem[wn], wv)
                        ksl = slice(il * 128, (il + 1) * 128)
                        kh, kl2 = km[mb][0], km[mb][1]
                        last = None
                        for head, ps_ in ((0, SA), (1, SB)):
                            qa_hi = qh_[pb][head][0]
                            qa_lo = qh_[pb][head][1]
                            t.matmul(ps_[:], kh[:, ksl], qa_hi[:, qcsl], start=True, stop=False)
                            t.matmul(ps_[:], kl2[:, ksl], qa_hi[:, qcsl], start=False, stop=False)
                            last = t.matmul(ps_[:], kh[:, ksl], qa_lo[:, qcsl], start=False, stop=True)
                        return last
                    tscw = k.t_op(tsc)
                    last_score[mb] = tscw
                    wv_pre = tscw
                    if il >= 16 * T:
                        sub = il - 16 * T
                        k.wait("vector", tscw, mw)
                        wv_pre = k.v_op(lambda v, SA=SA, SB=SB, sub=sub: (
                            v.tensor_tensor(out=SA[:], in0=SA[:], in1=msk[:, sub * 512:(sub + 1) * 512],
                                            op=OP.add),
                            v.drain(),
                            v.tensor_tensor(out=SB[:], in0=SB[:], in1=msk[:, sub * 512:(sub + 1) * 512],
                                            op=OP.add)))
                    k.wait("scalar", wv_pre, mw, *prev_vsplit)
                    se = k.s_op(lambda s, SA=SA, SB=SB: (
                        s.activation(out=p32[:, 0:512], in_=SA[:], func=AF.Exp, scale=0.125),
                        s.drain(),
                        s.activation(out=p32[:, 512:1024], in_=SB[:], func=AF.Exp, scale=0.125)))
                    exp_done[par] = (se,)
                    k.wait("scalar", se, *pv_gate)
                    sh = k.s_op(lambda s: s.copy(out=p_hi[:], in_=p32[:]))
                    k.wait("vector", sh, *pv_gate)
                    vsplit = k.v_op(lambda v: v.tensor_tensor(out=p_lo[:], in0=p32[:], in1=p_hi[:],
                                                              op=OP.subtract))
                    prev_vsplit = (vsplit,)
                    def tpv(t, il=il, mb=mb, vsplit=vsplit, pv_started=pv_started,
                            stop=(il == nil - 1)):
                        t.wait_ge(k.sem[vsplit[0]], vsplit[1])
                        vsl = slice(il * 64, (il + 1) * 64)
                        PV, DEN = k.ps[4], k.ps[5]
                        st = not pv_started
                        vh, vl = vm[mb][0], vm[mb][1]
                        for (vv, pp, fst, lst) in ((vh, p_hi, st, False), (vh, p_lo, False, False),
                                                   (vl, p_hi, False, stop)):
                            t.matmul(PV[0:64, :], vv[:, vsl], pp[:, 0:512], start=fst, stop=lst,
                                     tile_position=(0, 0))
                            t.matmul(PV[64:128, :], vv[:, vsl], pp[:, 512:1024], start=fst, stop=lst,
                                     tile_position=(0, 64), skip_group_check=True)
                        t.matmul(DEN[0:1, :], k.ones_bf[:, 0:1], p_hi[:, 0:512], start=st, stop=False,
                                 tile_position=(0, 0))
                        t.matmul(DEN[0:1, :], k.ones_bf[:, 0:1], p_lo[:, 0:512], start=False, stop=stop,
                                 tile_position=(0, 0))
                        t.matmul(DEN[32:33, :], k.ones_bf[:, 0:1], p_hi[:, 512:1024], start=st,
                                 stop=False, tile_position=(0, 32), skip_group_check=True)
                        last = t.matmul(DEN[32:33, :], k.ones_bf[:, 0:1], p_lo[:, 512:1024], start=False,
                                        stop=stop, tile_position=(0, 32), skip_group_check=True)
                        return last
                    tpvw = k.t_op(tpv)
                    pv_gate = (tpvw,)
                    pv_started = True
                k.wait("scalar", pv_gate[0])
                sden = k.s_op(lambda s: (s.copy(out=dsb[0:1, :], in_=k.ps[5][0:1, :]), s.drain(),
                                         s.copy(out=dsb[32:33, :], in_=k.ps[5][32:33, :])))
                k.wait("vector", sden)
                vr = k.v_op(lambda v: (v.reciprocal(out=rcp[0:1, :], in_=dsb[0:1, :]),
                                       v.drain(),
                                       v.reciprocal(out=rcp[32:33, :], in_=dsb[32:33, :])))
                def tbc(t, vr=vr):
                    t.wait_ge(k.sem[vr[0]], vr[1])
                    t.matmul(k.ps7[0:64, :], k.ones_f[0:1, 0:64], rcp[0:1, :], start=True, stop=True,
                             tile_position=(0, 0))
                    return t.matmul(k.ps7[64:128, :], k.ones_f[32:33, 0:64], rcp[32:33, :], start=True,
                                    stop=True, tile_position=(32, 64), skip_group_check=True)
                tb_ = k.t_op(tbc)
                k.wait("vector", tb_)
                vb = k.v_op(lambda v: v.tensor_copy(out=rbc[:], in_=k.ps7[:]))
                va = k.v_op(lambda v: v.tensor_tensor(out=atmp[:], in0=k.ps[4][:], in1=rbc[:], op=OP.mult))
                acsl = slice(P * TOK + T * 512, P * TOK + (T + 1) * 512)
                k.wait("scalar", va)
                sa_ = k.s_op(lambda s, acsl=acsl: s.copy(out=k.att_hi[:, acsl], in_=atmp[:]))
                k.wait("vector", sa_)
                vl_ = k.v_op(lambda v, acsl=acsl: v.tensor_tensor(out=k.att_lo[:, acsl], in0=atmp[:],
                                                                  in1=k.att_hi[:, acsl], op=OP.subtract))
                pv_gate = (va,)
            qgate[pb] = (last_score[mb],)
        kload_gate[mb] = (va,)
    k.barrier()


def emit_router(k, xln_hi, xln_lo, combT):
    lgf = k.sb("r_lgf", (16, 1024), F32)
    lgt = k.sb("r_lgt", (128, 128), F32)
    nm = k.sb("r_nm", (128, 8), F32)
    ex = k.sb("r_ex", (128, 16), F32)
    pp = k.sb("r_pp", (128, 16), F32)
    wk = k.sb("r_wk", (128, 16), F32)
    msk_ = k.sb("r_msk", (128, 16), F32)
    cmb = k.sb("r_cmb", (128, 128), F32)
    ve = None
    for th in range(2):
        def tl(t, th=th, ve=ve):
            if ve is not None:
                t.wait_ge(k.sem[ve[0]], ve[1])
            last = None
            for dc in range(NDC):
                gsl = slice(dc * 16, (dc + 1) * 16)
                xsl = slice(dc * TOK + th * 512, dc * TOK + (th + 1) * 512)
                last = mm3(t, k.ps[0][0:16, :], k.gw_hi[:, gsl], k.gw_lo[:, gsl],
                           xln_hi[:, xsl], xln_lo[:, xsl], start=(dc == 0), stop=(dc == NDC - 1))
            return last
        tw = k.t_op(tl)
        k.wait("vector", tw)
        ve = k.v_op(lambda v, th=th: v.tensor_copy(out=lgf[:, th * 512:(th + 1) * 512], in_=k.ps[0][0:16, :]))
    vt = None
    for tb in range(8):
        def tt(t, tb=tb, ve=ve, vt=vt):
            t.wait_ge(k.sem[ve[0]], ve[1])
            if vt is not None:
                t.wait_ge(k.sem[vt[0]], vt[1])
            return t.transpose(k.ps[1][:, 0:16], lgf[:, tb * 128:(tb + 1) * 128], k.ident_f[0:16, 0:16])
        tw = k.t_op(tt)
        k.wait("vector", tw)
        vt = k.v_op(lambda v, tb=tb: v.tensor_copy(out=lgt[:, tb * 16:(tb + 1) * 16], in_=k.ps[1][:, 0:16]))
    vcm = None
    for tb in range(8):
        tsl = slice(tb * 16, (tb + 1) * 16)
        v1 = k.v_op(lambda v, tsl=tsl, tb=tb: v.reduce_max(out=nm[:, tb:tb + 1], in_=lgt[:, tsl],
                                                           axis=mybir.AxisListType.X, negate=True))
        k.wait("scalar", v1)
        s1 = k.s_op(lambda s, tsl=tsl, tb=tb: s.activation(out=ex[:], in_=lgt[:, tsl], func=AF.Exp,
                                                           bias=nm[:, tb:tb + 1], scale=1.0))
        k.wait("vector", s1)
        def vtop(v):
            v.reduce_sum(out=wk[:, 0:1], in_=ex[:], axis=mybir.AxisListType.X)
            v.drain()
            v.reciprocal(out=wk[:, 1:2], in_=wk[:, 0:1])
            v.drain()
            v.tensor_scalar(out=pp[:], in0=ex[:], scalar1=wk[:, 1:2], scalar2=None, op0=OP.mult)
            v.drain()
            v.tensor_copy(out=wk[:], in_=pp[:])
            v.drain()
            for r in range(4):
                v.reduce_max(out=msk_[:, 0:1], in_=wk[:], axis=mybir.AxisListType.X)
                v.drain()
                if r < 3:
                    v.tensor_scalar(out=msk_[:], in0=wk[:], scalar1=msk_[:, 0:1], scalar2=None,
                                    op0=OP.is_ge)
                    v.drain()
                    v.scalar_tensor_tensor(out=wk[:], in0=msk_[:], scalar=NEG, in1=wk[:],
                                           op0=OP.mult, op1=OP.add)
                    v.drain()
            v.tensor_scalar(out=msk_[:], in0=pp[:], scalar1=msk_[:, 0:1], scalar2=None, op0=OP.is_ge)
            v.drain()
            v.tensor_tensor(out=pp[:], in0=pp[:], in1=msk_[:], op=OP.mult)
            v.drain()
            v.reduce_sum(out=wk[:, 2:3], in_=pp[:], axis=mybir.AxisListType.X)
            v.drain()
            v.reciprocal(out=wk[:, 3:4], in_=wk[:, 2:3])
            v.drain()
            v.tensor_scalar(out=pp[:], in0=pp[:], scalar1=wk[:, 3:4], scalar2=None, op0=OP.mult)
        v2 = k.v_op(vtop)
        vcm = k.v_op(lambda v, tb=tb: v.tensor_copy(out=cmb[:, tb * 16:(tb + 1) * 16], in_=pp[:]))
    vt2 = None
    for tb in range(8):
        def tt2(t, tb=tb, vcm=vcm, vt2=vt2):
            t.wait_ge(k.sem[vcm[0]], vcm[1])
            if vt2 is not None:
                t.wait_ge(k.sem[vt2[0]], vt2[1])
            return t.transpose(k.ps[1][0:16, 0:128], cmb[:, tb * 16:(tb + 1) * 16], k.ident_f[:])
        tw = k.t_op(tt2)
        k.wait("vector", tw)
        vt2 = k.v_op(lambda v, tb=tb: v.tensor_copy(out=combT[:, tb * 128:(tb + 1) * 128],
                                                    in_=k.ps[1][0:16, 0:128]))
    return vt2


def emit_moe(k, xln_hi, xln_lo, combT):
    bt = [k.sb(f"d_bt{i}", (128, 1024), U16) for i in range(2)]
    srep = [k.sb(f"d_sr{i}", (128, 1024), F32) for i in range(2)]
    b0 = k.sb("d_b0", (128, 1024), U16)
    b1 = k.sb("d_b1", (128, 1024), U16)
    b2 = k.sb("d_b2", (128, 1024), U16)
    wf0 = k.sb("d_wf", (128, 1024), F32)
    wf = [wf0, wf0]
    wwh = [k.sb(f"d_wwh{i}", (128, NDC * 256), BF16) for i in range(2)]
    wwl = [k.sb(f"d_wwl{i}", (128, NDC * 256), BF16) for i in range(2)]
    wdh = wwh
    wdl = wwl
    gbc = k.sb("d_gbc", (128, 1024), F32)
    sgb = k.sb("d_sgb", (128, 512), F32)
    act_hi = k.sb("d_ahi", (128, 12 * TOK), BF16)
    act_lo = k.sb("d_alo", (128, 12 * TOK), BF16)
    af = k.sb("d_af", (128, 512), F32)
    ob = k.sb("d_ob", (128, 512), F32)
    ohb = k.sb("d_ohb", (16, 128), F32)

    state = {"last_acc": k.acc_init, "prev_gbc": None, "act_ready": None}
    prev_ldb = [(), ()]
    wbuf_gate = [(), ()]

    def _chain(v, bts, sreps, n, wfp):
        v.tensor_scalar(out=b0[0:64, 0:n], in0=bts[0:64, 0:n], scalar1=15, scalar2=None,
                        op0=OP.bitwise_and)
        v.drain()
        v.tensor_scalar(out=b0[64:128, 0:n], in0=bts[64:128, 0:n], scalar1=4, scalar2=None,
                        op0=OP.logical_shift_right)
        v.drain()
        v.tensor_scalar(out=b1[:, 0:n], in0=b0[:, 0:n], scalar1=7, scalar2=None, op0=OP.bitwise_and)
        v.drain()
        v.tensor_scalar(out=b2[:, 0:n], in0=b0[:, 0:n], scalar1=8, scalar2=12, op0=OP.bitwise_and,
                        op1=OP.logical_shift_left)
        v.drain()
        v.tensor_scalar(out=b0[:, 0:n], in0=b1[:, 0:n], scalar1=1.0, scalar2=None, op0=OP.min)
        v.drain()
        v.scalar_tensor_tensor(out=b0[:, 0:n], in0=b0[:, 0:n], scalar=251, in1=b1[:, 0:n],
                               op0=OP.mult, op1=OP.add)
        v.drain()
        v.tensor_scalar(out=b1[:, 0:n], in0=b1[:, 0:n], scalar1=1, scalar2=None,
                        op0=OP.logical_shift_right)
        v.drain()
        v.tensor_scalar(out=b1[:, 0:n], in0=b1[:, 0:n], scalar1=1.0, scalar2=None, op0=OP.min)
        v.drain()
        v.tensor_tensor(out=b0[:, 0:n], in0=b0[:, 0:n], in1=b1[:, 0:n], op=OP.add)
        v.drain()
        v.tensor_scalar(out=b0[:, 0:n], in0=b0[:, 0:n], scalar1=6, scalar2=None,
                        op0=OP.logical_shift_left)
        v.drain()
        v.tensor_tensor(out=b0[:, 0:n], in0=b0[:, 0:n], in1=b2[:, 0:n], op=OP.bitwise_or)
        v.drain()
        v.tensor_tensor(out=wfp[:, 0:n], in0=b0[:, 0:n].bitcast(BF16), in1=sreps[:, 0:n], op=OP.mult)

    def emit_deq(H):
        kind, w, h, par = H["kind"], H["w"], H["h"], H["par"]
        gate = wbuf_gate[par]
        vl = None
        nrows = NDC if kind == "gu" else 12
        for g in range(0, nrows, 4):
            db = (g // 4) % 2
            lds = []
            for j in range(4):
                r = g + j
                if kind == "gu":
                    nc_gate, nc_up = 2 * w + h, 2 * w + 12 + h
                    lds.append(k.dma("gpsimd", bt[db][:, j * 256:j * 256 + 128],
                                     H["bap"](r, slice(nc_gate * 128, (nc_gate + 1) * 128)),
                                     waits=prev_ldb[db] if j == 0 else ()))
                    lds.append(k.dma("gpsimd", bt[db][:, j * 256 + 128:(j + 1) * 256],
                                     H["bap"](r, slice(nc_up * 128, (nc_up + 1) * 128))))
                    lds.append(k.dma("sync", srep[db][:, j * 256:j * 256 + 128],
                                     H["sap"](r, slice(nc_gate * 128, (nc_gate + 1) * 128))
                                     .unsqueeze(0).broadcast_to([16, 8, 128]),
                                     waits=prev_ldb[db] if j == 0 else ()))
                    lds.append(k.dma("sync", srep[db][:, j * 256 + 128:(j + 1) * 256],
                                     H["sap"](r, slice(nc_up * 128, (nc_up + 1) * 128))
                                     .unsqueeze(0).broadcast_to([16, 8, 128])))
                else:
                    cs = slice(w * 512 + h * 256, w * 512 + (h + 1) * 256)
                    lds.append(k.dma("gpsimd", bt[db][:, j * 256:(j + 1) * 256], H["bap"](r, cs),
                                     waits=prev_ldb[db] if j == 0 else ()))
                    lds.append(k.dma("sync", srep[db][:, j * 256:(j + 1) * 256],
                                     H["sap"](r, cs).unsqueeze(0).broadcast_to([16, 8, 256]),
                                     waits=prev_ldb[db] if j == 0 else ()))
            k.wait("vector", *lds, *( gate if g == 0 else () ))
            vc = k.v_op(lambda v, db=db, par=par: _chain(v, bt[db], srep[db], 1024, wf[par]))
            wb_h = wwh[par] if kind == "gu" else wdh[par]
            wb_l = wwl[par] if kind == "gu" else wdl[par]
            bsl = slice(g * 256, (g + 4) * 256)
            k.wait("scalar", vc, *( gate if g == 0 else () ))
            sh = k.s_op(lambda s, wb_h=wb_h, bsl=bsl, par=par: s.copy(out=wb_h[:, bsl],
                                                                      in_=wf[par][:, 0:1024]))
            k.wait("vector", sh)
            vl = k.v_op(lambda v, wb_l=wb_l, wb_h=wb_h, bsl=bsl, par=par: v.tensor_tensor(
                out=wb_l[:, bsl], in0=wf[par][:, 0:1024], in1=wb_h[:, bsl], op=OP.subtract))
            prev_ldb[db] = (vl,)
        return vl

    def emit_mm(H, deq_done):
        kind, w, h, par, bp = H["kind"], H["w"], H["h"], H["par"], H["bp"]
        outs = []
        if kind == "gu":
            for th in (H["th"],):
                PG = k.ps[2 * bp]
                PU = k.ps[2 * bp + 1]
                def tgu(t, th=th, par=par, PG=PG, PU=PU, deq_done=deq_done,
                        gate=k.moe_bankw.get(2 * bp, ()) + k.moe_bankw.get(2 * bp + 1, ())):
                    t.wait_ge(k.sem[deq_done[0]], deq_done[1])
                    for wn, wv in gate:
                        t.wait_ge(k.sem[wn], wv)
                    lastg = None
                    for dc in range(NDC):
                        wslg = slice(dc * 256, dc * 256 + 128)
                        wslu = slice(dc * 256 + 128, (dc + 1) * 256)
                        xsl = slice(dc * TOK + th * 512, dc * TOK + (th + 1) * 512)
                        mm3(t, PG[:], wwh[par][:, wslg], wwl[par][:, wslg],
                            xln_hi[:, xsl], xln_lo[:, xsl], start=(dc == 0), stop=(dc == NDC - 1))
                        lastg = mm3(t, PU[:], wwh[par][:, wslu], wwl[par][:, wslu],
                                    xln_hi[:, xsl], xln_lo[:, xsl], start=(dc == 0),
                                    stop=(dc == NDC - 1))
                    return lastg
                tw = k.t_op(tgu)
                outs.append((th, PG, PU, tw))
                k.moe_bankw[2 * bp] = (tw,)
                k.moe_bankw[2 * bp + 1] = (tw,)
        else:
            hcl = H["hcl"]
            for th in range(2):
                bidx = 2 * bp + th
                PB = k.ps[bidx]
                def tdn(t, h=h, hcl=hcl, th=th, par=par, PB=PB, deq_done=deq_done,
                        gate=k.moe_bankw.get(bidx, ()), act_ready=state["act_ready"]):
                    t.wait_ge(k.sem[deq_done[0]], deq_done[1])
                    t.wait_ge(k.sem[act_ready[0]], act_ready[1])
                    for wn, wv in gate:
                        t.wait_ge(k.sem[wn], wv)
                    last = None
                    for ic in range(12):
                        wsl = slice(ic * 256 + hcl * 128, ic * 256 + (hcl + 1) * 128)
                        asl = slice(ic * TOK + th * 512, ic * TOK + (th + 1) * 512)
                        last = mm3(t, PB[:], wdh[par][:, wsl], wdl[par][:, wsl],
                                   act_hi[:, asl], act_lo[:, asl],
                                   start=(ic == 0), stop=(ic == 11))
                    return last
                tw = k.t_op(tdn)
                outs.append((hcl, th, PB, tw))
                k.moe_bankw[2 * bp + th] = (tw,)
        return outs

    def emit_evac(H, outs):
        kind, w, h, par = H["kind"], H["w"], H["h"], H["par"]
        shared = H["shared"]
        last = None
        if kind == "gu":
            nc_gate = 2 * w + h
            for (th, PG, PU, tw) in outs:
                k.wait("scalar", tw)
                s1 = k.s_op(lambda s, PG=PG: s.activation(out=sgb[:], in_=PG[:], func=AF.Silu))
                asl = slice(nc_gate * TOK + th * 512, nc_gate * TOK + (th + 1) * 512)
                k.wait("vector", s1, *( (H["gbc_ready"],) if H["gbc_ready"] else () ))
                if shared:
                    va = k.v_op(lambda v, PU=PU: v.tensor_tensor(out=af[:], in0=sgb[:], in1=PU[:],
                                                                 op=OP.mult))
                else:
                    va = k.v_op(lambda v, PU=PU, th=th: (
                        v.tensor_tensor(out=af[:], in0=sgb[:], in1=PU[:], op=OP.mult),
                        v.drain(),
                        v.tensor_tensor(out=af[:], in0=af[:], in1=gbc[:, th * 512:(th + 1) * 512],
                                        op=OP.mult)))
                k.wait("scalar", va)
                sh = k.s_op(lambda s, asl=asl: s.copy(out=act_hi[:, asl], in_=af[:]))
                k.wait("vector", sh)
                last = k.v_op(lambda v, asl=asl: v.tensor_tensor(
                    out=act_lo[:, asl], in0=af[:], in1=act_hi[:, asl], op=OP.subtract))
            state["act_ready"] = last
            k.moe_bankw[2 * H["bp"]] = (last,)
            k.moe_bankw[2 * H["bp"] + 1] = (last,)
        else:
            for (hcl, th, PB, tw) in outs:
                hc = w * 4 + h * 2 + hcl
                k.wait("vector", tw, state["last_acc"])
                vo = k.v_op(lambda v, PB=PB: v.tensor_copy(out=ob[:], in_=PB[:]))
                state["last_acc"] = k.dma("gpsimd", k.acc_s[hc, :, th * 512:(th + 1) * 512], ob[:],
                                          waits=(vo,), accum=OP.add)
                last = vo
                k.moe_bankw[2 * H["bp"] + th] = (last,)
        wbuf_gate[par] = (last,)
        return last

    halves = []
    for e in range(E + 1):
        shared = (e == E)
        if shared:
            bap_g = lambda dc, cs: k.inp["sgup_b"][dc, :, cs]
            sap_g = lambda dc, cs: k.inp["sgus"][dc, :, cs]
            bap_d = lambda ic, cs: k.inp["sdnp_b"][ic, :, cs]
            sap_d = lambda ic, cs: k.inp["sdns"][ic, :, cs]
        else:
            bap_g = lambda dc, cs, e=e: k.inp["gup_b"][e, dc, :, cs]
            sap_g = lambda dc, cs, e=e: k.inp["gus"][e, dc, :, cs]
            bap_d = lambda ic, cs, e=e: k.inp["dnp_b"][e, ic, :, cs]
            sap_d = lambda ic, cs, e=e: k.inp["dns"][e, ic, :, cs]
        for w in range(6):
            for h in range(2):
                for th in range(2):
                    halves.append({"kind": "gu", "e": e, "w": w, "h": h, "th": th,
                                   "shared": shared, "bap": bap_g, "sap": sap_g,
                                   "deq": (th == 0), "hcl": None})
        for hw in range(4):
            for h in range(2):
                for hcl in range(2):
                    halves.append({"kind": "dn", "e": e, "w": hw, "h": h, "th": None,
                                   "shared": shared, "bap": bap_d, "sap": sap_d,
                                   "deq": (hcl == 0), "hcl": hcl})
    np_ = 0
    for i, H in enumerate(halves):
        if H["deq"]:
            np_ += 1
        H["par"] = np_ % 2
        H["bp"] = i % 2

    def emit_gbc(e):
        ohl = k.dma("sync", ohb[:], k.inp["oh_rows"][:, e * 128:(e + 1) * 128],
                    waits=(state["prev_gbc"],) if state["prev_gbc"] else ())
        vg = None
        w1 = None
        for th in range(2):
            def tgb(t, th=th, vg=vg, ohl=ohl):
                t.wait_ge(k.sem[ohl[0]], ohl[1])
                if vg is not None:
                    t.wait_ge(k.sem[vg[0]], vg[1])
                return t.matmul(k.ps7[:], ohb[:, :], combT[0:16, th * 512:(th + 1) * 512],
                                start=True, stop=True, tile_position=(0, 0))
            w1 = k.t_op(tgb)
            k.wait("vector", w1)
            vg = k.v_op(lambda v, th=th: v.tensor_copy(out=gbc[:, th * 512:(th + 1) * 512],
                                                       in_=k.ps7[:]))
        state["prev_gbc"] = w1
        return vg

    pend = None
    cur_e = -1
    gbc_ready = None
    for H in halves:
        if H["e"] != cur_e:
            if pend is not None:
                emit_evac(*pend)
                pend = None
            cur_e = H["e"]
            gbc_ready = None if H["shared"] else emit_gbc(cur_e)
        H["gbc_ready"] = gbc_ready
        if H["deq"]:
            dq = emit_deq(H)
            last_dq = dq
        else:
            dq = last_dq
        if pend is not None:
            emit_evac(*pend)
        outs = emit_mm(H, dq)
        pend = (H, outs)
    emit_evac(*pend)
    k.last_acc = state["last_acc"]
    k.barrier()


def emit_f(k, out32, ready):
    ob = k.sb("f_ob", (128, 2048), F32)
    pw = (ready,)
    obw = ()
    last = None
    for tb in range(8):
        for g4 in range(4):
            def tt(t, tb=tb, g4=g4, pw=pw):
                for wn, wv in pw:
                    t.wait_ge(k.sem[wn], wv)
                last_ = None
                for j in range(4):
                    dc = g4 * 4 + j
                    last_ = t.transpose(k.ps[4][:, j * 128:(j + 1) * 128],
                                        out32[:, dc * TOK + tb * 128: dc * TOK + (tb + 1) * 128],
                                        k.ident_f[:])
                return last_
            tw = k.t_op(tt)
            k.wait("vector", tw, *obw)
            vv = k.v_op(lambda v, g4=g4: v.tensor_copy(out=ob[:, g4 * 512:(g4 + 1) * 512], in_=k.ps[4][:]))
            pw = (vv,)
        last = k.dma("sync", k.out[tb, :, :], ob[:], waits=(pw[0],))
        obw = (last,)
    k.wait("sync", *k.queue_fence("sd"))
    k.wait("sync", *k.queue_fence("sg"))


def emit_stub_out(k):
    from contextlib import ExitStack
    es = ExitStack()
    ob = k.sb_on(es, "stub_ob", (128, 2048), F32)
    z = k.v_op(lambda v: v.memset(ob[:], 0.0))
    for tb in range(8):
        k.dma("sync", k.out[tb, :, :], ob[:], waits=(z,))
    k.wait("sync", *k.queue_fence("sd"))
    k.wait("sync", *k.queue_fence("sg"))
    es.close()

def build_nc(phases=99):
    from contextlib import ExitStack
    nc = bass.Bass("TRN2", target_bir_lowering=False)
    k = declare_io(nc)
    k.moe_bankw = {}
    with ExitStack() as es0:
        k.es = es0
        for s in ["sv", "ss", "st", "sgp", "sb"]:
            k.mksem(s)
        for base in ("sd", "sg", "sa"):
            for i in range(KB.NSLOT):
                k.mksem(f"{base}{i}")
        k.ps = [es0.enter_context(nc.psum_tensor(f"ps{i}", [128, 512], F32)) for i in range(6)]
        k.psb = es0.enter_context(nc.psum_tensor("psb", [128, 1024], BF16))
        k.ps7 = es0.enter_context(nc.psum_tensor("ps7", [128, 512], F32))
        with nc.Block() as block:
            es_rope = ExitStack()
            es_rtmp = ExitStack()
            es_wsp = ExitStack()
            emit_prologue(k, es_rope, es_rtmp, es_wsp)
            k.barrier()
            es_wsp.close()
            es_rtmp.close()
            if phases >= 2:
                es_a2 = ExitStack(); k.es = es_a2
                emit_a2(k)
                es_a2.close()
            if phases >= 3:
                es_a1 = ExitStack(); k.es = es_a1
                emit_a1(k)
                es_a1.close()
            es_rope.close()
            if phases < 4:
                emit_stub_out(k)
                k.emit_into(block)
                return nc
            es_att = ExitStack()
            k.att_hi = k.sb_on(es_att, "atth", (128, NDC * TOK), BF16)
            k.att_lo = k.sb_on(es_att, "attl", (128, NDC * TOK), BF16)
            es_b = ExitStack(); k.es = es_b
            emit_b(k, es_att)
            es_b.close()
            if phases < 5:
                es_att.close()
                emit_stub_out(k)
                k.emit_into(block)
                return nc
            es_c = ExitStack(); k.es = es_c
            emit_c(k)
            es_c.close()
            es_att.close()
            if phases < 6:
                emit_stub_out(k)
                k.emit_into(block)
                return nc
            es_mid = ExitStack()
            k.xln_hi = k.sb_on(es_mid, "xlnh", (128, NDC * TOK), BF16)
            k.xln_lo = k.sb_on(es_mid, "xlnl", (128, NDC * TOK), BF16)
            es_ln1 = ExitStack(); k.es = es_ln1
            ln1_done, acc_d = emit_ln_stream(k, k.xres_s, pfx="n1", out32=None, out_hi=k.xln_hi,
                                             out_lo=k.xln_lo, acc_hbm=k.acc_s,
                                             src_wait=k.c_spill)
            k.acc_init = acc_d
            k.barrier()
            es_ln1.close()
            if phases < 7:
                es_mid.close()
                emit_stub_out(k)
                k.emit_into(block)
                return nc
            es_d = ExitStack(); k.es = es_d
            combT = k.sb("combT", (16, 1024), F32)
            emit_router(k, k.xln_hi, k.xln_lo, combT)
            k.barrier()
            if phases >= 8:
                emit_moe(k, k.xln_hi, k.xln_lo, combT)
            else:
                k.last_acc = k.acc_init
            es_d.close()
            es_mid.close()
            if phases < 9:
                emit_stub_out(k)
                k.emit_into(block)
                return nc
            es_ln2 = ExitStack(); k.es = es_ln2
            k.out32 = k.sb("o32", (128, NDC * TOK), F32)
            es_ln2b = ExitStack(); k.es = es_ln2b
            ln2_done, _ = emit_ln_stream(k, k.acc_s, pfx="n2", out32=k.out32, src_wait=(k.last_acc,))
            k.barrier()
            es_ln2b.close()
            es_f = ExitStack(); k.es = es_f
            emit_f(k, k.out32, ln2_done)
            es_f.close()
            es_ln2.close()
            k.emit_into(block)
    return nc


def perm_ef(n):
    idx = []
    for c in range(0, n, 128):
        idx += list(range(c, c + 128, 2)) + list(range(c + 1, c + 128, 2))
    return np.array(idx)


PD = perm_ef(2048)
PI = perm_ef(1536)
INV_PD = np.argsort(PD)


def perm_rope(nheads):
    out = []
    for h in range(nheads):
        out += [h * 64 + j for j in range(0, 64, 2)]
        out += [h * 64 + j for j in range(1, 64, 2)]
    return np.array(out)


PQR = perm_rope(32)
PKR = perm_rope(8)


def stage_bytes(packed, kdim):
    sh = packed.shape
    n = sh[-1]
    w = sh[-2]
    b = np.ascontiguousarray(packed).view(np.uint8).reshape(sh[:-2] + (w, n, 4))
    b = np.moveaxis(b, -1, -2)
    b = b.reshape(sh[:-2] + (w * 4, n))
    kc = kdim // 128
    b = b.reshape(sh[:-2] + (kc, 64, n))
    b = np.concatenate([b, b], axis=-2)
    return np.ascontiguousarray(b)


def make_inputs(core, inp):
    b, off = core // 4, core % 4
    m = {}
    x = np.asarray(inp["L0_attn_x"])
    m["x_fm"] = np.ascontiguousarray(x[b].T[PD].reshape(NDC, 128, LSEQ))
    m["x_own"] = np.ascontiguousarray(x[b, off::4, :].T[PD].reshape(NDC, 128, TOK))
    m["qpos"] = np.tile(np.arange(off, LSEQ, 4, dtype=np.float32)[None, :], (128, 1))
    import ml_dtypes
    sub = np.arange(16)[:, None, None]
    p = np.arange(128)[None, :, None]
    i = np.arange(512)[None, None, :]
    mask = np.where(128 * sub + p <= off + 4 * i, 0.0, NEG).astype(ml_dtypes.bfloat16)
    m["mask_fine"] = mask
    m["ident_bf"] = np.eye(128, dtype=ml_dtypes.bfloat16)
    m["ident_f32"] = np.eye(128, dtype=np.float32)
    m["ones_bf"] = np.ones((128, 128), dtype=ml_dtypes.bfloat16)
    m["ones_f32"] = np.ones((128, 128), dtype=np.float32)
    oh = np.zeros((16, E * 128), dtype=np.float32)
    for e in range(E):
        oh[e, e * 128:(e + 1) * 128] = 1.0
    m["oh_rows"] = oh
    inv64 = 1.0 / np.power(10000.0, np.arange(0, 64, 2, dtype=np.float64) / 64.0)
    inv64 = np.tile(inv64, 4)
    ih = inv64.astype(np.float32)
    il = (inv64 - ih.astype(np.float64)).astype(np.float32)
    m["inv_hi"] = ih.reshape(128, 1)
    m["inv_lo"] = il.reshape(128, 1)
    wq = np.asarray(inp["L0_attn_Wq"])
    wk = np.asarray(inp["L0_attn_Wk"])
    wv = np.asarray(inp["L0_attn_Wv"])
    wo = np.asarray(inp["L0_attn_Wo"])
    m["wqT"] = np.ascontiguousarray(wq[np.ix_(PQR, PD)].T.reshape(NDC, 128, 2048))
    m["wkT"] = np.ascontiguousarray(wk[np.ix_(PKR, PD)].T.reshape(NDC, 128, 512))
    m["wvT"] = np.ascontiguousarray(wv[:, PD].T.reshape(NDC, 128, 512))
    m["woT"] = np.ascontiguousarray(wo[PD, :].T.reshape(NDC, 128, 2048))
    m["gwT"] = np.ascontiguousarray(np.asarray(inp["L0_ffn_gate_w"])[:, PD].T.reshape(NDC, 128, 16))
    cgu = np.concatenate([PI, 1536 + PI])
    gup = np.asarray(inp["L0_ffn_gate_up_packed"])[:, :, cgu]
    m["gup_b"] = stage_bytes(gup, 2048)
    m["gus"] = np.ascontiguousarray(np.repeat(np.asarray(inp["L0_ffn_gate_up_scales"])[:, :, None, cgu], 8, axis=2))
    dnp = np.asarray(inp["L0_ffn_down_packed"])[:, :, PD]
    m["dnp_b"] = stage_bytes(dnp, 1536)
    m["dns"] = np.ascontiguousarray(np.repeat(np.asarray(inp["L0_ffn_down_scales"])[:, :, None, PD], 8, axis=2))
    m["sgup_b"] = stage_bytes(np.asarray(inp["L0_ffn_shared_gate_up_packed"])[:, cgu], 2048)
    m["sgus"] = np.ascontiguousarray(np.repeat(np.asarray(inp["L0_ffn_shared_gate_up_scales"])[:, None, cgu], 8, axis=1))
    m["sdnp_b"] = stage_bytes(np.asarray(inp["L0_ffn_shared_down_packed"])[:, PD], 1536)
    m["sdns"] = np.ascontiguousarray(np.repeat(np.asarray(inp["L0_ffn_shared_down_scales"])[:, None, PD], 8, axis=1))
    return m


def kernel(**inputs):
    from concourse.bass_utils import run_bass_kernel_spmd
    in_maps = [make_inputs(c, inputs) for c in range(8)]
    nc = build_nc(PHASES)
    res = run_bass_kernel_spmd(nc, in_maps, core_ids=list(range(8)))
    out = np.zeros((2, LSEQ, 2048), dtype=np.float32)
    for c in range(8):
        b, off = c // 4, c % 4
        o = np.asarray(res.results[c]["out"]).reshape(TOK, 2048)
        out[b, off::4, :] = o[:, INV_PD]
    return out


def emit_c(k):
    gdone = k.queue_fence("sg")
    woh = k.sb("c_woh", (128, NDC * 128), BF16)
    wol = k.sb("c_wol", (128, NDC * 128), BF16)
    xot = [k.sb(f"c_xot{i}", (128, 512), F32) for i in range(2)]
    xrt = [k.sb(f"c_xrt{i}", (128, 512), F32) for i in range(2)]
    bank_w = {}
    w_ld = ()
    prev_xr = [(), ()]
    prev_xo = [(), ()]
    for oc in range(NDC):
        b = oc % 2
        lh = k.dma("sync", woh[:], k.wo_s[0, :, :, oc * 128:(oc + 1) * 128].transpose([1, 0, 2]),
                   waits=w_ld + (gdone if oc == 0 else ()))
        ll = k.dma("sync", wol[:], k.wo_s[1, :, :, oc * 128:(oc + 1) * 128].transpose([1, 0, 2]))
        evs = []
        for th in range(2):
            bk = 2 * b + th
            PQ = k.ps[2 + bk]
            pk = bank_w.get(bk, ())
            xb = (2 * oc + th) % 2
            lx = k.dma("sync", xot[xb][:], k.inp["x_own"][oc, :, th * 512:(th + 1) * 512],
                       waits=prev_xo[xb])
            def tq(t, th=th, pk=pk, lh=lh, ll=ll, PQ=PQ):
                t.wait_ge(k.sem[lh[0]], lh[1])
                t.wait_ge(k.sem[ll[0]], ll[1])
                for wn, wv in pk:
                    t.wait_ge(k.sem[wn], wv)
                last = None
                for dc in range(NDC):
                    wsl = slice(dc * 128, (dc + 1) * 128)
                    asl = slice(dc * TOK + th * 512, dc * TOK + (th + 1) * 512)
                    last = mm3(t, PQ[:], woh[:, wsl], wol[:, wsl], k.att_hi[:, asl], k.att_lo[:, asl],
                               start=(dc == 0), stop=(dc == NDC - 1))
                return last
            tw = k.t_op(tq)
            k.wait("vector", tw, lx, *prev_xr[xb])
            ev = k.v_op(lambda v, PQ=PQ, xb=xb: v.tensor_tensor(out=xrt[xb][:], in0=xot[xb][:],
                                                                in1=PQ[:], op=OP.add))
            dx = k.dma("gpsimd", k.xres_s[oc, :, th * 512:(th + 1) * 512], xrt[xb][:], waits=(ev,))
            prev_xr[xb] = (dx,)
            prev_xo[xb] = (ev,)
            evs.append(ev)
            bank_w[bk] = (ev,)
        w_ld = (evs[1],)
    k.c_spill = k.queue_fence("sg")
    k.barrier()


def emit_ln_stream(k, src_hbm, out32=None, out_hi=None, out_lo=None, acc_hbm=None, src_wait=(), pfx="ln"):
    st_ = [k.sb(f"{pfx}lns{i}", (128, 512), F32) for i in range(2)]
    tmph = k.sb(f"{pfx}_th", (128, 512), BF16)
    tmpl = k.sb(f"{pfx}_tl", (128, 512), BF16)
    sqf = k.sb(f"{pfx}_sq", (128, 512), F32)
    sqh = k.sb(f"{pfx}_sqh", (128, 512), BF16)
    sql = k.sb(f"{pfx}_sql", (128, 512), BF16)
    rows = k.sb(f"{pfx}_rows", (1, 7 * 1024), F32)
    bca = k.sb(f"{pfx}_bca", (128, 512), F32)
    o32t = [k.sb(f"{pfx}ln_o{i}", (128, 512), F32) for i in range(2)]

    def R(i, th=None):
        if th is None:
            return rows[0:1, i * 1024:(i + 1) * 1024]
        return rows[0:1, i * 1024 + th * 512: i * 1024 + (th + 1) * 512]

    prev_use = [(), ()]
    prev_tmm = None
    vev = None
    for th in range(2):
        for dc in range(NDC):
            xb = dc % 2
            ld = k.dma("sync", st_[xb][:], src_hbm[dc, :, th * 512:(th + 1) * 512],
                       waits=prev_use[xb] + (tuple(src_wait) if (th == 0 and dc == 0) else ()))
            wts = (prev_tmm,) if prev_tmm else ()
            k.wait("scalar", ld, *wts)
            s1 = k.s_op(lambda s, xb=xb: (s.copy(out=tmph[:], in_=st_[xb][:]), s.drain(),
                                          s.activation(out=sqf[:], in_=st_[xb][:], func=AF.Square)))
            k.wait("vector", s1, *wts)
            v1 = k.v_op(lambda v, xb=xb: v.tensor_tensor(out=tmpl[:], in0=st_[xb][:], in1=tmph[:],
                                                         op=OP.subtract))
            prev_use[xb] = (v1,)
            s2 = k.s_op(lambda s: s.copy(out=sqh[:], in_=sqf[:]))
            k.wait("vector", s2)
            v2 = k.v_op(lambda v: v.tensor_tensor(out=sql[:], in0=sqf[:], in1=sqh[:], op=OP.subtract))
            def tmm(t, dc=dc, v2=v2, vev=vev):
                t.wait_ge(k.sem[v2[0]], v2[1])
                if dc == 0 and vev is not None:
                    t.wait_ge(k.sem[vev[0]], vev[1])
                t.matmul(k.ps[0][0:1, :], k.ones_bf[:, 0:1], tmph[:], start=(dc == 0), stop=False,
                         tile_position=(0, 0))
                t.matmul(k.ps[0][0:1, :], k.ones_bf[:, 0:1], tmpl[:], start=False, stop=(dc == NDC - 1),
                         tile_position=(0, 0))
                t.matmul(k.ps[1][0:1, :], k.ones_bf[:, 0:1], sqh[:], start=(dc == 0), stop=False,
                         tile_position=(0, 0))
                return t.matmul(k.ps[1][0:1, :], k.ones_bf[:, 0:1], sql[:], start=False,
                                stop=(dc == NDC - 1), tile_position=(0, 0))
            prev_tmm = k.t_op(tmm)
        k.wait("vector", prev_tmm)
        vev = k.v_op(lambda v, th=th: (
            v.tensor_copy(out=R(0, th), in_=k.ps[0][0:1, :]),
            v.drain(),
            v.tensor_copy(out=R(1, th), in_=k.ps[1][0:1, :])))
    def vrow(v):
        v.tensor_scalar(out=R(2), in0=R(0), scalar1=1.0 / 2048.0, scalar2=None, op0=OP.mult)
        v.drain()
        v.tensor_scalar(out=R(3), in0=R(1), scalar1=1.0 / 2048.0, scalar2=None, op0=OP.mult)
        v.drain()
        v.tensor_tensor(out=R(4), in0=R(2), in1=R(2), op=OP.mult)
        v.drain()
        v.tensor_tensor(out=R(3), in0=R(3), in1=R(4), op=OP.subtract)
        v.drain()
        v.tensor_scalar(out=R(3), in0=R(3), scalar1=1e-5, scalar2=None, op0=OP.add)
    vr = k.v_op(vrow)
    k.wait("scalar", vr)
    sr = k.s_op(lambda s: s.activation(out=R(4), in_=R(3), func=AF.Sqrt))
    k.wait("vector", sr)
    vr2 = k.v_op(lambda v: (
        v.reciprocal(out=R(5), in_=R(4)),
        v.drain(),
        v.tensor_tensor(out=R(6), in0=R(2), in1=R(5), op=OP.mult),
        v.drain(),
        v.tensor_scalar(out=R(6), in0=R(6), scalar1=-1.0, scalar2=None, op0=OP.mult)))
    out_ready = vr2
    prev_o = [(), ()]
    acc_d = None
    for th in range(2):
        def tbc(t, th=th, vr2=vr2, prev=out_ready):
            t.wait_ge(k.sem[vr2[0]], vr2[1])
            t.wait_ge(k.sem[prev[0]], prev[1])
            t.matmul(k.ps[2][:], k.ones_f[0:1, 0:128], R(5, th), start=True, stop=True,
                     tile_position=(0, 0))
            return t.matmul(k.ps[3][:], k.ones_f[0:1, 0:128], R(6, th), start=True, stop=True,
                            tile_position=(0, 0))
        tb_ = k.t_op(tbc)
        k.wait("vector", tb_)
        vb = k.v_op(lambda v: v.tensor_copy(out=bca[:], in_=k.ps[2][:]))
        last = vb
        for dc in range(NDC):
            xb = dc % 2
            ld = k.dma("sync", st_[xb][:], src_hbm[dc, :, th * 512:(th + 1) * 512],
                       waits=prev_use[xb])
            k.wait("vector", ld, *prev_o[xb])
            va = k.v_op(lambda v, xb=xb: (
                v.tensor_tensor(out=o32t[xb][:], in0=st_[xb][:], in1=bca[:], op=OP.mult),
                v.drain(),
                v.tensor_tensor(out=o32t[xb][:], in0=o32t[xb][:], in1=k.ps[3][:], op=OP.add)))
            prev_use[xb] = (va,)
            last = va
            sl = slice(dc * TOK + th * 512, dc * TOK + (th + 1) * 512)
            if out32 is not None:
                last = k.v_op(lambda v, sl=sl, xb=xb: v.tensor_copy(out=out32[:, sl], in_=o32t[xb][:]))
            if acc_hbm is not None:
                acc_d = k.dma("gpsimd", acc_hbm[dc, :, th * 512:(th + 1) * 512], o32t[xb][:],
                              waits=(last,))
            if out_hi is not None:
                k.wait("scalar", va)
                sh = k.s_op(lambda s, sl=sl, xb=xb: s.copy(out=out_hi[:, sl], in_=o32t[xb][:]))
                k.wait("vector", sh)
                last = k.v_op(lambda v, sl=sl, xb=xb: v.tensor_tensor(out=out_lo[:, sl], in0=o32t[xb][:],
                                                                     in1=out_hi[:, sl], op=OP.subtract))
            prev_o[xb] = (last,) if acc_hbm is None else (last, acc_d)
        out_ready = last
    return out_ready, acc_d
